# version 04
# baseline (speedup 1.0000x reference)
import math
import numpy as np
import ml_dtypes
import concourse.bass as bass
import concourse.mybir as mybir
from concourse.bass_utils import run_bass_kernel_spmd

F32 = mybir.dt.float32
BF16 = mybir.dt.bfloat16
I32 = mybir.dt.int32
U8 = mybir.dt.uint8
AF = mybir.ActivationFunctionType
ALU = mybir.AluOpType
AX = mybir.AxisListType

T = 4096
D = 1024
NCK = 8
NTT = 32
NT5 = 8
KB = 1024

ENGINES = ("tensor", "vector", "scalar", "gpsimd", "sync")


class Sem:
    __slots__ = ("h", "n")

    def __init__(self, h):
        self.h = h
        self.n = 0

    def inc(self, k=1):
        self.n += k
        return self.n


class Prog:
    def __init__(self, nc):
        self.nc = nc
        self.q = {e: [] for e in ENGINES}

    def wait(self, eng, sem, lvl):
        if lvl is None or lvl <= 0:
            return
        self.q[eng].append(lambda E, s=sem.h, v=lvl: E.wait_ge(s, v))

    def op(self, eng, meth, *args, sem=None, inc=1, low_prec=False, **kw):
        lvl = sem.inc(inc) if sem is not None else None
        nc = self.nc

        def fn(E, meth=meth, args=args, kw=kw, sem=sem, inc=inc):
            try:
                if low_prec:
                    with nc.allow_low_precision(reason="sanctioned bf16"):
                        inst = getattr(E, meth)(*args, **kw)
                else:
                    inst = getattr(E, meth)(*args, **kw)
            except Exception as e:
                raise RuntimeError(
                    f"op {meth} failed: {e} args0={str(args[0])[:160]}") from e
            if sem is not None:
                inst.then_inc(sem.h, inc)
        self.q[eng].append(fn)
        return lvl

    def dma(self, eng, out, in_, sem, **kw):
        return self.op(eng, "dma_start", sem=sem, inc=16, out=out, in_=in_, **kw)

    def replay(self, block):
        for eng in ENGINES:
            q = self.q[eng]

            def section(E, q=q):
                for fn in q:
                    fn(E)
            getattr(block, eng)(section)


class Arena:
    def __init__(self, nc, ctx, name, nbytes):
        self.nc = nc
        self.backing = ctx.enter_context(nc.sbuf_tensor(name, [128, nbytes], U8))
        self.base = nc.lookup_mloc(self.backing).addr
        self.name = name
        self.n = 0

    def view(self, shape, dtype, offset_bytes):
        name = f"{self.name}_v{self.n}"
        self.n += 1
        return self.nc.alloc_sbuf_tensor_at(
            name, list(shape), dtype, offset=self.base + offset_bytes)


class St:
    pass


def emit_const_setup(p, g):
    p.op("gpsimd", "memset", g.ones_bf[:], 1.0)
    p.op("gpsimd", "memset", g.onescol_bf[:], 1.0)
    p.op("gpsimd", "memset", g.epscol[:], 1e-5)
    p.op("gpsimd", "memset", g.negln8[:], -math.log(8.0))
    p.op("gpsimd", "memset", g.negpi[:], -math.pi)
    p.op("gpsimd", "memset", g.neg1e30[:], -1e30)
    p.op("gpsimd", "memset", g.negln128h[:], -0.5 * math.log(128.0))
    p.op("gpsimd", "memset", g.onesf32[:], 1.0)
    lv = p.op("gpsimd", "memset", g.ident_bf[:], 0.0, sem=g.s_gp)
    p.wait("gpsimd", g.s_gp, lv)
    p.op("gpsimd", "affine_select", g.ident_bf[:], g.ident_bf[:],
         pattern=[[-1, 128]], compare_op=ALU.not_equal, fill=1.0,
         base=0, channel_multiplier=1)
    lv = p.op("gpsimd", "memset", g.ident_f32[:], 0.0, sem=g.s_gp)
    p.wait("gpsimd", g.s_gp, lv)
    p.op("gpsimd", "affine_select", g.ident_f32[:], g.ident_f32[:],
         pattern=[[-1, 128]], compare_op=ALU.not_equal, fill=1.0,
         base=0, channel_multiplier=1, sem=g.s_gp)
    g.const_lvl = g.s_gp.n
    rwa = bass.AP(g.weights["L0_ffn_router_w"], 0,
                  [[8, 128], [128 * 8, 8], [1, 8]])
    g.rw_lvl = p.dma("gpsimd", g.rw_sb[:], rwa, g.s_rw)


def emit_bridge(p, g, *, yspec=None, a_reg=None, rin_x=False, out_mode=False,
                router=False, spill_A=False):
    if rin_x:
        sqbuf, rstd_row, mr_row, rstd_b, mr_b = (g.sqbuf_x, g.rstd_row_x,
                                                 g.mr_row_x, g.rstd_b_x,
                                                 g.mr_b_x)
    else:
        sqbuf, rstd_row, mr_row, rstd_b, mr_b = (g.sqbuf, g.rstd_row,
                                                 g.mr_row, g.rstd_b, g.mr_b)
    """Stream R: R += Y -> Rbuf; A = LN(R) -> a_reg; optional router logits.

    yspec: None | ("sbuf", bf16 [128x32768] handle) | ("hbm", f32 dram handle)
    """
    stag = g.stag
    ps = g.ps
    psbf = g.psbf
    bank = g.bank

    ld_seq = 0
    add_lv = [None] * NCK
    rb_lv = [None] * NCK
    ev_lv = {}
    st_lv = [None] * NCK
    rst_lv = [None] * NCK
    g.a_spill_lv = [0] * NCK
    stat_mm = [None] * NCK
    stat_ev = [None] * NCK
    stats_bank = bank(6)[:, 0:64]

    dve_entry = g.s_dve.n
    act_entry = g.s_act.n
    p.wait("sync", g.s_dve, dve_entry)
    p.wait("sync", g.s_gp, g.s_gp.n)
    for _s in g.s_st + g.s_ya:
        p.wait("sync", _s, _s.n)
    z_lv = p.op("vector", "memset", g.statsb[:], 0.0, sem=g.s_dve)

    for c in range(NCK):
        sl = stag[:, (c % 2) * T:(c % 2) * T + T]
        sem_ld = g.s_ld[c % 2]
        if out_mode:
            sl = stag[:, 0:T]
            if c >= 1:
                p.wait("sync", g.s_dve, ev_lv[(c - 1, NTT - 1)])
        elif c >= 2:
            p.wait("sync", g.s_dve, rb_lv[c - 2])
            p.wait("sync", g.s_st[c % 2], rst_lv[c - 2])
        if rin_x:
            xa = bass.AP(g.x_in, c * 128,
                         [[1024, 128], [128 * 1024, 32], [1, 128]])
            xsl = g.xstage[:, (c % 2) * T:(c % 2) * T + T]
            if c >= 2:
                p.wait("sync", g.s_pe, g.xtp_lv[c - 2])
            lv = p.dma("sync", xsl, xa, sem_ld)
            p.wait("tensor", sem_ld, lv)
            if c == 0:
                p.wait("tensor", g.s_gp, g.const_lvl)
            if c >= 2:
                p.wait("vector", g.s_st[c % 2], rst_lv[c - 2])
            for tt in range(NTT):
                tb = bank(7 if tt % 2 == 0 else 5)[:, 0:128]
                prev = (c, tt - 2) if tt >= 2 else \
                    ((c - 1, NTT - 2 + tt) if c >= 1 else None)
                if prev is not None:
                    p.wait("tensor", g.s_dve, ev_lv[prev])
                xlv = p.op("tensor", "transpose", tb,
                           xsl[:, tt * 128:(tt + 1) * 128], g.ident_f32[:],
                           sem=g.s_pe)
                p.wait("vector", g.s_pe, xlv)
                ev_lv[(c, tt)] = p.op(
                    "vector", "tensor_copy", sl[:, tt * 128:(tt + 1) * 128],
                    tb, sem=g.s_dve)
            g.xtp_lv[c] = xlv
        else:
            lv = p.dma("sync", sl, g.Rbuf[c * 128:(c + 1) * 128, :], sem_ld)
            if yspec is None or yspec[0] == "sbuf":
                p.wait("vector", sem_ld, lv)
            if out_mode and yspec is None:
                p.wait("tensor", sem_ld, lv)

        if yspec is not None:
            kind, yh = yspec
            if kind == "sbuf":
                ysl = yh[:, c * T:(c + 1) * T]
                add_lv[c] = p.op("vector", "tensor_tensor", sl, sl, ysl,
                                 ALU.add, sem=g.s_dve)
            elif out_mode:
                ysl = stag[:, T:2 * T]
                ylv = p.dma("sync", ysl,
                            bass.AP(yh, c * 128 * T, [[T, 128], [1, T]]),
                            sem_ld)
                p.wait("vector", sem_ld, ylv)
                add_lv[c] = p.op("vector", "tensor_tensor", sl, sl, ysl,
                                 ALU.add, sem=g.s_dve)
            else:
                ysl = g.ystag[:, (c % 2) * 2048:(c % 2) * 2048 + 2048]
                ylv = p.dma("sync", ysl,
                            bass.AP(yh, c * 128 * T, [[T, 128], [1, 2048]]),
                            sem_ld)
                ylv = p.dma("sync", g.ystag2[:, (c % 2) * 2048:(c % 2) * 2048 + 2048],
                            bass.AP(yh, c * 128 * T + 2048, [[T, 128], [1, 2048]]),
                            sem_ld)
                p.wait("vector", sem_ld, ylv)
                p.op("vector", "tensor_tensor", sl[:, 0:2048], sl[:, 0:2048],
                     ysl, ALU.add)
                add_lv[c] = p.op("vector", "tensor_tensor", sl[:, 2048:4096],
                                 sl[:, 2048:4096],
                                 g.ystag2[:, (c % 2) * 2048:(c % 2) * 2048 + 2048],
                                 ALU.add, sem=g.s_dve)

        if out_mode:
            if c >= 2:
                p.wait("vector", g.s_st[c % 2], st_lv[c - 2])
            for tt in range(NTT):
                tb = bank(7 if tt % 2 == 0 else 5)[:, 0:128]
                p.wait("tensor", g.s_dve, add_lv[c])
                if add_lv[c] is None and tt == 0 and c == 0:
                    p.wait("tensor", g.s_dve, dve_entry)
                    p.wait("tensor", g.s_act, act_entry)
                prev = (c, tt - 2) if tt >= 2 else \
                    ((c - 1, NTT - 2 + tt) if c >= 1 else None)
                if prev is not None:
                    p.wait("tensor", g.s_dve, ev_lv[prev])
                xlv = p.op("tensor", "transpose", tb,
                           sl[:, tt * 128:(tt + 1) * 128], g.ident_f32[:],
                           sem=g.s_pe)
                p.wait("vector", g.s_pe, xlv)
                ev_lv[(c, tt)] = p.op(
                    "vector", "tensor_copy",
                    g.xstage[:, (c % 2) * T + tt * 128:(c % 2) * T + (tt + 1) * 128],
                    tb, sem=g.s_dve)
            p.wait("sync", g.s_dve, ev_lv[(c, NTT - 1)])
            oa = bass.AP(g.x_out, c * 128,
                         [[1024, 128], [128 * 1024, 32], [1, 128]])
            st_lv[c] = p.dma("sync", oa,
                             g.xstage[:, (c % 2) * T:(c % 2) * T + T],
                             g.s_st[c % 2])
            continue

        if yspec is not None or rin_x:
            p.wait("sync", g.s_dve, add_lv[c] if yspec else ev_lv[(c, NTT - 1)])
            rst_lv[c] = p.dma("sync", g.Rbuf[c * 128:(c + 1) * 128, :], sl,
                              g.s_st[c % 2])

        rb = a_reg[:, c * T:(c + 1) * T]
        if yspec is not None:
            p.wait("vector", g.s_dve, add_lv[c])
        elif rin_x:
            p.wait("vector", g.s_dve, ev_lv[(c, NTT - 1)])
        rb_lv[c] = p.op("vector", "tensor_copy", rb, sl, sem=g.s_dve)
        p.wait("scalar", g.s_dve, rb_lv[c])
        if c >= 1:
            p.wait("scalar", g.s_pe, stat_mm[c - 1])
        sq_lv = p.op("scalar", "activation", sqbuf[:], rb, AF.Square,
                     sem=g.s_act)
        p.wait("tensor", g.s_dve, rb_lv[c])
        p.wait("tensor", g.s_act, sq_lv)
        if c == 0:
            p.wait("tensor", g.s_gp, g.const_lvl)
        else:
            p.wait("tensor", g.s_dve, stat_ev[c - 1])
        for tt in range(NTT):
            p.op("tensor", "matmul", stats_bank[:, 2 * tt:2 * tt + 1],
                 rb[:, tt * 128:(tt + 1) * 128], g.onescol_bf[:, 0:1],
                 start=True, stop=True)
            stat_mm[c] = p.op(
                "tensor", "matmul", stats_bank[:, 2 * tt + 1:2 * tt + 2],
                sqbuf[:, tt * 128:(tt + 1) * 128], g.onescol_bf[:, 0:1],
                start=True, stop=True,
                sem=(g.s_pe if tt == NTT - 1 else None))
        p.wait("vector", g.s_pe, stat_mm[c])
        p.wait("vector", g.s_dve, z_lv if c == 0 else stat_ev[c - 1])
        stat_ev[c] = p.op("vector", "tensor_tensor", g.statsb[:], g.statsb[:],
                          stats_bank, ALU.add, sem=g.s_dve)

    if out_mode:
        for c in range(NCK):
            p.wait("sync", g.s_st[c % 2], st_lv[c])
        return

    sb = g.statsb
    p.wait("vector", g.s_dve, stat_ev[NCK - 1])
    lv = p.op("vector", "tensor_scalar_mul", g.stat_m[:], sb[:, 0:64:2],
              1.0 / D, sem=g.s_dve)
    p.op("vector", "tensor_scalar_mul", g.stat_v[:], sb[:, 1:64:2], 1.0 / D)
    p.wait("vector", g.s_dve, lv)
    lv = p.op("vector", "tensor_tensor", g.stat_t[:], g.stat_m[:], g.stat_m[:],
              ALU.mult, sem=g.s_dve)
    p.wait("vector", g.s_dve, lv)
    lv = p.op("vector", "tensor_tensor", g.stat_v[:], g.stat_v[:], g.stat_t[:],
              ALU.subtract, sem=g.s_dve)
    p.wait("scalar", g.s_dve, lv)
    lv = p.op("scalar", "activation", g.stat_v[:], g.stat_v[:], AF.Ln,
              bias=g.epscol[:, 0:1], sem=g.s_act)
    p.wait("scalar", g.s_act, lv)
    lv = p.op("scalar", "activation", g.stat_v[:], g.stat_v[:], AF.Exp,
              scale=-0.5, sem=g.s_act)
    p.wait("vector", g.s_act, lv)
    lv = p.op("vector", "tensor_tensor", g.stat_t[:], g.stat_m[:], g.stat_v[:],
              ALU.mult, sem=g.s_dve)
    p.op("vector", "tensor_copy", g.stat_pk[:, 0:32], g.stat_v[:])
    p.wait("vector", g.s_dve, lv)
    lv = p.op("vector", "tensor_copy", g.stat_pk[:, 32:64], g.stat_t[:],
              sem=g.s_dve)
    p.wait("tensor", g.s_dve, lv)
    p.wait("tensor", g.s_dve, stat_ev[NCK - 1])
    for tt in range(NTT):
        p.op("tensor", "transpose", psbf[0:1, tt * 128:(tt + 1) * 128],
             g.stat_pk[:, tt:tt + 1], g.ident_bf[:])
    for tt in range(NTT):
        p.op("tensor", "transpose",
             psbf[0:1, 4096 + tt * 128:4096 + (tt + 1) * 128],
             g.stat_pk[:, 32 + tt:32 + tt + 1], g.ident_bf[:],
             sem=(g.s_pe if tt == NTT - 1 else None))
    p.wait("vector", g.s_pe, g.s_pe.n)
    p.op("vector", "tensor_copy", rstd_row[:], psbf[0:1, 0:4096])
    lv = p.op("vector", "tensor_copy", mr_row[:], psbf[0:1, 4096:8192],
              sem=g.s_dve)
    p.wait("tensor", g.s_dve, lv)
    bc_lv = {}
    for j in range(2 * NT5):
        row = rstd_row if j < NT5 else mr_row
        dstt = rstd_b if j < NT5 else mr_b
        jj = j % NT5
        b = bank(6 + (j % 2))
        if j >= 2:
            p.wait("tensor", g.s_dve, bc_lv[j - 2])
        p.op("tensor", "matmul", b, g.ones_bf[0:1, 0:128],
             row[:, jj * 512:(jj + 1) * 512], start=True, stop=True, sem=g.s_pe)
        p.wait("vector", g.s_pe, g.s_pe.n)
        bc_lv[j] = p.op("vector", "tensor_copy",
                        dstt[:, jj * 512:(jj + 1) * 512], b, sem=g.s_dve)

    if router:
        lgb = bank(4)[:, 0:256]
        zl = p.op("vector", "memset", g.lg_sb[:], 0.0, sem=g.s_dve)
        lg_mm = None
        lg_ev = None
    ap_lv = [None] * NCK
    for c in range(NCK):
        rb = a_reg[:, c * T:(c + 1) * T]
        if c == 0:
            p.wait("vector", g.s_dve, bc_lv[2 * NT5 - 1])
        lv = p.op("vector", "tensor_tensor", rb, rb, rstd_b[:], ALU.mult,
                  sem=g.s_dve)
        p.wait("vector", g.s_dve, lv)
        ap_lv[c] = p.op("vector", "tensor_tensor", rb, rb, mr_b[:],
                        ALU.subtract, sem=g.s_dve)
        if spill_A:
            p.wait("sync", g.s_dve, ap_lv[c])
            g.a_spill_lv[c] = p.dma("sync", g.Abuf[c * 128:(c + 1) * 128, :],
                                    rb, g.s_st[c % 2])
        if router:
            p.wait("tensor", g.s_dve, ap_lv[c])
            if c == 0:
                p.wait("tensor", g.s_rw, g.rw_lvl)
            else:
                p.wait("tensor", g.s_dve, lg_ev)
            for tt in range(NTT):
                lg_mm = p.op("tensor", "matmul", lgb[:, tt * 8:(tt + 1) * 8],
                             rb[:, tt * 128:(tt + 1) * 128],
                             g.rw_sb[:, c * 8:(c + 1) * 8],
                             start=True, stop=True,
                             sem=(g.s_pe if tt == NTT - 1 else None))
            p.wait("vector", g.s_pe, lg_mm)
            p.wait("vector", g.s_dve, zl if c == 0 else lg_ev)
            lg_ev = p.op("vector", "tensor_tensor", g.lg_sb[:], g.lg_sb[:],
                         lgb, ALU.add, sem=g.s_dve)
    g.a_lv = list(ap_lv)
    g.a_spilled = spill_A
    if router:
        emit_router_combine(p, g)


def emit_router_combine(p, g):
    L = g.lg_sb
    last = g.s_dve.n

    def dop(meth, *a, **k):
        nonlocal last
        p.wait("vector", g.s_dve, last)
        last = p.op("vector", meth, *a, sem=g.s_dve, **k)
        return last

    for tt in range(NTT):
        l8 = L[:, tt * 8:(tt + 1) * 8]
        t8 = g.r_tmp[:, tt * 8:(tt + 1) * 8]
        w8 = g.r_tmp2[:, tt * 8:(tt + 1) * 8]
        nm1 = g.r_m1[:, tt:tt + 1]
        nm2 = g.r_m2[:, tt:tt + 1]
        el = g.r_el[:, tt:tt + 1]
        dop("reduce_max", nm1, l8, axis=AX.X, negate=True)
        dop("tensor_scalar", t8, l8, nm1, None, ALU.add)
        dop("tensor_single_scalar", w8, t8, 0.0, ALU.is_ge)
        dop("scalar_tensor_tensor", w8, w8, -1e30, l8, ALU.mult, ALU.add)
        dop("reduce_max", nm2, w8, axis=AX.X, negate=True)
        lv = dop("tensor_scalar", w8, l8, nm2, None, ALU.add)
        p.wait("scalar", g.s_dve, lv)
        elv = p.op("scalar", "activation", t8, t8, AF.Exp, sem=g.s_act)
        dop("tensor_single_scalar", w8, w8, 0.0, ALU.is_ge)
        p.wait("vector", g.s_act, elv)
        dop("tensor_tensor", w8, w8, t8, ALU.mult)
        dop("reduce_sum", el, w8, axis=AX.X)
        dop("reciprocal", el, el)
        dop("tensor_scalar", g.c_tm[:, tt * 8:(tt + 1) * 8], w8, el, 0.5,
            ALU.mult, ALU.mult)
    g.ctm_lv = g.s_dve.n


def emit_moe(p, g, L, first_call, n_experts=8, n_tau=NT5, prefetch=True, no_accum=False):
    ps = g.ps
    psbf = g.psbf
    bank = g.bank
    W = g.weights
    w1 = W["L0_ffn_w1"]
    w3 = W["L0_ffn_w3"]
    w2 = W["L0_ffn_w2"]

    gbuf = g.m_gbuf
    asc = g.m_asc
    asl = g.m_asl
    w1sb = g.m_w1
    w3sb = g.m_w3
    w2sb = g.m_w2
    cbb = g.m_cbb
    ystg = g.m_ystg
    crow = g.m_crow

    assert g.a_spilled, "MoE requires A spilled to HBM"
    p.wait("vector", g.s_gp, g.s_gp.n)
    p.wait("gpsimd", g.s_gp, g.s_gp.n)
    for _s in g.s_st + g.s_ld:
        p.wait("vector", _s, _s.n)
        p.wait("gpsimd", _s, _s.n)

    def w1ap(e, fc):
        return bass.AP(w1, e * 1024 * 2048 + fc * 128,
                       [[2048, 128], [128 * 2048, 8], [1, 128]])

    def w3ap(e, fc):
        return bass.AP(w3, e * 1024 * 2048 + fc * 128,
                       [[2048, 128], [128 * 2048, 8], [1, 128]])

    def w2ap(e, fc):
        return bass.AP(w2, e * 2048 * 1024 + fc * 128 * 1024,
                       [[1024, 128], [1, 1024]])

    p.wait("gpsimd", g.s_pe, g.s_pe.n)
    p.wait("gpsimd", g.s_dve, g.s_dve.n)
    w13_lv = 0
    w2_lv = 0
    for fc in range(16):
        p.dma("gpsimd", w1sb[:, fc * 1024:(fc + 1) * 1024], w1ap(0, fc),
              g.s_wt[0])
        w13_lv = p.dma("gpsimd", w3sb[:, fc * 1024:(fc + 1) * 1024],
                       w3ap(0, fc), g.s_wt[0])
        w2_lv = p.dma("gpsimd", w2sb[:, fc * 1024:(fc + 1) * 1024],
                      w2ap(0, fc), g.s_w2)
    w13_done = [w13_lv] + [None] * 8
    w2_done = [w2_lv] + [None] * 8

    cb_lv = [0, 0]
    ystg_dma = [0, 0]
    yev = [0] * 8
    sil_lv = {}
    g1_lv = {}
    h3_last = 0
    asl_lv = 0
    asl_sem = g.s_ld[0]
    out_fc_lv = {}

    for e in range(n_experts):
        if not prefetch and e > 0:
            p.wait("gpsimd", g.s_pe, g.s_pe.n)
            p.wait("gpsimd", g.s_dve, g.s_dve.n)
            for fc in range(16):
                p.dma("gpsimd", w1sb[:, fc * 1024:(fc + 1) * 1024],
                      w1ap(e, fc), g.s_wt[e % 2])
                w13_done[e] = p.dma("gpsimd",
                                    w3sb[:, fc * 1024:(fc + 1) * 1024],
                                    w3ap(e, fc), g.s_wt[e % 2])
                w2_done[e] = p.dma("gpsimd", w2sb[:, fc * 1024:(fc + 1) * 1024],
                                   w2ap(e, fc), g.s_w2)
        p.wait("tensor", g.s_dve, g.ctm_lv)
        for dc in range(4):
            p.wait("tensor", g.s_dve, yev[dc])
        for tt in range(NTT):
            p.op("tensor", "transpose", psbf[0:1, tt * 128:(tt + 1) * 128],
                 g.c_tm[:, tt * 8 + e:tt * 8 + e + 1], g.ident_bf[:],
                 sem=(g.s_pe if tt == NTT - 1 else None))
        p.wait("vector", g.s_pe, g.s_pe.n)
        crow_lv = p.op("vector", "tensor_copy", crow[:], psbf[0:1, 0:4096],
                       sem=g.s_dve)

        for tau in range(n_tau):
            par = tau % 2
            p.wait("sync", g.s_pe, h3_last)
            aap = bass.AP(g.Abuf, tau * 512,
                          [[4096, 128], [128 * 4096, 8], [1, 512]])
            if e == 0 and tau == 0:
                p.wait("sync", g.s_st[0], g.a_spill_lv[NCK - 2])
                p.wait("sync", g.s_st[1], g.a_spill_lv[NCK - 1])
            asl_lv = p.dma("sync", asl[:], aap, asl_sem)
            p.wait("tensor", g.s_dve, crow_lv)
            p.wait("tensor", g.s_dve, yev[5])
            p.wait("tensor", g.s_dve, cb_lv[par])
            b5 = bank(5)
            lv = p.op("tensor", "matmul", b5, g.ones_bf[0:1, 0:128],
                      crow[:, tau * 512:(tau + 1) * 512], start=True,
                      stop=True, sem=g.s_pe)
            p.wait("vector", g.s_pe, lv)
            cbt = cbb[:, par * 512:(par + 1) * 512]
            cb_lv[par] = p.op("vector", "tensor_copy", cbt, b5, sem=g.s_dve)
            p.wait("vector", g.s_dve, cb_lv[par])
            p.wait("vector", asl_sem, asl_lv)
            p.wait("vector", g.s_pe, h3_last)
            asc_lv = None
            for dc in range(NCK):
                asc_lv = p.op("vector", "tensor_tensor",
                              asc[:, dc * 512:(dc + 1) * 512],
                              asl[:, dc * 512:(dc + 1) * 512], cbt,
                              ALU.mult, sem=g.s_dve)
            if tau == 0:
                p.wait("tensor", g.s_wt[e % 2], w13_done[e])
            for fc in range(16):
                fp = fc % 2
                bh1 = bank(fp)
                bh3 = bank(2 + fp)
                key = fc - 2 if fc >= 2 else 14 + fc
                if (e + tau > 0) or fc >= 2:
                    p.wait("tensor", g.s_act, sil_lv.get(key, 0))
                    p.wait("tensor", g.s_dve, g1_lv.get(key, 0))
                if fc < 2:
                    p.wait("tensor", g.s_dve, yev[fp])
                    p.wait("tensor", g.s_dve, yev[2 + fp])
                if e > 0 and tau == 0 and fc < 2:
                    p.wait("tensor", g.s_dve, crow_lv)
                if fc == 0:
                    p.wait("tensor", asl_sem, asl_lv)
                h1_lv = None
                for dc in range(NCK):
                    h1_lv = p.op("tensor", "matmul", bh1,
                                 w1sb[:, fc * 1024 + dc * 128:
                                      fc * 1024 + (dc + 1) * 128],
                                 asl[:, dc * 512:(dc + 1) * 512],
                                 start=(dc == 0), stop=(dc == NCK - 1),
                                 sem=(g.s_pe if dc == NCK - 1 else None))
                if fc == 0:
                    p.wait("tensor", g.s_dve, asc_lv)
                h3_lv = None
                for dc in range(NCK):
                    h3_lv = p.op("tensor", "matmul", bh3,
                                 w3sb[:, fc * 1024 + dc * 128:
                                      fc * 1024 + (dc + 1) * 128],
                                 asc[:, dc * 512:(dc + 1) * 512],
                                 start=(dc == 0), stop=(dc == NCK - 1),
                                 sem=(g.s_pe if dc == NCK - 1 else None))
                h3_last = h3_lv
                if tau == n_tau - 1:
                    h13_final = h3_lv
                    if prefetch and e < n_experts - 1:
                        p.wait("gpsimd", g.s_pe, h3_lv)
                        p.dma("gpsimd", w1sb[:, fc * 1024:(fc + 1) * 1024],
                              w1ap(e + 1, fc), g.s_wt[(e + 1) % 2])
                        lvw = p.dma("gpsimd",
                                    w3sb[:, fc * 1024:(fc + 1) * 1024],
                                    w3ap(e + 1, fc), g.s_wt[(e + 1) % 2])
                        if fc == 15:
                            w13_done[e + 1] = lvw
                p.wait("scalar", g.s_pe, h1_lv)
                gslice = gbuf[:, fc * 512:(fc + 1) * 512]
                sil_lv[fc] = p.op("scalar", "activation", gslice, bh1, AF.Silu,
                                  sem=g.s_act)
                p.wait("vector", g.s_act, sil_lv[fc])
                p.wait("vector", g.s_pe, h3_lv)
                g1_lv[fc] = p.op("vector", "tensor_tensor", gslice, gslice,
                                 bh3, ALU.mult, sem=g.s_dve)
            if tau == 0:
                p.wait("tensor", g.s_w2, w2_done[e])
            out_mm = [None] * 8
            last_tau = (tau == n_tau - 1)
            for fc in range(16):
                p.wait("tensor", g.s_dve, g1_lv[fc])
                for dc in (4, 5, 6, 7, 0, 1, 2, 3):
                    if fc == 0:
                        p.wait("tensor", g.s_dve, yev[dc])
                        if dc == 0:
                            p.wait("tensor", g.s_act, sil_lv[15])
                            p.wait("tensor", g.s_dve, g1_lv[15])
                    want_sem = (fc == 15) or (last_tau and dc == 3)
                    out_mm[dc] = p.op(
                        "tensor", "matmul", bank(dc),
                        w2sb[:, fc * 1024 + dc * 128:fc * 1024 + (dc + 1) * 128],
                        gbuf[:, fc * 512:(fc + 1) * 512],
                        start=(fc == 0), stop=(fc == 15),
                        sem=(g.s_pe if want_sem else None))
                if last_tau:
                    out_fc_lv[fc] = out_mm[3]
            if last_tau and prefetch and e < n_experts - 1:
                for fc in range(16):
                    p.wait("gpsimd", g.s_pe, out_fc_lv[fc])
                    lvw = p.dma("gpsimd", w2sb[:, fc * 1024:(fc + 1) * 1024],
                                w2ap(e + 1, fc), g.s_w2)
                    if fc == 15:
                        w2_done[e + 1] = lvw
            for dc in range(8):
                p.wait("vector", g.s_pe, out_mm[dc])
                sp = dc % 2
                p.wait("vector", g.s_ya[sp], ystg_dma[sp])
                yev[dc] = p.op("vector", "tensor_copy",
                               ystg[:, sp * 512:(sp + 1) * 512], bank(dc),
                               sem=g.s_dve)
                p.wait("gpsimd", g.s_dve, yev[dc])
                ya = bass.AP(g.Ybuf, dc * 128 * T + tau * 512,
                             [[T, 128], [1, 512]])
                kw = {} if (e == 0 or no_accum) else {"accum_op": ALU.add}
                ystg_dma[sp] = p.dma("gpsimd", ya,
                                     ystg[:, sp * 512:(sp + 1) * 512],
                                     g.s_ya[sp], **kw)
    g.moe_done = True


def emit_attn0(p, g, a_reg):
    ps = g.ps
    bank = g.bank
    W = g.weights
    oreg = g.REGB if a_reg is g.REGA else g.REGA

    qb, kb, vb = g.a0_q, g.a0_k, g.a0_v
    cosb, sinb = g.a0_cos, g.a0_sin
    wqs, wqr, wks, wkr, wvs = g.a0_wq, g.a0_wqr, g.a0_wk, g.a0_wkr, g.a0_wv
    pb = g.a0_p
    t1 = g.a0_t1
    rl = g.a0_t1
    t2 = g.a0_t2
    rlbf, bcb, tmpB = g.a0_rlbf, g.a0_bcb, g.a0_tmpB
    rows = g.a0_rows
    mm8 = g.a0_mm
    negmb = g.a0_negmb
    sqb = g.a0_sq
    invf = g.a0_invf

    last = g.s_dve.n

    def dop(meth, *a, **k):
        nonlocal last
        p.wait("vector", g.s_dve, last)
        last = p.op("vector", meth, *a, sem=g.s_dve, **k)
        return last

    for eng in ("gpsimd", "tensor"):
        p.wait(eng, g.s_pe, g.s_pe.n)
        p.wait(eng, g.s_dve, g.s_dve.n)
        p.wait(eng, g.s_gp, g.s_gp.n)
        for _s in g.s_st + g.s_ld:
            p.wait(eng, _s, _s.n)
    p.wait("vector", g.s_ld[0], g.s_ld[0].n)
    p.wait("vector", g.s_ld[1], g.s_ld[1].n)
    p.wait("vector", g.s_gp, g.s_gp.n)

    i32 = g.a0_i32b
    ntile = g.a0_ntile
    tang = g.a0_tang
    p.op("gpsimd", "iota", invf[0:1, 0:32], pattern=[[1, 32]], base=0,
         channel_multiplier=0, allow_small_or_imprecise_dtypes=True,
         sem=g.s_gp)
    iota_lv = g.s_gp.n
    p.op("gpsimd", "memset", vb[:], 1.0, sem=g.s_gp)
    vb1_lv = g.s_gp.n
    p.wait("vector", g.s_gp, iota_lv)
    lnf = math.log(10000.0) * 2.0 / 64.0
    dop("tensor_scalar_mul", invf[0:1, 0:32], invf[0:1, 0:32], -lnf)
    p.wait("scalar", g.s_dve, last)
    alv = p.op("scalar", "activation", invf[0:1, 0:32], invf[0:1, 0:32],
               AF.Exp, sem=g.s_act)
    p.wait("vector", g.s_act, alv)
    for off in (32, 33, 96, 97):
        dop("tensor_copy", invf[0:1, off:min(off + 64, 160):2],
            invf[0:1, 0:32])
    invf_lv = last
    p.wait("tensor", g.s_dve, invf_lv)
    tl = p.op("tensor", "transpose", g.ps[:, 0:1],
              invf[0:1, 32:160], g.ident_f32[0:1, 0:1], sem=g.s_pe)
    p.wait("vector", g.s_pe, tl)
    invfc_lv = dop("tensor_copy", g.a0_invfc[:], g.ps[:, 0:1])
    for j in range(NT5):
        p.wait("gpsimd", g.s_dve, last)
        p.op("gpsimd", "iota", ntile[:], pattern=[[1, 512]], base=j * 512,
             channel_multiplier=0, allow_small_or_imprecise_dtypes=True,
             sem=g.s_gp)
        p.wait("vector", g.s_gp, g.s_gp.n)
        dop("tensor_scalar_mul", tang[:], ntile[:], g.a0_invfc[:, 0:1])
        for cw in range(2):
            dst = (sinb if cw == 0 else cosb)[:, j * 512:(j + 1) * 512]
            dop("tensor_scalar", t1[:], tang[:], 1.0 / (2 * math.pi),
                0.25 * cw, ALU.mult, ALU.add)
            dop("tensor_copy", i32[:], t1[:])
            dop("tensor_copy", t2[:], i32[:])
            dop("tensor_tensor", t2[:], t2[:], t1[:], ALU.is_gt)
            dop("tensor_copy", t1[:], i32[:])
            dop("tensor_tensor", t1[:], t1[:], t2[:], ALU.subtract)
            if cw == 0:
                dop("scalar_tensor_tensor", t1[:], t1[:], -2 * math.pi,
                    tang[:], ALU.mult, ALU.add)
            else:
                dop("tensor_scalar", t2[:], tang[:], 1.0, math.pi / 2,
                    ALU.mult, ALU.add)
                dop("scalar_tensor_tensor", t1[:], t1[:], -2 * math.pi,
                    t2[:], ALU.mult, ALU.add)
            p.wait("scalar", g.s_dve, last)
            a1 = p.op("scalar", "activation", dst, t1[:], AF.Sin,
                      bias=g.negpi[:, 0:1], sem=g.s_act)
            p.wait("vector", g.s_act, a1)
            dop("tensor_scalar_mul", dst, dst, -1.0)
        cs_act = g.s_act.n

    p.wait("gpsimd", g.s_dve, last)
    bl = 0
    for i, nm in enumerate(["L0_attn_bq", "L0_attn_bk", "L0_attn_bv"]):
        off = (0, 2048, 4096)[i]
        bl = p.dma("gpsimd", rows[0:1, off:off + 1024],
                   bass.AP(W[nm], 0, [[1024, 1], [1, 1024]]), g.s_rw)
    p.wait("vector", g.s_rw, bl)
    for src_off, dst_off in ((0, 1024), (2048, 3072)):
        dop("tensor_scalar_mul", rows[0:1, dst_off:dst_off + 1024:2],
            rows[0:1, src_off + 1:src_off + 1024:2], -1.0)
        dop("tensor_copy", rows[0:1, dst_off + 1:dst_off + 1024:2],
            rows[0:1, src_off:src_off + 1024:2])
    rotb_lv = last

    a_ready = max(g.a_lv)
    negm8p = g.a1_negm8
    last_proj_mm = 0
    pspill = [0, 0]
    for pr in range(8):
        sw = g.s_wt[pr % 2]
        p.wait("gpsimd", g.s_pe, last_proj_mm)
        wl = 0
        for mat, dst in (("L0_attn_Wq", wqs), ("L0_attn_Wk", wks),
                         ("L0_attn_Wv", wvs)):
            wl = p.dma("gpsimd", dst[:],
                       bass.AP(W[mat], pr * 128,
                               [[1024, 128], [128 * 1024, 8], [1, 128]]), sw)
        p.wait("vector", sw, wl)
        p.wait("vector", g.s_pe, last_proj_mm)
        p.wait("vector", g.s_st[(pr + 1) % 2], pspill[(pr + 1) % 2])
        for src, dst in ((wqs, wqr), (wks, wkr)):
            dop("tensor_scalar_mul", dst[:, 0:1024:2], src[:, 1:1024:2], -1.0)
            dop("tensor_copy", dst[:, 1:1024:2], src[:, 0:1024:2])
        wrot_lv = last
        mm_mx = dop("memset", mm8[:], -3e38)

        p.wait("tensor", g.s_dve, wrot_lv)
        if pr == 0:
            p.wait("tensor", g.s_dve, a_ready)
            p.wait("tensor", g.s_act, cs_act)
            p.wait("tensor", g.s_dve, rotb_lv)
        rope_ev = {}
        for qk in range(2):
            wsl, wrl, dstbuf, boff = ((wqs, wqr, qb, 0) if qk == 0
                                      else (wks, wkr, kb, 2048))
            for j in range(NT5):
                b0 = bank(j % 2)
                b1 = bank(2 + j % 2)
                if (qk, j) not in ((0, 0), (0, 1)):
                    prev = (qk, j - 2) if j >= 2 else (qk - 1, NT5 - 2 + j)
                    p.wait("tensor", g.s_dve, rope_ev[prev])
                for dc in range(NCK):
                    p.op("tensor", "matmul", b0,
                         wsl[:, dc * 128:(dc + 1) * 128],
                         a_reg[:, dc * T + j * 512:dc * T + (j + 1) * 512],
                         start=(dc == 0), stop=False)
                p.op("tensor", "matmul", b0,
                     rows[0:1, boff + pr * 128:boff + (pr + 1) * 128],
                     g.ones_bf[0:1, 0:512], start=False, stop=True)
                for dc in range(NCK):
                    p.op("tensor", "matmul", b1,
                         wrl[:, dc * 128:(dc + 1) * 128],
                         a_reg[:, dc * T + j * 512:dc * T + (j + 1) * 512],
                         start=(dc == 0), stop=False)
                pj = p.op("tensor", "matmul", b1,
                          rows[0:1, boff + 1024 + pr * 128:
                               boff + 1024 + (pr + 1) * 128],
                          g.ones_bf[0:1, 0:512], start=False, stop=True,
                          sem=g.s_pe)
                p.wait("vector", g.s_pe, pj)
                dsl = dstbuf[:, j * 512:(j + 1) * 512]
                dop("tensor_tensor", t1[:], b0,
                    cosb[:, j * 512:(j + 1) * 512], ALU.mult)
                dop("tensor_tensor", dsl, b1,
                    sinb[:, j * 512:(j + 1) * 512], ALU.mult)
                rope_ev[(qk, j)] = dop("tensor_tensor", dsl, dsl, t1[:],
                                       ALU.add)
                p.wait("scalar", g.s_dve, rope_ev[(qk, j)])
                sq = p.op("scalar", "activation", sqb[:], dsl, AF.Square,
                          sem=g.s_act)
                for h in range(2):
                    p.wait("tensor", g.s_act, sq)
                    if not (pr == 0 and qk == 0 and j == 0 and h == 0):
                        p.wait("tensor", g.s_dve, last)
                    mmh = p.op("tensor", "matmul", bank(6)[0:1, :],
                               g.onescol_bf[h * 64:h * 64 + 64, 0:1],
                               sqb[h * 64:h * 64 + 64, :],
                               start=True, stop=True, sem=g.s_pe)
                    p.wait("vector", g.s_pe, mmh)
                    dop("reduce_max", mm8[0:1, 15:16], bank(6)[0:1, :],
                        axis=AX.X)
                    dop("tensor_tensor", mm8[0:1, qk * 2 + h:qk * 2 + h + 1],
                        mm8[0:1, qk * 2 + h:qk * 2 + h + 1],
                        mm8[0:1, 15:16], ALU.max)
        p.wait("tensor", g.s_gp, vb1_lv)
        vp_ev = {}
        for tt in range(NTT):
            bv_ = bank(4 + tt % 2)
            if tt >= 2:
                p.wait("tensor", g.s_dve, vp_ev[tt - 2])
            for dc in range(NCK):
                p.op("tensor", "matmul", bv_[:, 0:128],
                     a_reg[:, dc * T + tt * 128:dc * T + (tt + 1) * 128],
                     wvs[:, dc * 128:(dc + 1) * 128],
                     start=(dc == 0), stop=False)
            pv = p.op("tensor", "matmul", bv_[:, 0:128], g.ones_bf[0:1, 0:128],
                      rows[0:1, 4096 + pr * 128:4096 + (pr + 1) * 128],
                      start=False, stop=True, sem=g.s_pe)
            p.wait("vector", g.s_pe, pv)
            dop("tensor_copy", vb[:, tt * 130:tt * 130 + 64], bv_[:, 0:64])
            vp_ev[tt] = dop("tensor_copy",
                            vb[:, tt * 130 + 65:tt * 130 + 129],
                            bv_[:, 64:128])
        last_proj_mm = g.s_pe.n
        p.wait("scalar", g.s_dve, last)
        alv = p.op("scalar", "activation", mm8[0:1, 8:12], mm8[0:1, 0:4],
                   AF.Ln, bias=g.epscol[0:1, 0:1], sem=g.s_act)
        p.wait("vector", g.s_act, alv)
        dop("tensor_tensor", mm8[0:1, 12:14], mm8[0:1, 8:10],
            mm8[0:1, 10:12], ALU.add)
        p.wait("scalar", g.s_dve, last)
        alv = p.op("scalar", "activation", mm8[0:1, 12:14], mm8[0:1, 12:14],
                   AF.Exp, scale=0.5, bias=g.negln8[0:1, 0:1], sem=g.s_act)
        p.wait("vector", g.s_act, alv)
        dop("reduce_max", mm8[0:1, 14:15], mm8[0:1, 12:14], axis=AX.X,
            negate=True)
        dop("tensor_copy", g.a0_negbf[0:1, :], mm8[0:1, 14:15])
        p.wait("tensor", g.s_dve, last)
        bmm = p.op("tensor", "matmul", bank(6)[:, 0:1], g.ones_bf[0:1, 0:128],
                   g.a0_negbf[0:1, 0:1], start=True, stop=True, sem=g.s_pe)
        p.wait("vector", g.s_pe, bmm)
        dop("tensor_copy", negm8p[:, pr:pr + 1], bank(6)[:, 0:1])
        p.wait("sync", g.s_dve, last)
        p.dma("sync", g.QS[pr * 128:(pr + 1) * 128, :], qb[:], g.s_st[pr % 2])
        p.dma("sync", g.KS[pr * 128:(pr + 1) * 128, :], kb[:], g.s_st[pr % 2])
        pspill[pr % 2] = p.dma(
            "sync", g.VS[pr * 128:(pr + 1) * 128, :], vb[:], g.s_st[pr % 2])
    spill_done = (g.s_st[0].n, g.s_st[1].n)

    qsl2 = (qb, g.a0_qs1)
    ksl2 = (kb, g.a0_ks1)
    vsl2 = (vb, g.a0_vs1)
    last_exp = 0
    av_ev = {}
    bc_ev = {}
    prev_b_dma = 0
    exp_lv = {}
    av_lv = {}
    pair_ld = [0, 0]
    pair_att_end = [0, 0]
    for pr in range(8):
        par = pr % 2
        sem_ld = g.s_ld[par]
        qbuf, kbuf, vbuf = qsl2[par], ksl2[par], vsl2[par]
        p.wait("sync", g.s_st[0], spill_done[0])
        p.wait("sync", g.s_st[1], spill_done[1])
        p.wait("sync", g.s_pe, pair_att_end[par])
        p.dma("sync", qbuf[:], g.QS[pr * 128:(pr + 1) * 128, :], sem_ld)
        p.dma("sync", kbuf[:], g.KS[pr * 128:(pr + 1) * 128, :], sem_ld)
        pair_ld[par] = p.dma("sync", vbuf[:],
                             g.VS[pr * 128:(pr + 1) * 128, :], sem_ld)
        p.wait("tensor", sem_ld, pair_ld[par])
        opair = oreg[:, pr * 4096:(pr + 1) * 4096]
        for qc in range(NT5):
            avA = bank(4)
            avB = bank(5)

            def emit_av(kt, qcv):
                p.wait("tensor", g.s_act, exp_lv[kt])
                if kt == 0:
                    p.wait("tensor", g.s_dve, av_ev.get((pr, qcv - 1), 0))
                    if qcv == 0 and pr > 0:
                        p.wait("tensor", g.s_dve,
                               av_ev.get((pr - 1, NT5 - 1), 0))
                pbsl = pb[:, (kt % 2) * 1024:(kt % 2) * 1024 + 1024]
                p.op("tensor", "matmul", avA[0:65, :],
                     vbuf[:, kt * 130:kt * 130 + 65], pbsl[:, 0:512],
                     start=(kt == 0), stop=(kt == NTT - 1))
                av_lv[kt] = p.op("tensor", "matmul", avB[0:65, :],
                                 vbuf[:, kt * 130 + 65:kt * 130 + 130],
                                 pbsl[:, 512:1024],
                                 start=(kt == 0), stop=(kt == NTT - 1),
                                 sem=g.s_pe)

            for kt in range(NTT):
                sb0 = bank(2 * (kt % 2))
                sb1 = bank(2 * (kt % 2) + 1)
                if kt >= 2:
                    p.wait("tensor", g.s_act, exp_lv[kt - 2])
                elif qc > 0 or pr > 0:
                    p.wait("tensor", g.s_act,
                           exp_lv.get(NTT - 2 + kt, last_exp))
                p.op("tensor", "matmul", sb0,
                     kbuf[0:64, kt * 128:(kt + 1) * 128],
                     qbuf[0:64, qc * 512:(qc + 1) * 512], start=True,
                     stop=True, tile_position=(0, 0))
                smm = p.op("tensor", "matmul", sb1,
                           kbuf[64:128, kt * 128:(kt + 1) * 128],
                           qbuf[64:128, qc * 512:(qc + 1) * 512],
                           start=True, stop=True, tile_position=(64, 0),
                           sem=g.s_pe)
                if kt >= 1:
                    emit_av(kt - 1, qc)
                p.wait("scalar", g.s_pe, smm)
                if kt >= 2:
                    p.wait("scalar", g.s_pe, av_lv[kt - 2])
                elif qc > 0 or pr > 0:
                    p.wait("scalar", g.s_pe, av_lv.get(NTT - 2 + kt, 0))
                pbsl = pb[:, (kt % 2) * 1024:(kt % 2) * 1024 + 1024]
                psp = ps[:, 2 * (kt % 2) * 512:2 * (kt % 2) * 512 + 1024]
                exp_lv[kt] = p.op("scalar", "activation", pbsl, psp, AF.Exp,
                                  bias=negm8p[:, pr:pr + 1], scale=0.125,
                                  sem=g.s_act)
            emit_av(NTT - 1, qc)
            last_exp = exp_lv[NTT - 1]
            p.wait("vector", g.s_pe, av_lv[NTT - 1])
            for h, avb_ in ((0, avA), (1, avB)):
                dop("reciprocal", rlbf[64:65, 0:512], avb_[64:65, :],
                    low_prec=True)
                p.wait("tensor", g.s_dve, last)
                prevk = (pr, qc - 1, h) if qc > 0 else (pr - 1, NT5 - 1, h)
                p.wait("tensor", g.s_dve, bc_ev.get(prevk, 0))
                bmm = p.op("tensor", "matmul", bank(6 + h)[0:64, :],
                           g.ones_bf[64:65, 0:64], rlbf[64:65, 0:512],
                           start=True, stop=True, sem=g.s_pe)
                p.wait("vector", g.s_pe, bmm)
                dop("tensor_copy", bcb[0:64, :], bank(6 + h)[0:64, :])
                bc_ev[(pr, qc, h)] = last
                if h == 0:
                    dop("tensor_tensor", opair[0:64, qc * 512:(qc + 1) * 512],
                        avb_[0:64, :], bcb[0:64, :], ALU.mult)
                else:
                    p.wait("vector", g.s_st[1], prev_b_dma)
                    dop("tensor_tensor", tmpB[0:64, :], avb_[0:64, :],
                        bcb[0:64, :], ALU.mult)
                    p.wait("sync", g.s_dve, last)
                    prev_b_dma = p.dma(
                        "sync", opair[64:128, qc * 512:(qc + 1) * 512],
                        tmpB[0:64, :], g.s_st[1])
            av_ev[(pr, qc)] = last
        pair_att_end[par] = g.s_pe.n

    wpf = g.a0_wpf
    bprow = g.a0_bprow
    p.wait("gpsimd", g.s_pe, g.s_pe.n)
    p.wait("gpsimd", g.s_dve, g.s_dve.n)
    p.wait("gpsimd", g.s_act, g.s_act.n)
    wl = 0
    for dc in range(NCK):
        wl = p.dma("gpsimd", wpf[:, dc * 1024:(dc + 1) * 1024],
                   bass.AP(W["L0_attn_Wp"], dc * 128 * 1024,
                           [[1024, 128], [1, 1024]]), g.s_wt[0])
    blr = p.dma("gpsimd", bprow[0:1, :],
                bass.AP(W["L0_attn_bp"], 0, [[1024, 1], [1, 1024]]), g.s_rw)
    p.wait("tensor", g.s_wt[0], wl)
    p.wait("tensor", g.s_rw, blr)
    p.wait("tensor", g.s_st[1], prev_b_dma)
    p.wait("tensor", g.s_dve, g.s_dve.n)
    p.wait("tensor", g.s_act, g.s_act.n)
    yev = {}
    for tau in range(NT5):
        for oc in range(NCK):
            i = tau * NCK + oc
            b = bank(i % 4)
            if i >= 4:
                p.wait("tensor", g.s_dve, yev[i - 4])
            for pr2 in range(8):
                p.op("tensor", "matmul", b,
                     wpf[:, pr2 * 1024 + oc * 128:pr2 * 1024 + (oc + 1) * 128],
                     oreg[:, pr2 * 4096 + tau * 512:
                          pr2 * 4096 + (tau + 1) * 512],
                     start=(pr2 == 0), stop=False)
            mm = p.op("tensor", "matmul", b,
                      bprow[0:1, oc * 128:(oc + 1) * 128],
                      g.ones_bf[0:1, 0:512], start=False, stop=True,
                      sem=g.s_pe)
            p.wait("vector", g.s_pe, mm)
            yev[i] = p.op("vector", "tensor_copy",
                          a_reg[:, oc * T + tau * 512:oc * T + (tau + 1) * 512],
                          b, sem=g.s_dve)
    g.attn0_done = g.s_dve.n


def emit_attn1(p, g, a_reg, decay_reg):
    ps = g.ps
    bank = g.bank
    W = g.weights
    dec = g.a1_decA if decay_reg is g.REGA else g.a1_decB

    ntile, ebuf = g.a1_nt, g.a1_eb
    wo = g.a1_wo
    gcol, bcol, borow = g.a1_gcol, g.a1_bcol, g.a1_borow
    thr = g.a1_th
    wq_, wk_, wv_ = g.a1_wq, g.a1_wk, g.a1_wv
    qh, kh, vh = g.a1_qh, g.a1_kh, g.a1_vh
    ksl, vsl, qsl = g.a1_qh, g.a1_kh, g.a1_qsl
    sd, pbuf = g.a1_sd, g.a1_pb
    lnb = g.a1_ln
    sqb = g.a1_sq
    bcb, rlbf = g.a1_bcb, g.a1_rlbf
    r1, r2 = g.a1_r1, g.a1_r2
    mrow, rrow = g.a1_mrow, g.a1_rrow
    negmball = g.a1_negm8
    mm8 = g.a0_mm

    last = g.s_dve.n

    def dop(meth, *a, **k):
        nonlocal last
        p.wait("vector", g.s_dve, last)
        last = p.op("vector", meth, *a, sem=g.s_dve, **k)
        return last

    for eng in ("gpsimd", "tensor"):
        p.wait(eng, g.s_pe, g.s_pe.n)
        p.wait(eng, g.s_dve, g.s_dve.n)
        p.wait(eng, g.s_gp, g.s_gp.n)
        for _s in g.s_st + g.s_ld:
            p.wait(eng, _s, _s.n)
    p.wait("vector", g.s_gp, g.s_gp.n)

    p.op("gpsimd", "memset", g.onesf32[:], 1.0)
    tl = p.dma("gpsimd", thr[0:1, 0:8],
               bass.AP(W["L1_attn_theta"], 0, [[8, 1], [1, 8]]), g.s_rw)
    grow = g.a1_grow
    tl = p.dma("gpsimd", grow[0:1, 0:1024],
               bass.AP(W["L1_attn_gamma"], 0, [[1024, 1], [1, 1024]]), g.s_rw)
    tl = p.dma("gpsimd", grow[0:1, 1024:2048],
               bass.AP(W["L1_attn_beta"], 0, [[1024, 1], [1, 1024]]), g.s_rw)
    bl = p.dma("gpsimd", borow[0:1, :],
               bass.AP(W["L1_attn_bo"], 0, [[1024, 1], [1, 1024]]), g.s_rw)
    wl = 0
    for dc in range(NCK):
        wl = p.dma("gpsimd", wo[:, dc * 1024:(dc + 1) * 1024],
                   bass.AP(W["L1_attn_Wo"], dc * 128 * 1024,
                           [[1024, 128], [1, 1024]]), g.s_wt[0])
    p.wait("tensor", g.s_rw, bl)
    p.wait("tensor", g.s_gp, g.const_lvl)
    for i in range(16):
        p.op("tensor", "transpose", g.psbf[:, 2 * i:2 * i + 1],
             grow[0:1, i * 128:(i + 1) * 128], g.ident_bf[0:1, 0:1],
             sem=(g.s_pe if i == 15 else None))
    p.wait("vector", g.s_pe, g.s_pe.n)
    for dc in range(NCK):
        dop("tensor_copy", gcol[:, dc:dc + 1], g.psbf[:, 2 * dc:2 * dc + 1])
        dop("tensor_copy", bcol[:, dc:dc + 1],
            g.psbf[:, 16 + 2 * dc:17 + 2 * dc])
    gcol_lv = last
    p.wait("tensor", g.s_wt[0], wl)
    p.wait("tensor", g.s_dve, gcol_lv)
    b4 = ps[:, 4 * 512:4 * 512 + 1024]
    bo_mm = None
    for half in range(2):
        for dc in range(NCK):
            bo_mm = p.op("tensor", "matmul",
                         ps[:, (4 + half) * 512:(5 + half) * 512][0:1, :],
                         bcol[:, dc:dc + 1],
                         wo[:, dc * 1024 + half * 512:
                            dc * 1024 + half * 512 + 512],
                         start=(dc == 0), stop=(dc == NCK - 1),
                         sem=(g.s_pe if dc == NCK - 1 else None))
    p.wait("vector", g.s_pe, bo_mm)
    dop("tensor_tensor", borow[0:1, :], borow[0:1, :], b4[0:1, :], ALU.add)
    for dc in range(NCK):
        dop("tensor_scalar_mul", wo[:, dc * 1024:(dc + 1) * 1024],
            wo[:, dc * 1024:(dc + 1) * 1024], gcol[:, dc:dc + 1])
    wo_lv = last
    p.wait("vector", g.s_rw, g.s_rw.n)
    dop("tensor_scalar_mul", thr[0:1, 8:16], thr[0:1, 0:8], -1.0)
    th_lv = last
    p.wait("tensor", g.s_dve, th_lv)
    thc = g.a1_thc
    for h in range(8):
        p.op("tensor", "matmul", ps[:, 5 * 512 + h:5 * 512 + h + 1],
             g.onesf32[0:1, :], thr[0:1, 8 + h:9 + h], start=True, stop=True,
             sem=(g.s_pe if h == 7 else None))
    p.wait("vector", g.s_pe, g.s_pe.n)
    dop("tensor_copy", thc[:], ps[:, 5 * 512:5 * 512 + 8])
    thc_lv = last
    mtile = g.a1_mt
    for ch in range(8):
        p.wait("gpsimd", g.s_dve, last)
        p.op("gpsimd", "iota", ntile[:], pattern=[[128, 4], [1, 512]],
             base=-384 + ch * 4 * 128, channel_multiplier=-1,
             allow_small_or_imprecise_dtypes=True, sem=g.s_gp)
        p.wait("vector", g.s_gp, g.s_gp.n)
        dop("tensor_single_scalar", mtile[:], ntile[:], 0.0, ALU.is_ge)
        dop("tensor_single_scalar", ntile[:], ntile[:], 0.0, ALU.max)
        ntl = last
        dsl = dec[:, ch * 2048:(ch + 1) * 2048]
        for h in range(8):
            p.wait("scalar", g.s_dve, ntl if h == 0 else last)
            alv = p.op("scalar", "activation", (dsl if h == 0 else ebuf[:]),
                       ntile[:], AF.Exp, scale=thc[:, h:h + 1], sem=g.s_act)
            if h > 0:
                p.wait("vector", g.s_act, alv)
                dop("tensor_tensor", dsl, dsl, ebuf[:], ALU.add)
        dop("scalar_tensor_tensor", dsl, dsl, 0.125, mtile[:], ALU.mult,
            ALU.mult)
    ngm = g.a1_ngm
    p.wait("gpsimd", g.s_act, g.s_act.n)
    p.wait("gpsimd", g.s_dve, g.s_dve.n)
    nmz = p.op("gpsimd", "memset", ngm[:], 0.0, sem=g.s_gp)
    p.wait("gpsimd", g.s_gp, nmz)
    for di in range(4):
        p.op("gpsimd", "affine_select", ngm[:, di * 512:(di + 1) * 512],
             ngm[:, di * 512:(di + 1) * 512], pattern=[[1, 512]],
             compare_op=ALU.is_ge, fill=-1e30, base=-128 * di,
             channel_multiplier=-1, sem=(g.s_gp if di == 3 else None))
    dec_lv = g.s_gp.n
    p.wait("tensor", g.s_dve, last)

    a_ready = max(g.a_lv)
    hspill = [0, 0]
    last_pj = 0
    ev_lv = {}
    for h in range(8):
        sw = g.s_wt[h % 2]
        p.wait("gpsimd", g.s_pe, last_pj)
        wl = 0
        for mat, dst in (("L1_attn_Wq", wq_), ("L1_attn_Wk", wk_),
                         ("L1_attn_Wv", wv_)):
            wl = p.dma("gpsimd", dst[:],
                       bass.AP(W[mat], h * 128,
                               [[1024, 128], [128 * 1024, 8], [1, 128]]), sw)
        p.wait("tensor", sw, wl)
        if h == 0:
            p.wait("tensor", g.s_dve, a_ready)
        p.wait("vector", g.s_st[(h + 1) % 2], hspill[(h + 1) % 2])
        mm_mx = dop("memset", mm8[:], -3e38)
        for qk in range(2):
            wsl = wq_ if qk == 0 else wk_
            buf = qh if qk == 0 else kh
            for j in range(NT5):
                b0 = bank(j % 2)
                if (qk, j) not in ((0, 0), (0, 1)):
                    prev_lv = ev_lv[(qk, j - 2) if j >= 2
                                    else (qk - 1, NT5 - 2 + j)]
                    p.wait("tensor", g.s_dve, prev_lv)
                elif h > 0:
                    p.wait("tensor", g.s_dve, ev_lv[(1, NT5 - 2 + j)])
                mmq = None
                for dc in range(NCK):
                    mmq = p.op("tensor", "matmul", b0,
                               wsl[:, dc * 128:(dc + 1) * 128],
                               a_reg[:, dc * T + j * 512:
                                     dc * T + (j + 1) * 512],
                               start=(dc == 0), stop=(dc == NCK - 1),
                               sem=(g.s_pe if dc == NCK - 1 else None))
                p.wait("vector", g.s_pe, mmq)
                dsl_ = buf[:, j * 512:(j + 1) * 512]
                ev_lv[(qk, j)] = dop("tensor_copy", dsl_, b0)
                p.wait("scalar", g.s_dve, ev_lv[(qk, j)])
                sq = p.op("scalar", "activation", sqb[:], dsl_, AF.Square,
                          sem=g.s_act)
                p.wait("tensor", g.s_act, sq)
                if not (h == 0 and qk == 0 and j == 0):
                    p.wait("tensor", g.s_dve, last)
                mmh = p.op("tensor", "matmul", bank(6)[0:1, :],
                           g.onescol_bf[:, 0:1], sqb[:],
                           start=True, stop=True, sem=g.s_pe)
                p.wait("vector", g.s_pe, mmh)
                dop("reduce_max", mm8[0:1, 15:16], bank(6)[0:1, :], axis=AX.X)
                dop("tensor_tensor", mm8[0:1, qk:qk + 1],
                    mm8[0:1, qk:qk + 1], mm8[0:1, 15:16], ALU.max)
        vp = {}
        for tt in range(NTT):
            bv_ = bank(2 + tt % 2)
            if tt >= 2:
                p.wait("tensor", g.s_dve, vp[tt - 2])
            mmv = None
            for dc in range(NCK):
                mmv = p.op("tensor", "matmul", bv_[:, 0:128],
                           a_reg[:, dc * T + tt * 128:dc * T + (tt + 1) * 128],
                           wv_[:, dc * 128:(dc + 1) * 128],
                           start=(dc == 0), stop=(dc == NCK - 1),
                           sem=(g.s_pe if dc == NCK - 1 else None))
            p.wait("vector", g.s_pe, mmv)
            vp[tt] = dop("tensor_copy", vh[:, tt * 128:(tt + 1) * 128],
                         bv_[:, 0:128])
        last_pj = g.s_pe.n
        p.wait("scalar", g.s_dve, last)
        alv = p.op("scalar", "activation", mm8[0:1, 8:10], mm8[0:1, 0:2],
                   AF.Ln, bias=g.epscol[0:1, 0:1], sem=g.s_act)
        p.wait("vector", g.s_act, alv)
        dop("tensor_tensor", mm8[0:1, 12:13], mm8[0:1, 8:9],
            mm8[0:1, 9:10], ALU.add)
        p.wait("scalar", g.s_dve, last)
        alv = p.op("scalar", "activation", mm8[0:1, 12:13], mm8[0:1, 12:13],
                   AF.Exp, scale=0.5, bias=g.negln128h[0:1, 0:1],
                   sem=g.s_act)
        p.wait("vector", g.s_act, alv)
        dop("tensor_scalar_mul", mm8[0:1, 14:15], mm8[0:1, 12:13], -1.0)
        dop("tensor_copy", g.a0_negbf[0:1, :], mm8[0:1, 14:15])
        p.wait("tensor", g.s_dve, last)
        bmm = p.op("tensor", "matmul", bank(6)[:, 1:2], g.ones_bf[0:1, 0:128],
                   g.a0_negbf[0:1, 0:1], start=True, stop=True, sem=g.s_pe)
        p.wait("vector", g.s_pe, bmm)
        dop("tensor_copy", negmball[:, h:h + 1], bank(6)[:, 1:2])
        p.wait("sync", g.s_dve, last)
        p.dma("sync", g.QS[h * 128:(h + 1) * 128, :], qh[:], g.s_st[h % 2])
        p.dma("sync", g.KS[h * 128:(h + 1) * 128, :], kh[:], g.s_st[h % 2])
        hspill[h % 2] = p.dma("sync", g.VS[h * 128:(h + 1) * 128, 0:4096],
                              vh[:], g.s_st[h % 2])
    spill_all = (g.s_st[0].n, g.s_st[1].n)

    exp_prev = 0
    av_done = 0
    ln_ev = {}
    yev = {}
    ld_lv = 0
    g.prev_sdtt = 0
    for qc in range(NT5):
        nkt = qc * 4 + 4
        for h in range(8):
            sem_ld = g.s_ld[h % 2]
            p.wait("sync", g.s_st[0], spill_all[0])
            p.wait("sync", g.s_st[1], spill_all[1])
            p.wait("sync", g.s_pe, av_done)
            p.dma("sync", ksl[:, 0:nkt * 128],
                  g.KS[h * 128:(h + 1) * 128, 0:nkt * 128], sem_ld)
            p.dma("sync", vsl[:, 0:nkt * 128],
                  g.VS[h * 128:(h + 1) * 128, 0:nkt * 128], sem_ld)
            ld_lv = p.dma("sync", qsl[:],
                          g.QS[h * 128:(h + 1) * 128,
                               qc * 512:(qc + 1) * 512], sem_ld)
            p.wait("tensor", sem_ld, ld_lv)
            if qc == 0 and h == 0:
                p.wait("tensor", g.s_gp, dec_lv)
            avb = bank(4)
            ell = bank(6)[0:1, :]
            sdtt = {}
            exp2 = {}
            av2 = {}
            npair = nkt // 2
            p.wait("tensor", g.s_dve, g.prev_sdtt)
            if h == 0 and qc > 0:
                p.wait("tensor", g.s_dve, yev[(qc - 1, NCK - 2)])
                p.wait("tensor", g.s_dve, yev[(qc - 1, NCK - 1)])

            def emit_av1(pi):
                p.wait("tensor", g.s_act, exp2[pi])
                pbsl = pbuf[:, (pi % 2) * 1024:(pi % 2) * 1024 + 1024]
                for u in range(2):
                    kt = 2 * pi + u
                    lastk = (kt == nkt - 1)
                    p.op("tensor", "matmul", avb,
                         vsl[:, kt * 128:(kt + 1) * 128],
                         pbsl[:, u * 512:(u + 1) * 512],
                         start=(kt == 0), stop=lastk)
                    av2[pi] = p.op("tensor", "matmul", ell,
                                   g.onescol_bf[:, 0:1],
                                   pbsl[:, u * 512:(u + 1) * 512],
                                   start=(kt == 0), stop=lastk,
                                   sem=(g.s_pe if u == 1 else None))

            for pi in range(npair):
                for u in range(2):
                    kt = 2 * pi + u
                    sb = bank(2 * (pi % 2) + u)
                    if pi >= 2 and u == 0:
                        p.wait("tensor", g.s_dve, sdtt[pi - 2])
                    smm = p.op("tensor", "matmul", sb,
                               ksl[:, kt * 128:(kt + 1) * 128], qsl[:],
                               start=True, stop=True,
                               sem=(g.s_pe if u == 1 else None))
                if pi >= 1:
                    emit_av1(pi - 1)
                p.wait("vector", g.s_pe, smm)
                if pi >= 2:
                    p.wait("vector", g.s_act, exp2[pi - 2])
                msk = None
                for u in range(2):
                    kt = 2 * pi + u
                    d = 4 * qc - kt + 3
                    sb = bank(2 * (pi % 2) + u)
                    sdh = sd[:, (pi % 2) * 1024 + u * 512:
                             (pi % 2) * 1024 + (u + 1) * 512]
                    dop("tensor_tensor", sdh, sb,
                        dec[:, d * 512:(d + 1) * 512], ALU.mult)
                    if kt >= 4 * qc:
                        di = kt - 4 * qc
                        p.wait("gpsimd", g.s_dve, last)
                        msk = p.op("gpsimd", "tensor_tensor", sdh, sdh,
                                   ngm[:, di * 512:(di + 1) * 512], ALU.add,
                                   sem=g.s_gp)
                sd_lv = last
                sdtt[pi] = sd_lv
                g.prev_sdtt = sd_lv
                p.wait("scalar", g.s_dve, sd_lv)
                if msk is not None:
                    p.wait("scalar", g.s_gp, msk)
                if pi >= 2:
                    p.wait("scalar", g.s_pe, av2[pi - 2])
                sdp = sd[:, (pi % 2) * 1024:(pi % 2) * 1024 + 1024]
                pbsl = pbuf[:, (pi % 2) * 1024:(pi % 2) * 1024 + 1024]
                exp2[pi] = p.op("scalar", "activation", pbsl, sdp, AF.Exp,
                                bias=negmball[:, h:h + 1],
                                scale=float(1.0 / math.sqrt(128.0)),
                                sem=g.s_act)
            emit_av1(npair - 1)
            av_done = av2[npair - 1]
            p.wait("vector", g.s_pe, av_done)
            dop("reciprocal", rlbf[0:1, :], ell, low_prec=True)
            p.wait("tensor", g.s_dve, last)
            bmm = p.op("tensor", "matmul", bank(7), g.ones_bf[0:1, 0:128],
                       rlbf[0:1, :], start=True, stop=True, sem=g.s_pe)
            p.wait("vector", g.s_pe, bmm)
            dop("tensor_copy", bcb[:], bank(7))
            onb_lv = dop("tensor_tensor", g.a1_onb[:], avb, bcb[:], ALU.mult)
            p.wait("scalar", g.s_dve, onb_lv)
            sq = p.op("scalar", "activation", sqb[:], g.a1_onb[:], AF.Square,
                      sem=g.s_act)
            p.wait("tensor", g.s_dve, onb_lv)
            p.wait("tensor", g.s_act, sq)
            s1m = p.op("tensor", "matmul", bank(6)[0:1, :],
                       g.onescol_bf[:, 0:1], g.a1_onb[:],
                       start=True, stop=True)
            s2m = p.op("tensor", "matmul", bank(5)[0:1, :],
                       g.onescol_bf[:, 0:1], sqb[:],
                       start=True, stop=True, sem=g.s_pe)
            p.wait("vector", g.s_pe, s2m)
            dop("tensor_scalar_mul", r1[0:1, :], bank(6)[0:1, :], 1.0 / 128)
            dop("tensor_scalar_mul", r2[0:1, :], bank(5)[0:1, :], 1.0 / 128)
            dop("tensor_tensor", g.a1_rl[0:1, :], r1[0:1, :], r1[0:1, :],
                ALU.mult)
            dop("tensor_tensor", r2[0:1, :], r2[0:1, :], g.a1_rl[0:1, :],
                ALU.subtract)
            p.wait("scalar", g.s_dve, last)
            alv = p.op("scalar", "activation", r2[0:1, :], r2[0:1, :],
                       AF.Ln, bias=g.epscol[0:1, 0:1], sem=g.s_act)
            p.wait("scalar", g.s_act, alv)
            alv = p.op("scalar", "activation", r2[0:1, :], r2[0:1, :],
                       AF.Exp, scale=-0.5, sem=g.s_act)
            p.wait("vector", g.s_act, alv)
            dop("tensor_copy", rrow[0:1, :], r2[0:1, :])
            dop("tensor_tensor", g.a1_rl[0:1, :], r1[0:1, :], r2[0:1, :],
                ALU.mult)
            dop("tensor_copy", mrow[0:1, :], g.a1_rl[0:1, :])
            mr_lv = last
            p.wait("tensor", g.s_dve, mr_lv)
            bm1 = p.op("tensor", "matmul", bank(7), g.ones_bf[0:1, 0:128],
                       rrow[0:1, :], start=True, stop=True, sem=g.s_pe)
            p.wait("vector", g.s_pe, bm1)
            dop("tensor_copy", bcb[:], bank(7))
            lnsl = lnb[:, h * 512:(h + 1) * 512]
            dop("tensor_tensor", lnsl, g.a1_onb[:], bcb[:], ALU.mult)
            p.wait("tensor", g.s_dve, last)
            bm2 = p.op("tensor", "matmul", bank(7), g.ones_bf[0:1, 0:128],
                       mrow[0:1, :], start=True, stop=True, sem=g.s_pe)
            p.wait("vector", g.s_pe, bm2)
            dop("tensor_copy", bcb[:], bank(7))
            ln_ev[(qc, h)] = dop("tensor_tensor", lnsl, lnsl, bcb[:],
                                 ALU.subtract)
        p.wait("tensor", g.s_dve, ln_ev[(qc, 7)])
        p.wait("tensor", g.s_dve, wo_lv)
        for oc in range(NCK):
            b = bank(2 + oc % 2)
            if oc >= 2:
                p.wait("tensor", g.s_dve, yev[(qc, oc - 2)])
            elif qc > 0:
                p.wait("tensor", g.s_dve, yev[(qc - 1, NCK - 2 + oc)])
            for h2 in range(8):
                p.op("tensor", "matmul", b,
                     wo[:, h2 * 1024 + oc * 128:h2 * 1024 + (oc + 1) * 128],
                     lnb[:, h2 * 512:(h2 + 1) * 512],
                     start=(h2 == 0), stop=False)
            mm = p.op("tensor", "matmul", b,
                      borow[0:1, oc * 128:(oc + 1) * 128],
                      g.ones_bf[0:1, 0:512], start=False, stop=True,
                      sem=g.s_pe)
            p.wait("vector", g.s_pe, mm)
            yev[(qc, oc)] = p.op(
                "vector", "tensor_copy",
                a_reg[:, oc * T + qc * 512:oc * T + (qc + 1) * 512], b,
                sem=g.s_dve)
    g.attn1_done = g.s_dve.n


def emit_ffn1(p, g, a_reg, y_reg):
    ps = g.ps
    bank = g.bank
    W = g.weights
    wt = W["L1_ffn_weight"]
    dis = W["L1_ffn_disha"]

    weff = g.f_weff
    wstg = g.f_wstg
    dish = g.f_dish

    p.wait("gpsimd", g.s_pe, g.s_pe.n)
    p.wait("gpsimd", g.s_dve, g.s_dve.n)
    p.wait("gpsimd", g.s_gp, g.s_gp.n)
    for _s in g.s_st + g.s_ld:
        p.wait("gpsimd", _s, _s.n)

    wl = 0
    for oc in range(NCK):
        wl = p.dma("gpsimd", wstg[:, oc * 1024:(oc + 1) * 1024],
                   bass.AP(wt, oc * 128 * 1024, [[1024, 128], [1, 1024]]),
                   g.s_wt[0])
    dl = p.dma("gpsimd", dish[0:64, :], bass.AP(dis, 0, [[1024, 64], [1, 1024]]),
               g.s_rw)
    dl = p.dma("gpsimd", dish[64:128, :], bass.AP(dis, 0, [[1024, 64], [1, 1024]]),
               g.s_rw)
    p.wait("vector", g.s_rw, dl)
    dsc = p.op("vector", "tensor_scalar_mul", dish[:], dish[:], 0.5,
               sem=g.s_dve)

    p.wait("tensor", g.s_wt[0], wl)
    p.wait("tensor", g.s_gp, g.const_lvl)
    ev = {}
    for dc in range(NCK):
        for oc in range(NCK):
            i = dc * NCK + oc
            tb = g.psbf[:, (7 if i % 2 == 0 else 5) * 1024:
                        (7 if i % 2 == 0 else 5) * 1024 + 128]
            if i >= 2:
                p.wait("tensor", g.s_dve, ev[i - 2])
            lv = p.op("tensor", "transpose", tb,
                      wstg[:, oc * 1024 + dc * 128:oc * 1024 + (dc + 1) * 128],
                      g.ident_bf[:], sem=g.s_pe)
            p.wait("vector", g.s_pe, lv)
            if i == 0:
                p.wait("vector", g.s_dve, dsc)
            ev[i] = p.op("vector", "scalar_tensor_tensor",
                         weff[:, dc * 1024 + oc * 128:dc * 1024 + (oc + 1) * 128],
                         tb, 0.5, dish[:, oc * 128:(oc + 1) * 128],
                         ALU.mult, ALU.add, sem=g.s_dve)
    weff_lv = ev[NCK * NCK - 2]
    weff_lv2 = ev[NCK * NCK - 1]

    a_ready = max(g.a_lv)
    p.wait("tensor", g.s_dve, max(weff_lv, weff_lv2))
    p.wait("tensor", g.s_dve, a_ready)
    yev = {}
    for tau in range(NT5):
        for oc in range(NCK):
            i = tau * NCK + oc
            b = bank(i % 4)
            if i >= 4:
                p.wait("tensor", g.s_dve, yev[i - 4])
            mm = None
            for dc in range(NCK):
                mm = p.op("tensor", "matmul", b,
                          weff[:, dc * 1024 + oc * 128:dc * 1024 + (oc + 1) * 128],
                          a_reg[:, dc * T + tau * 512:dc * T + (tau + 1) * 512],
                          start=(dc == 0), stop=(dc == NCK - 1),
                          sem=(g.s_pe if dc == NCK - 1 else None))
            p.wait("vector", g.s_pe, mm)
            yev[i] = p.op("vector", "tensor_copy",
                          y_reg[:, oc * T + tau * 512:oc * T + (tau + 1) * 512],
                          b, sem=g.s_dve)
    g.ffn_done = g.s_dve.n


SEQ_FULL = ["moe", "attn0", "moe", "ffn1", "attn1", "ffn1"]

WNAMES = [
    "L0_attn_Wq", "L0_attn_bq", "L0_attn_Wk", "L0_attn_bk", "L0_attn_Wv",
    "L0_attn_bv", "L0_attn_Wp", "L0_attn_bp", "L0_ffn_router_w", "L0_ffn_w1",
    "L0_ffn_w3", "L0_ffn_w2", "L1_attn_Wq", "L1_attn_Wk", "L1_attn_Wv",
    "L1_attn_Wo", "L1_attn_bo", "L1_attn_theta", "L1_attn_gamma",
    "L1_attn_beta", "L1_ffn_weight", "L1_ffn_disha",
]

WSHAPES = {
    "L0_attn_Wq": [D, D], "L0_attn_bq": [D], "L0_attn_Wk": [D, D],
    "L0_attn_bk": [D], "L0_attn_Wv": [D, D], "L0_attn_bv": [D],
    "L0_attn_Wp": [D, D], "L0_attn_bp": [D], "L0_ffn_router_w": [D, 8],
    "L0_ffn_w1": [8, D, 2048], "L0_ffn_w3": [8, D, 2048],
    "L0_ffn_w2": [8, 2048, D], "L1_attn_Wq": [D, D], "L1_attn_Wk": [D, D],
    "L1_attn_Wv": [D, D], "L1_attn_Wo": [D, D], "L1_attn_bo": [D],
    "L1_attn_theta": [8], "L1_attn_gamma": [D], "L1_attn_beta": [D],
    "L1_ffn_weight": [D, D], "L1_ffn_disha": [64, D],
}


def build_nc(stages, dbg=False):
    from contextlib import ExitStack
    nc = bass.Bass("TRN2", target_bir_lowering=False,
                   dynamic_dma_scratch_size=16384)
    g = St()
    g.weights = {}
    for n in WNAMES:
        g.weights[n] = nc.dram_tensor(n, WSHAPES[n], F32, kind="ExternalInput")
    g.x_in = nc.dram_tensor("x_in", [T, D], F32, kind="ExternalInput")
    g.x_out = nc.dram_tensor("x_out", [T, D], F32, kind="ExternalOutput")
    dkind = "ExternalOutput" if dbg else "Internal"
    g.Rbuf = nc.dram_tensor("Rbuf", [D, T], F32, kind=dkind)
    g.Ybuf = nc.dram_tensor("Ybuf", [D, T], F32, kind=dkind)
    g.QS = nc.dram_tensor("QS", [D, T], BF16, kind=dkind)
    g.KS = nc.dram_tensor("KS", [D, T], BF16, kind=dkind)
    g.VS = nc.dram_tensor("VS", [1024, 4160], BF16, kind=dkind)

    with ExitStack() as ctx:
        E = ctx.enter_context
        ra = Arena(nc, ctx, "regA", 65536)
        rb_ = Arena(nc, ctx, "regB", 65536)
        rw = Arena(nc, ctx, "regW", 72 * KB)
        g.REGA = ra.view([128, 32768], BF16, 0)
        g.REGB = rb_.view([128, 32768], BF16, 0)
        g.stag = rw.view([128, 2 * T], F32, 0)
        g.sqbuf = rw.view([128, T], BF16, 32 * KB)
        g.rstd_row = rw.view([1, T], BF16, 40 * KB)
        g.mr_row = rw.view([1, T], BF16, 48 * KB)
        g.rstd_b = rw.view([128, T], BF16, 56 * KB)
        g.mr_b = rw.view([128, T], BF16, 64 * KB)
        g.ystag = rw.view([128, T], F32, 40 * KB)
        g.ystag2 = rw.view([128, T], F32, 56 * KB)
        g.xstage = rw.view([128, 2 * T], F32, 32 * KB)
        g.sq_x = None
        g.sqbuf_x = rw.view([128, T], BF16, 64 * KB)
        g.rstd_row_x = rw.view([1, T], BF16, 32 * KB)
        g.mr_row_x = rw.view([1, T], BF16, 40 * KB)
        g.rstd_b_x = rw.view([128, T], BF16, 48 * KB)
        g.mr_b_x = rw.view([128, T], BF16, 56 * KB)
        g.m_w1 = ra.view([128, 16 * 1024], BF16, 0)
        g.m_w3 = ra.view([128, 16 * 1024], BF16, 32 * KB)
        g.m_w2 = rb_.view([128, 16 * 1024], BF16, 0)
        g.m_gbuf = rb_.view([128, 16 * 512], BF16, 32 * KB)
        g.m_asc = rb_.view([128, 8 * 512], BF16, 48 * KB)
        g.m_asl = rb_.view([128, 8 * 512], BF16, 56 * KB)
        KBf = 1024
        g.a0_q = rw.view([128, T], BF16, 0)
        g.a0_k = rw.view([128, T], BF16, 8 * KBf)
        g.a0_v = rw.view([128, 32 * 130], BF16, 16 * KBf)
        g.a0_cos = rw.view([128, T], BF16, 24 * KBf + 512)
        g.a0_sin = rw.view([128, T], BF16, 32 * KBf + 512)
        g.a0_wq = rw.view([128, 1024], BF16, 40 * KBf + 512)
        g.a0_wqr = rw.view([128, 1024], BF16, 42 * KBf + 512)
        g.a0_wk = rw.view([128, 1024], BF16, 44 * KBf + 512)
        g.a0_wkr = rw.view([128, 1024], BF16, 46 * KBf + 512)
        g.a0_wv = rw.view([128, 1024], BF16, 48 * KBf + 512)
        g.a0_p = rw.view([128, 2048], BF16, 50 * KBf + 512)
        g.a0_t1 = rw.view([128, 512], F32, 54 * KBf + 512)
        g.a0_rows = rw.view([1, 5120], BF16, 56 * KBf + 512)
        g.a0_t2 = rw.view([128, 512], F32, 56 * KBf + 512)
        g.a0_ntile = rw.view([128, 512], F32, 58 * KBf + 512)
        g.a0_i32b = rw.view([128, 512], I32, 60 * KBf + 512)
        g.a0_tang = rw.view([128, 512], F32, 62 * KBf + 512)
        g.a0_rlbf = rw.view([128, 512], BF16, 66 * KBf + 512)
        g.a0_bcb = rw.view([128, 512], F32, 67 * KBf + 512)
        g.a0_tmpB = rw.view([128, 512], BF16, 69 * KBf + 512)
        g.a0_sq = rw.view([128, 512], BF16, 70 * KBf + 512)
        g.a0_invf = rw.view([1, 160], F32, 64 * KBf + 512)
        g.onesf32 = rw.view([1, 128], F32, 71 * KBf + 512)
        g.a0_qs1 = rw.view([128, T], BF16, 24 * KBf + 512)
        g.a0_ks1 = rw.view([128, T], BF16, 32 * KBf + 512)
        g.a0_vs1 = rw.view([128, 32 * 130], BF16, 40 * KBf + 512)
        g.a0_wpf = rw.view([128, 8192], BF16, 0)
        g.a0_bprow = rw.view([1, 1024], BF16, 16 * KBf)
        g.a1_decA = ra.view([128, 16384], F32, 0)
        g.a1_decB = rb_.view([128, 16384], F32, 0)
        g.a1_nt = rw.view([128, 2048], F32, 0)
        g.a1_sd = rw.view([128, 2048], F32, 0)
        g.a1_pb = rw.view([128, 2048], BF16, 8 * KBf)
        g.a1_ngm = rw.view([128, 2048], BF16, 12 * KBf)
        g.a1_onb = rw.view([128, 512], BF16, 0)
        g.a1_eb = rw.view([128, 2048], F32, 8 * KBf)
        g.a1_ln = rw.view([128, 4096], BF16, 60 * KBf + 512)
        g.a1_wo = rw.view([128, 8192], BF16, 16 * KBf)
        g.a1_grow = rw.view([1, 2048], BF16, 32 * KBf)
        g.a1_rl = rw.view([128, 512], F32, 32 * KBf)
        g.a1_bcb = rw.view([128, 512], F32, 34 * KBf)
        g.a1_borow = rw.view([1, 1024], BF16, 36 * KBf)
        g.a1_wq = rw.view([128, 1024], BF16, 38 * KBf + 512)
        g.a1_r1 = rw.view([1, 512], F32, 38 * KBf + 512)
        g.a1_wk = rw.view([128, 1024], BF16, 40 * KBf + 512)
        g.a1_r2 = rw.view([1, 512], F32, 40 * KBf + 512)
        g.a1_wv = rw.view([128, 1024], BF16, 42 * KBf + 512)
        g.a1_mrow = rw.view([1, 512], BF16, 42 * KBf + 512)
        g.a1_rrow = rw.view([1, 512], BF16, 43 * KBf + 512)
        g.a1_qh = rw.view([128, 4096], BF16, 44 * KBf + 512)
        g.a1_mt = rw.view([128, 2048], BF16, 44 * KBf + 512)
        g.a1_kh = rw.view([128, 4096], BF16, 52 * KBf + 512)
        g.a1_vh = rw.view([128, 4096], BF16, 60 * KBf + 512)
        g.a1_sq = rw.view([128, 512], BF16, 68 * KBf + 512)
        g.a1_qsl = rw.view([128, 512], BF16, 69 * KBf + 512)
        g.a1_rlbf = rw.view([128, 512], BF16, 70 * KBf + 512)
        g.f_weff = rw.view([128, 8 * 1024], BF16, 0)
        g.f_wstg = rw.view([128, 8 * 1024], BF16, 16 * KB)
        g.f_dish = rw.view([128, 1024], BF16, 32 * KB)
        g.m_cbb = rw.view([128, 2 * 512], BF16, 0)
        g.m_ystg = rw.view([128, 2 * 512], F32, 2 * KB)
        g.m_crow = rw.view([1, T], BF16, 6 * KB)
        g.Abuf = nc.dram_tensor("Abuf", [D, T], BF16, kind=dkind)
        g.ones_bf = E(nc.sbuf_tensor("ones_bf", [128, 512], BF16))
        g.onescol_bf = E(nc.sbuf_tensor("onescol_bf", [128, 1], BF16))
        g.epscol = E(nc.sbuf_tensor("epscol", [128, 1], F32))
        g.ident_bf = E(nc.sbuf_tensor("ident_bf", [128, 128], BF16))
        g.ident_f32 = E(nc.sbuf_tensor("ident_f32", [128, 128], F32))
        g.statsb = E(nc.sbuf_tensor("statsb", [128, 64], F32))
        g.stat_m = E(nc.sbuf_tensor("stat_m", [128, 32], F32))
        g.stat_v = E(nc.sbuf_tensor("stat_v", [128, 32], F32))
        g.stat_t = E(nc.sbuf_tensor("stat_t", [128, 32], F32))
        g.stat_pk = E(nc.sbuf_tensor("stat_pk", [128, 64], BF16))
        g.lg_sb = E(nc.sbuf_tensor("lg_sb", [128, 256], F32))
        g.r_tmp = E(nc.sbuf_tensor("r_tmp", [128, 256], F32))
        g.r_tmp2 = E(nc.sbuf_tensor("r_tmp2", [128, 256], F32))
        g.r_m1 = E(nc.sbuf_tensor("r_m1", [128, 32], F32))
        g.r_m2 = E(nc.sbuf_tensor("r_m2", [128, 32], F32))
        g.r_el = E(nc.sbuf_tensor("r_el", [128, 32], F32))
        g.c_tm = E(nc.sbuf_tensor("c_tm", [128, 256], BF16))
        g.rw_sb = E(nc.sbuf_tensor("rw_sb", [128, 64], BF16))
        g.a0_mm = E(nc.sbuf_tensor("a0_mm", [1, 16], F32))
        g.a0_negbf = E(nc.sbuf_tensor("a0_negbf", [1, 1], BF16))
        g.a0_invfc = E(nc.sbuf_tensor("a0_invfc", [128, 1], F32))
        g.a0_negmb = E(nc.sbuf_tensor("a0_negmb", [128, 1], F32))
        g.negln8 = E(nc.sbuf_tensor("negln8", [1, 1], F32))
        g.neg1e30 = E(nc.sbuf_tensor("neg1e30", [128, 1], F32))
        nc.const_aps.register(F32, -1e30, g.neg1e30[:, 0:1])
        g.negpi = E(nc.sbuf_tensor("negpi", [128, 1], F32))
        g.negln128h = E(nc.sbuf_tensor("negln128h", [1, 1], F32))
        g.a1_gcol = E(nc.sbuf_tensor("a1_gcol", [128, 8], F32))
        g.a1_bcol = E(nc.sbuf_tensor("a1_bcol", [128, 8], BF16))
        g.a1_thc = E(nc.sbuf_tensor("a1_thc", [128, 8], F32))
        g.a1_th = E(nc.sbuf_tensor("a1_th", [1, 16], F32))
        g.a1_negm8 = E(nc.sbuf_tensor("a1_negm8", [128, 8], F32))
        g.ps = E(nc.psum_tensor("ps", [128, 4096], F32))[:]
        g.psbf = g.ps.bitcast(BF16)
        g.bank = lambda b: g.ps[:, 512 * b:512 * (b + 1)]
        sems = {}
        for nm in ("pe", "act", "dve", "gp"):
            sems[nm] = Sem(E(nc.semaphore(f"s_{nm}")))
        g.s_pe, g.s_act, g.s_dve, g.s_gp = (sems[n] for n in
                                            ("pe", "act", "dve", "gp"))
        g.s_ld = [Sem(E(nc.semaphore(f"s_ld{i}"))) for i in range(2)]
        g.s_st = [Sem(E(nc.semaphore(f"s_st{i}"))) for i in range(2)]
        g.s_wt = [Sem(E(nc.semaphore(f"s_wt{i}"))) for i in range(2)]
        g.s_ya = [Sem(E(nc.semaphore(f"s_ya{i}"))) for i in range(2)]
        g.s_rw = Sem(E(nc.semaphore("s_rw")))
        g.s_w2 = Sem(E(nc.semaphore("s_w2")))
        g.xtp_lv = [0] * NCK

        block = E(nc.Block())
        p = Prog(nc)
        emit_const_setup(p, g)

        other = lambda r: g.REGB if r is g.REGA else g.REGA
        cur_a = g.REGA
        moe_kw = dict()
        if isinstance(stages, dict):
            moe_kw = stages.get("moe_kw", {})
            seq = stages["seq"]
        else:
            seq = stages
        emit_bridge(p, g, yspec=None, a_reg=cur_a, rin_x=True,
                    router=(seq and seq[0] == "moe"),
                    spill_A=(seq and seq[0] == "moe"))
        cur_y = None
        for i, body in enumerate(seq):
            nxt = seq[i + 1] if i + 1 < len(seq) else None
            if body == "moe":
                emit_moe(p, g, 0, True, **moe_kw)
                cur_y = ("hbm", g.Ybuf)
            elif body == "ffn1":
                yreg = other(cur_a)
                emit_ffn1(p, g, cur_a, yreg)
                cur_y = ("sbuf", yreg)
            elif body == "attn0":
                emit_attn0(p, g, cur_a)
                cur_y = ("sbuf", cur_a)
            elif body == "attn1":
                emit_attn1(p, g, cur_a, other(cur_a))
                cur_y = ("sbuf", cur_a)
            else:
                raise NotImplementedError(body)
            if nxt is None:
                emit_bridge(p, g, yspec=cur_y, a_reg=None, out_mode=True)
            else:
                a_new = cur_y[1] if cur_y[0] == "sbuf" else cur_a
                emit_bridge(p, g, yspec=cur_y, a_reg=a_new,
                            router=(nxt == "moe"), spill_A=(nxt == "moe"))
                cur_a = a_new
        if not seq:
            emit_bridge(p, g, yspec=None, a_reg=None, out_mode=True)

        for eng in ENGINES:
            for s in (g.s_st + g.s_ya + g.s_ld + g.s_wt):
                p.wait(eng, s, s.n)
        p.replay(block)
    return nc, g


def kernel(**inputs):
    x = np.ascontiguousarray(np.asarray(inputs["L0_attn_x"], dtype=np.float32))
    nc, g = build_nc(SEQ_FULL)
    wmap = {n: np.ascontiguousarray(np.asarray(inputs[n], dtype=np.float32))
            for n in WNAMES}
    in_maps = [dict(wmap, x_in=x[c]) for c in range(8)]
    res = run_bass_kernel_spmd(nc, in_maps, core_ids=list(range(8)))
    return np.stack([r["x_out"] for r in res.results], axis=0)


if __name__ == "__main__":
    pass


# version 05
# speedup vs baseline: 1.0454x; 1.0454x over previous
import math
import numpy as np
import ml_dtypes
import concourse.bass as bass
import concourse.mybir as mybir
from concourse.bass_utils import run_bass_kernel_spmd

F32 = mybir.dt.float32
BF16 = mybir.dt.bfloat16
I32 = mybir.dt.int32
U8 = mybir.dt.uint8
AF = mybir.ActivationFunctionType
ALU = mybir.AluOpType
AX = mybir.AxisListType

T = 4096
D = 1024
NCK = 8
NTT = 32
NT5 = 8
KB = 1024

ENGINES = ("tensor", "vector", "scalar", "gpsimd", "sync")


class Sem:
    __slots__ = ("h", "n")

    def __init__(self, h):
        self.h = h
        self.n = 0

    def inc(self, k=1):
        self.n += k
        return self.n


class Prog:
    def __init__(self, nc):
        self.nc = nc
        self.q = {e: [] for e in ENGINES}

    def wait(self, eng, sem, lvl):
        if lvl is None or lvl <= 0:
            return
        self.q[eng].append(lambda E, s=sem.h, v=lvl: E.wait_ge(s, v))

    def op(self, eng, meth, *args, sem=None, inc=1, low_prec=False, **kw):
        lvl = sem.inc(inc) if sem is not None else None
        nc = self.nc

        def fn(E, meth=meth, args=args, kw=kw, sem=sem, inc=inc):
            try:
                if low_prec:
                    with nc.allow_low_precision(reason="sanctioned bf16"):
                        inst = getattr(E, meth)(*args, **kw)
                else:
                    inst = getattr(E, meth)(*args, **kw)
            except Exception as e:
                raise RuntimeError(
                    f"op {meth} failed: {e} args0={str(args[0])[:160]}") from e
            if sem is not None:
                inst.then_inc(sem.h, inc)
        self.q[eng].append(fn)
        return lvl

    def dma(self, eng, out, in_, sem, **kw):
        return self.op(eng, "dma_start", sem=sem, inc=16, out=out, in_=in_, **kw)

    def replay(self, block):
        for eng in ENGINES:
            q = self.q[eng]

            def section(E, q=q):
                for fn in q:
                    fn(E)
            getattr(block, eng)(section)


class Arena:
    def __init__(self, nc, ctx, name, nbytes):
        self.nc = nc
        self.backing = ctx.enter_context(nc.sbuf_tensor(name, [128, nbytes], U8))
        self.base = nc.lookup_mloc(self.backing).addr
        self.name = name
        self.n = 0

    def view(self, shape, dtype, offset_bytes):
        name = f"{self.name}_v{self.n}"
        self.n += 1
        return self.nc.alloc_sbuf_tensor_at(
            name, list(shape), dtype, offset=self.base + offset_bytes)


class St:
    pass


def emit_const_setup(p, g):
    p.op("gpsimd", "memset", g.ones_bf[:], 1.0)
    p.op("gpsimd", "memset", g.onescol_bf[:], 1.0)
    p.op("gpsimd", "memset", g.epscol[:], 1e-5)
    p.op("gpsimd", "memset", g.negln8[:], -math.log(8.0))
    p.op("gpsimd", "memset", g.negpi[:], -math.pi)
    p.op("gpsimd", "memset", g.neg1e30[:], -1e30)
    p.op("gpsimd", "memset", g.negln128h[:], -0.5 * math.log(128.0))
    p.op("gpsimd", "memset", g.onesf32[:], 1.0)
    lv = p.op("gpsimd", "memset", g.ident_bf[:], 0.0, sem=g.s_gp)
    p.wait("gpsimd", g.s_gp, lv)
    p.op("gpsimd", "affine_select", g.ident_bf[:], g.ident_bf[:],
         pattern=[[-1, 128]], compare_op=ALU.not_equal, fill=1.0,
         base=0, channel_multiplier=1)
    lv = p.op("gpsimd", "memset", g.ident_f32[:], 0.0, sem=g.s_gp)
    p.wait("gpsimd", g.s_gp, lv)
    p.op("gpsimd", "affine_select", g.ident_f32[:], g.ident_f32[:],
         pattern=[[-1, 128]], compare_op=ALU.not_equal, fill=1.0,
         base=0, channel_multiplier=1, sem=g.s_gp)
    g.const_lvl = g.s_gp.n
    rwa = bass.AP(g.weights["L0_ffn_router_w"], 0,
                  [[8, 128], [128 * 8, 8], [1, 8]])
    g.rw_lvl = p.dma("gpsimd", g.rw_sb[:], rwa, g.s_rw)


def emit_bridge(p, g, *, yspec=None, a_reg=None, rin_x=False, out_mode=False,
                router=False, spill_A=False):
    if rin_x:
        sqbuf, rstd_row, mr_row, rstd_b, mr_b = (g.sqbuf_x, g.rstd_row_x,
                                                 g.mr_row_x, g.rstd_b_x,
                                                 g.mr_b_x)
    else:
        sqbuf, rstd_row, mr_row, rstd_b, mr_b = (g.sqbuf, g.rstd_row,
                                                 g.mr_row, g.rstd_b, g.mr_b)
    """Stream R: R += Y -> Rbuf; A = LN(R) -> a_reg; optional router logits.

    yspec: None | ("sbuf", bf16 [128x32768] handle) | ("hbm", f32 dram handle)
    """
    stag = g.stag
    ps = g.ps
    psbf = g.psbf
    bank = g.bank

    ld_seq = 0
    add_lv = [None] * NCK
    rb_lv = [None] * NCK
    ev_lv = {}
    st_lv = [None] * NCK
    rst_lv = [None] * NCK
    g.a_spill_lv = [0] * NCK
    stat_mm = [None] * NCK
    stat_ev = [None] * NCK
    stats_bank = bank(6)[:, 0:64]

    dve_entry = g.s_dve.n
    act_entry = g.s_act.n
    p.wait("sync", g.s_dve, dve_entry)
    p.wait("sync", g.s_gp, g.s_gp.n)
    for _s in g.s_st + g.s_ya:
        p.wait("sync", _s, _s.n)
    z_lv = p.op("vector", "memset", g.statsb[:], 0.0, sem=g.s_dve)

    for c in range(NCK):
        sl = stag[:, (c % 2) * T:(c % 2) * T + T]
        sem_ld = g.s_ld[c % 2]
        if out_mode:
            sl = stag[:, 0:T]
            if c >= 1:
                p.wait("sync", g.s_dve, ev_lv[(c - 1, NTT - 1)])
        elif c >= 2:
            p.wait("sync", g.s_dve, rb_lv[c - 2])
            p.wait("sync", g.s_st[c % 2], rst_lv[c - 2])
        if rin_x:
            xa = bass.AP(g.x_in, c * 128,
                         [[1024, 128], [128 * 1024, 32], [1, 128]])
            xsl = g.xstage[:, (c % 2) * T:(c % 2) * T + T]
            if c >= 2:
                p.wait("sync", g.s_pe, g.xtp_lv[c - 2])
            lv = p.dma("sync", xsl, xa, sem_ld)
            p.wait("tensor", sem_ld, lv)
            if c == 0:
                p.wait("tensor", g.s_gp, g.const_lvl)
            if c >= 2:
                p.wait("vector", g.s_st[c % 2], rst_lv[c - 2])
            for tt in range(NTT):
                tb = bank(7 if tt % 2 == 0 else 5)[:, 0:128]
                prev = (c, tt - 2) if tt >= 2 else \
                    ((c - 1, NTT - 2 + tt) if c >= 1 else None)
                if prev is not None:
                    p.wait("tensor", g.s_dve, ev_lv[prev])
                xlv = p.op("tensor", "transpose", tb,
                           xsl[:, tt * 128:(tt + 1) * 128], g.ident_f32[:],
                           sem=g.s_pe)
                p.wait("vector", g.s_pe, xlv)
                ev_lv[(c, tt)] = p.op(
                    "vector", "tensor_copy", sl[:, tt * 128:(tt + 1) * 128],
                    tb, sem=g.s_dve)
            g.xtp_lv[c] = xlv
        else:
            lv = p.dma("sync", sl, g.Rbuf[c * 128:(c + 1) * 128, :], sem_ld)
            if yspec is None or yspec[0] == "sbuf":
                p.wait("vector", sem_ld, lv)
            if out_mode and yspec is None:
                p.wait("tensor", sem_ld, lv)

        if yspec is not None:
            kind, yh = yspec
            if kind == "sbuf":
                ysl = yh[:, c * T:(c + 1) * T]
                add_lv[c] = p.op("vector", "tensor_tensor", sl, sl, ysl,
                                 ALU.add, sem=g.s_dve)
            elif out_mode:
                ysl = stag[:, T:2 * T]
                ylv = p.dma("sync", ysl,
                            bass.AP(yh, c * 128 * T, [[T, 128], [1, T]]),
                            sem_ld)
                p.wait("vector", sem_ld, ylv)
                add_lv[c] = p.op("vector", "tensor_tensor", sl, sl, ysl,
                                 ALU.add, sem=g.s_dve)
            else:
                ysl = g.ystag[:, (c % 2) * 2048:(c % 2) * 2048 + 2048]
                ylv = p.dma("sync", ysl,
                            bass.AP(yh, c * 128 * T, [[T, 128], [1, 2048]]),
                            sem_ld)
                ylv = p.dma("sync", g.ystag2[:, (c % 2) * 2048:(c % 2) * 2048 + 2048],
                            bass.AP(yh, c * 128 * T + 2048, [[T, 128], [1, 2048]]),
                            sem_ld)
                p.wait("vector", sem_ld, ylv)
                p.op("vector", "tensor_tensor", sl[:, 0:2048], sl[:, 0:2048],
                     ysl, ALU.add)
                add_lv[c] = p.op("vector", "tensor_tensor", sl[:, 2048:4096],
                                 sl[:, 2048:4096],
                                 g.ystag2[:, (c % 2) * 2048:(c % 2) * 2048 + 2048],
                                 ALU.add, sem=g.s_dve)

        if out_mode:
            if c >= 2:
                p.wait("vector", g.s_st[c % 2], st_lv[c - 2])
            for tt in range(NTT):
                tb = bank(7 if tt % 2 == 0 else 5)[:, 0:128]
                p.wait("tensor", g.s_dve, add_lv[c])
                if add_lv[c] is None and tt == 0 and c == 0:
                    p.wait("tensor", g.s_dve, dve_entry)
                    p.wait("tensor", g.s_act, act_entry)
                prev = (c, tt - 2) if tt >= 2 else \
                    ((c - 1, NTT - 2 + tt) if c >= 1 else None)
                if prev is not None:
                    p.wait("tensor", g.s_dve, ev_lv[prev])
                xlv = p.op("tensor", "transpose", tb,
                           sl[:, tt * 128:(tt + 1) * 128], g.ident_f32[:],
                           sem=g.s_pe)
                p.wait("vector", g.s_pe, xlv)
                ev_lv[(c, tt)] = p.op(
                    "vector", "tensor_copy",
                    g.xstage[:, (c % 2) * T + tt * 128:(c % 2) * T + (tt + 1) * 128],
                    tb, sem=g.s_dve)
            p.wait("sync", g.s_dve, ev_lv[(c, NTT - 1)])
            oa = bass.AP(g.x_out, c * 128,
                         [[1024, 128], [128 * 1024, 32], [1, 128]])
            st_lv[c] = p.dma("sync", oa,
                             g.xstage[:, (c % 2) * T:(c % 2) * T + T],
                             g.s_st[c % 2])
            continue

        if yspec is not None or rin_x:
            p.wait("sync", g.s_dve, add_lv[c] if yspec else ev_lv[(c, NTT - 1)])
            rst_lv[c] = p.dma("sync", g.Rbuf[c * 128:(c + 1) * 128, :], sl,
                              g.s_st[c % 2])

        rb = a_reg[:, c * T:(c + 1) * T]
        if yspec is not None:
            p.wait("vector", g.s_dve, add_lv[c])
        elif rin_x:
            p.wait("vector", g.s_dve, ev_lv[(c, NTT - 1)])
        rb_lv[c] = p.op("vector", "tensor_copy", rb, sl, sem=g.s_dve)
        p.wait("scalar", g.s_dve, rb_lv[c])
        if c >= 1:
            p.wait("scalar", g.s_pe, stat_mm[c - 1])
        sq_lv = p.op("scalar", "activation", sqbuf[:], rb, AF.Square,
                     sem=g.s_act)
        p.wait("tensor", g.s_dve, rb_lv[c])
        p.wait("tensor", g.s_act, sq_lv)
        if c == 0:
            p.wait("tensor", g.s_gp, g.const_lvl)
        else:
            p.wait("tensor", g.s_dve, stat_ev[c - 1])
        for tt in range(NTT):
            p.op("tensor", "matmul", stats_bank[:, 2 * tt:2 * tt + 1],
                 rb[:, tt * 128:(tt + 1) * 128], g.onescol_bf[:, 0:1],
                 start=True, stop=True)
            stat_mm[c] = p.op(
                "tensor", "matmul", stats_bank[:, 2 * tt + 1:2 * tt + 2],
                sqbuf[:, tt * 128:(tt + 1) * 128], g.onescol_bf[:, 0:1],
                start=True, stop=True,
                sem=(g.s_pe if tt == NTT - 1 else None))
        p.wait("vector", g.s_pe, stat_mm[c])
        p.wait("vector", g.s_dve, z_lv if c == 0 else stat_ev[c - 1])
        stat_ev[c] = p.op("vector", "tensor_tensor", g.statsb[:], g.statsb[:],
                          stats_bank, ALU.add, sem=g.s_dve)

    if out_mode:
        for c in range(NCK):
            p.wait("sync", g.s_st[c % 2], st_lv[c])
        return

    sb = g.statsb
    p.wait("vector", g.s_dve, stat_ev[NCK - 1])
    lv = p.op("vector", "tensor_scalar_mul", g.stat_m[:], sb[:, 0:64:2],
              1.0 / D, sem=g.s_dve)
    p.op("vector", "tensor_scalar_mul", g.stat_v[:], sb[:, 1:64:2], 1.0 / D)
    p.wait("vector", g.s_dve, lv)
    lv = p.op("vector", "tensor_tensor", g.stat_t[:], g.stat_m[:], g.stat_m[:],
              ALU.mult, sem=g.s_dve)
    p.wait("vector", g.s_dve, lv)
    lv = p.op("vector", "tensor_tensor", g.stat_v[:], g.stat_v[:], g.stat_t[:],
              ALU.subtract, sem=g.s_dve)
    p.wait("scalar", g.s_dve, lv)
    lv = p.op("scalar", "activation", g.stat_v[:], g.stat_v[:], AF.Ln,
              bias=g.epscol[:, 0:1], sem=g.s_act)
    p.wait("scalar", g.s_act, lv)
    lv = p.op("scalar", "activation", g.stat_v[:], g.stat_v[:], AF.Exp,
              scale=-0.5, sem=g.s_act)
    p.wait("vector", g.s_act, lv)
    lv = p.op("vector", "tensor_tensor", g.stat_t[:], g.stat_m[:], g.stat_v[:],
              ALU.mult, sem=g.s_dve)
    p.op("vector", "tensor_copy", g.stat_pk[:, 0:32], g.stat_v[:])
    p.wait("vector", g.s_dve, lv)
    lv = p.op("vector", "tensor_copy", g.stat_pk[:, 32:64], g.stat_t[:],
              sem=g.s_dve)
    p.wait("tensor", g.s_dve, lv)
    p.wait("tensor", g.s_dve, stat_ev[NCK - 1])
    for tt in range(NTT):
        p.op("tensor", "transpose", psbf[0:1, tt * 128:(tt + 1) * 128],
             g.stat_pk[:, tt:tt + 1], g.ident_bf[:])
    for tt in range(NTT):
        p.op("tensor", "transpose",
             psbf[0:1, 4096 + tt * 128:4096 + (tt + 1) * 128],
             g.stat_pk[:, 32 + tt:32 + tt + 1], g.ident_bf[:],
             sem=(g.s_pe if tt == NTT - 1 else None))
    p.wait("vector", g.s_pe, g.s_pe.n)
    p.op("vector", "tensor_copy", rstd_row[:], psbf[0:1, 0:4096])
    lv = p.op("vector", "tensor_copy", mr_row[:], psbf[0:1, 4096:8192],
              sem=g.s_dve)
    p.wait("tensor", g.s_dve, lv)
    bc_lv = {}
    for j in range(2 * NT5):
        row = rstd_row if j < NT5 else mr_row
        dstt = rstd_b if j < NT5 else mr_b
        jj = j % NT5
        b = bank(6 + (j % 2))
        if j >= 2:
            p.wait("tensor", g.s_dve, bc_lv[j - 2])
        p.op("tensor", "matmul", b, g.ones_bf[0:1, 0:128],
             row[:, jj * 512:(jj + 1) * 512], start=True, stop=True, sem=g.s_pe)
        p.wait("vector", g.s_pe, g.s_pe.n)
        bc_lv[j] = p.op("vector", "tensor_copy",
                        dstt[:, jj * 512:(jj + 1) * 512], b, sem=g.s_dve)

    if router:
        lgb = bank(4)[:, 0:256]
        zl = p.op("vector", "memset", g.lg_sb[:], 0.0, sem=g.s_dve)
        lg_mm = None
        lg_ev = None
    ap_lv = [None] * NCK
    for c in range(NCK):
        rb = a_reg[:, c * T:(c + 1) * T]
        if c == 0:
            p.wait("vector", g.s_dve, bc_lv[2 * NT5 - 1])
        lv = p.op("vector", "tensor_tensor", rb, rb, rstd_b[:], ALU.mult,
                  sem=g.s_dve)
        p.wait("vector", g.s_dve, lv)
        ap_lv[c] = p.op("vector", "tensor_tensor", rb, rb, mr_b[:],
                        ALU.subtract, sem=g.s_dve)
        if spill_A:
            p.wait("sync", g.s_dve, ap_lv[c])
            g.a_spill_lv[c] = p.dma("sync", g.Abuf[c * 128:(c + 1) * 128, :],
                                    rb, g.s_st[c % 2])
        if router:
            p.wait("tensor", g.s_dve, ap_lv[c])
            if c == 0:
                p.wait("tensor", g.s_rw, g.rw_lvl)
            else:
                p.wait("tensor", g.s_dve, lg_ev)
            for tt in range(NTT):
                lg_mm = p.op("tensor", "matmul", lgb[:, tt * 8:(tt + 1) * 8],
                             rb[:, tt * 128:(tt + 1) * 128],
                             g.rw_sb[:, c * 8:(c + 1) * 8],
                             start=True, stop=True,
                             sem=(g.s_pe if tt == NTT - 1 else None))
            p.wait("vector", g.s_pe, lg_mm)
            p.wait("vector", g.s_dve, zl if c == 0 else lg_ev)
            lg_ev = p.op("vector", "tensor_tensor", g.lg_sb[:], g.lg_sb[:],
                         lgb, ALU.add, sem=g.s_dve)
    g.a_lv = list(ap_lv)
    g.a_spilled = spill_A
    if router:
        emit_router_combine(p, g)


def emit_router_combine(p, g):
    L = g.lg_sb
    last = g.s_dve.n

    def dop(meth, *a, **k):
        nonlocal last
        p.wait("vector", g.s_dve, last)
        last = p.op("vector", meth, *a, sem=g.s_dve, **k)
        return last

    for tt in range(NTT):
        l8 = L[:, tt * 8:(tt + 1) * 8]
        t8 = g.r_tmp[:, tt * 8:(tt + 1) * 8]
        w8 = g.r_tmp2[:, tt * 8:(tt + 1) * 8]
        nm1 = g.r_m1[:, tt:tt + 1]
        nm2 = g.r_m2[:, tt:tt + 1]
        el = g.r_el[:, tt:tt + 1]
        dop("reduce_max", nm1, l8, axis=AX.X, negate=True)
        dop("tensor_scalar", t8, l8, nm1, None, ALU.add)
        dop("tensor_single_scalar", w8, t8, 0.0, ALU.is_ge)
        dop("scalar_tensor_tensor", w8, w8, -1e30, l8, ALU.mult, ALU.add)
        dop("reduce_max", nm2, w8, axis=AX.X, negate=True)
        lv = dop("tensor_scalar", w8, l8, nm2, None, ALU.add)
        p.wait("scalar", g.s_dve, lv)
        elv = p.op("scalar", "activation", t8, t8, AF.Exp, sem=g.s_act)
        dop("tensor_single_scalar", w8, w8, 0.0, ALU.is_ge)
        p.wait("vector", g.s_act, elv)
        dop("tensor_tensor", w8, w8, t8, ALU.mult)
        dop("reduce_sum", el, w8, axis=AX.X)
        dop("reciprocal", el, el)
        dop("tensor_scalar", g.c_tm[:, tt * 8:(tt + 1) * 8], w8, el, 0.5,
            ALU.mult, ALU.mult)
    g.ctm_lv = g.s_dve.n


def emit_moe(p, g, L, first_call, n_experts=8, n_tau=NT5, prefetch=True, no_accum=False):
    ps = g.ps
    psbf = g.psbf
    bank = g.bank
    W = g.weights
    w1 = W["L0_ffn_w1"]
    w3 = W["L0_ffn_w3"]
    w2 = W["L0_ffn_w2"]

    gbuf = g.m_gbuf
    asc = g.m_asc
    asl = g.m_asl
    w1sb = g.m_w1
    w3sb = g.m_w3
    w2sb = g.m_w2
    cbb = g.m_cbb
    ystg = g.m_ystg
    crow = g.m_crow

    assert g.a_spilled, "MoE requires A spilled to HBM"
    p.wait("vector", g.s_gp, g.s_gp.n)
    p.wait("gpsimd", g.s_gp, g.s_gp.n)
    for _s in g.s_st + g.s_ld:
        p.wait("vector", _s, _s.n)
        p.wait("gpsimd", _s, _s.n)

    def w1ap(e, fc):
        return bass.AP(w1, e * 1024 * 2048 + fc * 128,
                       [[2048, 128], [128 * 2048, 8], [1, 128]])

    def w3ap(e, fc):
        return bass.AP(w3, e * 1024 * 2048 + fc * 128,
                       [[2048, 128], [128 * 2048, 8], [1, 128]])

    def w2ap(e, fc):
        return bass.AP(w2, e * 2048 * 1024 + fc * 128 * 1024,
                       [[1024, 128], [1, 1024]])

    p.wait("gpsimd", g.s_pe, g.s_pe.n)
    p.wait("gpsimd", g.s_dve, g.s_dve.n)
    w13_lv = 0
    w2_lv = 0
    for fc in range(16):
        p.dma("gpsimd", w1sb[:, fc * 1024:(fc + 1) * 1024], w1ap(0, fc),
              g.s_wt[0])
        w13_lv = p.dma("gpsimd", w3sb[:, fc * 1024:(fc + 1) * 1024],
                       w3ap(0, fc), g.s_wt[0])
        w2_lv = p.dma("gpsimd", w2sb[:, fc * 1024:(fc + 1) * 1024],
                      w2ap(0, fc), g.s_w2)
    w13_done = [w13_lv] + [None] * 8
    w2_done = [w2_lv] + [None] * 8

    cb_lv = [0, 0]
    ystg_dma = [0, 0]
    yev = [0] * 8
    sil_lv = {}
    g1_lv = {}
    h3_last = 0
    asl_lv = 0
    asl_sem = g.s_ld[0]
    out_fc_lv = {}

    for e in range(n_experts):
        if not prefetch and e > 0:
            p.wait("gpsimd", g.s_pe, g.s_pe.n)
            p.wait("gpsimd", g.s_dve, g.s_dve.n)
            for fc in range(16):
                p.dma("gpsimd", w1sb[:, fc * 1024:(fc + 1) * 1024],
                      w1ap(e, fc), g.s_wt[e % 2])
                w13_done[e] = p.dma("gpsimd",
                                    w3sb[:, fc * 1024:(fc + 1) * 1024],
                                    w3ap(e, fc), g.s_wt[e % 2])
                w2_done[e] = p.dma("gpsimd", w2sb[:, fc * 1024:(fc + 1) * 1024],
                                   w2ap(e, fc), g.s_w2)
        p.wait("tensor", g.s_dve, g.ctm_lv)
        for dc in range(4):
            p.wait("tensor", g.s_dve, yev[dc])
        for tt in range(NTT):
            p.op("tensor", "transpose", psbf[0:1, tt * 128:(tt + 1) * 128],
                 g.c_tm[:, tt * 8 + e:tt * 8 + e + 1], g.ident_bf[:],
                 sem=(g.s_pe if tt == NTT - 1 else None))
        p.wait("vector", g.s_pe, g.s_pe.n)
        crow_lv = p.op("vector", "tensor_copy", crow[:], psbf[0:1, 0:4096],
                       sem=g.s_dve)

        for tau in range(n_tau):
            par = tau % 2
            p.wait("sync", g.s_pe, h3_last)
            aap = bass.AP(g.Abuf, tau * 512,
                          [[4096, 128], [128 * 4096, 8], [1, 512]])
            if e == 0 and tau == 0:
                p.wait("sync", g.s_st[0], g.a_spill_lv[NCK - 2])
                p.wait("sync", g.s_st[1], g.a_spill_lv[NCK - 1])
            asl_lv = p.dma("sync", asl[:], aap, asl_sem)
            p.wait("tensor", g.s_dve, crow_lv)
            p.wait("tensor", g.s_dve, yev[5])
            p.wait("tensor", g.s_dve, cb_lv[par])
            b5 = bank(5)
            lv = p.op("tensor", "matmul", b5, g.ones_bf[0:1, 0:128],
                      crow[:, tau * 512:(tau + 1) * 512], start=True,
                      stop=True, sem=g.s_pe)
            p.wait("vector", g.s_pe, lv)
            cbt = cbb[:, par * 512:(par + 1) * 512]
            cb_lv[par] = p.op("vector", "tensor_copy", cbt, b5, sem=g.s_dve)
            p.wait("vector", g.s_dve, cb_lv[par])
            p.wait("vector", asl_sem, asl_lv)
            p.wait("vector", g.s_pe, h3_last)
            asc_lv = None
            for dc in range(NCK):
                asc_lv = p.op("vector", "tensor_tensor",
                              asc[:, dc * 512:(dc + 1) * 512],
                              asl[:, dc * 512:(dc + 1) * 512], cbt,
                              ALU.mult, sem=g.s_dve)
            if tau == 0:
                p.wait("tensor", g.s_wt[e % 2], w13_done[e])
            for fc in range(16):
                fp = fc % 2
                bh1 = bank(fp)
                bh3 = bank(2 + fp)
                key = fc - 2 if fc >= 2 else 14 + fc
                if (e + tau > 0) or fc >= 2:
                    p.wait("tensor", g.s_act, sil_lv.get(key, 0))
                    p.wait("tensor", g.s_dve, g1_lv.get(key, 0))
                if fc < 2:
                    p.wait("tensor", g.s_dve, yev[fp])
                    p.wait("tensor", g.s_dve, yev[2 + fp])
                if e > 0 and tau == 0 and fc < 2:
                    p.wait("tensor", g.s_dve, crow_lv)
                if fc == 0:
                    p.wait("tensor", asl_sem, asl_lv)
                h1_lv = None
                for dc in range(NCK):
                    h1_lv = p.op("tensor", "matmul", bh1,
                                 w1sb[:, fc * 1024 + dc * 128:
                                      fc * 1024 + (dc + 1) * 128],
                                 asl[:, dc * 512:(dc + 1) * 512],
                                 start=(dc == 0), stop=(dc == NCK - 1),
                                 sem=(g.s_pe if dc == NCK - 1 else None))
                if fc == 0:
                    p.wait("tensor", g.s_dve, asc_lv)
                h3_lv = None
                for dc in range(NCK):
                    h3_lv = p.op("tensor", "matmul", bh3,
                                 w3sb[:, fc * 1024 + dc * 128:
                                      fc * 1024 + (dc + 1) * 128],
                                 asc[:, dc * 512:(dc + 1) * 512],
                                 start=(dc == 0), stop=(dc == NCK - 1),
                                 sem=(g.s_pe if dc == NCK - 1 else None))
                h3_last = h3_lv
                if tau == n_tau - 1:
                    h13_final = h3_lv
                    if prefetch and e < n_experts - 1:
                        p.wait("gpsimd", g.s_pe, h3_lv)
                        p.dma("gpsimd", w1sb[:, fc * 1024:(fc + 1) * 1024],
                              w1ap(e + 1, fc), g.s_wt[(e + 1) % 2])
                        lvw = p.dma("gpsimd",
                                    w3sb[:, fc * 1024:(fc + 1) * 1024],
                                    w3ap(e + 1, fc), g.s_wt[(e + 1) % 2])
                        if fc == 15:
                            w13_done[e + 1] = lvw
                p.wait("scalar", g.s_pe, h1_lv)
                gslice = gbuf[:, fc * 512:(fc + 1) * 512]
                sil_lv[fc] = p.op("scalar", "activation", gslice, bh1, AF.Silu,
                                  sem=g.s_act)
                p.wait("vector", g.s_act, sil_lv[fc])
                p.wait("vector", g.s_pe, h3_lv)
                g1_lv[fc] = p.op("vector", "tensor_tensor", gslice, gslice,
                                 bh3, ALU.mult, sem=g.s_dve)
            if tau == 0:
                p.wait("tensor", g.s_w2, w2_done[e])
            out_mm = [None] * 8
            last_tau = (tau == n_tau - 1)
            for fc in range(16):
                p.wait("tensor", g.s_dve, g1_lv[fc])
                for dc in (4, 5, 6, 7, 0, 1, 2, 3):
                    if fc == 0:
                        p.wait("tensor", g.s_dve, yev[dc])
                        if dc == 0:
                            p.wait("tensor", g.s_act, sil_lv[15])
                            p.wait("tensor", g.s_dve, g1_lv[15])
                    want_sem = (fc == 15) or (last_tau and dc == 3)
                    out_mm[dc] = p.op(
                        "tensor", "matmul", bank(dc),
                        w2sb[:, fc * 1024 + dc * 128:fc * 1024 + (dc + 1) * 128],
                        gbuf[:, fc * 512:(fc + 1) * 512],
                        start=(fc == 0), stop=(fc == 15),
                        sem=(g.s_pe if want_sem else None))
                if last_tau:
                    out_fc_lv[fc] = out_mm[3]
            if last_tau and prefetch and e < n_experts - 1:
                for fc in range(16):
                    p.wait("gpsimd", g.s_pe, out_fc_lv[fc])
                    lvw = p.dma("gpsimd", w2sb[:, fc * 1024:(fc + 1) * 1024],
                                w2ap(e + 1, fc), g.s_w2)
                    if fc == 15:
                        w2_done[e + 1] = lvw
            for ei, dc in enumerate((5, 0, 2, 1, 3, 4, 6, 7)):
                p.wait("vector", g.s_pe, out_mm[dc])
                sp = ei % 2
                p.wait("vector", g.s_ya[sp], ystg_dma[sp])
                yev[dc] = p.op("vector", "tensor_copy",
                               ystg[:, sp * 512:(sp + 1) * 512], bank(dc),
                               sem=g.s_dve)
                p.wait("gpsimd", g.s_dve, yev[dc])
                ya = bass.AP(g.Ybuf, dc * 128 * T + tau * 512,
                             [[T, 128], [1, 512]])
                kw = {} if (e == 0 or no_accum) else {"accum_op": ALU.add}
                ystg_dma[sp] = p.dma("gpsimd", ya,
                                     ystg[:, sp * 512:(sp + 1) * 512],
                                     g.s_ya[sp], **kw)
    g.moe_done = True


def emit_attn0(p, g, a_reg):
    ps = g.ps
    bank = g.bank
    W = g.weights
    oreg = g.REGB if a_reg is g.REGA else g.REGA

    qb, kb, vb = g.a0_q, g.a0_k, g.a0_v
    cosb, sinb = g.a0_cos, g.a0_sin
    wqs, wqr, wks, wkr, wvs = g.a0_wq, g.a0_wqr, g.a0_wk, g.a0_wkr, g.a0_wv
    pb = g.a0_p
    t1 = g.a0_t1
    rl = g.a0_t1
    t2 = g.a0_t2
    rlbf, bcb, tmpB = g.a0_rlbf, g.a0_bcb, g.a0_tmpB
    rows = g.a0_rows
    mm8 = g.a0_mm
    negmb = g.a0_negmb
    sqb = g.a0_sq
    invf = g.a0_invf

    last = g.s_dve.n

    def dop(meth, *a, **k):
        nonlocal last
        p.wait("vector", g.s_dve, last)
        last = p.op("vector", meth, *a, sem=g.s_dve, **k)
        return last

    for eng in ("gpsimd", "tensor"):
        p.wait(eng, g.s_pe, g.s_pe.n)
        p.wait(eng, g.s_dve, g.s_dve.n)
        p.wait(eng, g.s_gp, g.s_gp.n)
        for _s in g.s_st + g.s_ld:
            p.wait(eng, _s, _s.n)
    p.wait("vector", g.s_ld[0], g.s_ld[0].n)
    p.wait("vector", g.s_ld[1], g.s_ld[1].n)
    p.wait("vector", g.s_gp, g.s_gp.n)

    i32 = g.a0_i32b
    ntile = g.a0_ntile
    tang = g.a0_tang
    p.op("gpsimd", "iota", invf[0:1, 0:32], pattern=[[1, 32]], base=0,
         channel_multiplier=0, allow_small_or_imprecise_dtypes=True,
         sem=g.s_gp)
    iota_lv = g.s_gp.n
    p.op("gpsimd", "memset", vb[:], 1.0, sem=g.s_gp)
    vb1_lv = g.s_gp.n
    p.wait("vector", g.s_gp, iota_lv)
    lnf = math.log(10000.0) * 2.0 / 64.0
    dop("tensor_scalar_mul", invf[0:1, 0:32], invf[0:1, 0:32], -lnf)
    p.wait("scalar", g.s_dve, last)
    alv = p.op("scalar", "activation", invf[0:1, 0:32], invf[0:1, 0:32],
               AF.Exp, sem=g.s_act)
    p.wait("vector", g.s_act, alv)
    for off in (32, 33, 96, 97):
        dop("tensor_copy", invf[0:1, off:min(off + 64, 160):2],
            invf[0:1, 0:32])
    invf_lv = last
    p.wait("tensor", g.s_dve, invf_lv)
    tl = p.op("tensor", "transpose", g.ps[:, 0:1],
              invf[0:1, 32:160], g.ident_f32[0:1, 0:1], sem=g.s_pe)
    p.wait("vector", g.s_pe, tl)
    invfc_lv = dop("tensor_copy", g.a0_invfc[:], g.ps[:, 0:1])
    for j in range(NT5):
        p.wait("gpsimd", g.s_dve, last)
        p.op("gpsimd", "iota", ntile[:], pattern=[[1, 512]], base=j * 512,
             channel_multiplier=0, allow_small_or_imprecise_dtypes=True,
             sem=g.s_gp)
        p.wait("vector", g.s_gp, g.s_gp.n)
        dop("tensor_scalar_mul", tang[:], ntile[:], g.a0_invfc[:, 0:1])
        for cw in range(2):
            dst = (sinb if cw == 0 else cosb)[:, j * 512:(j + 1) * 512]
            dop("tensor_scalar", t1[:], tang[:], 1.0 / (2 * math.pi),
                0.25 * cw, ALU.mult, ALU.add)
            dop("tensor_copy", i32[:], t1[:])
            dop("tensor_copy", t2[:], i32[:])
            dop("tensor_tensor", t2[:], t2[:], t1[:], ALU.is_gt)
            dop("tensor_copy", t1[:], i32[:])
            dop("tensor_tensor", t1[:], t1[:], t2[:], ALU.subtract)
            if cw == 0:
                dop("scalar_tensor_tensor", t1[:], t1[:], -2 * math.pi,
                    tang[:], ALU.mult, ALU.add)
            else:
                dop("tensor_scalar", t2[:], tang[:], 1.0, math.pi / 2,
                    ALU.mult, ALU.add)
                dop("scalar_tensor_tensor", t1[:], t1[:], -2 * math.pi,
                    t2[:], ALU.mult, ALU.add)
            p.wait("scalar", g.s_dve, last)
            a1 = p.op("scalar", "activation", dst, t1[:], AF.Sin,
                      bias=g.negpi[:, 0:1], sem=g.s_act)
            p.wait("vector", g.s_act, a1)
            dop("tensor_scalar_mul", dst, dst, -1.0)
        cs_act = g.s_act.n

    p.wait("gpsimd", g.s_dve, last)
    bl = 0
    for i, nm in enumerate(["L0_attn_bq", "L0_attn_bk", "L0_attn_bv"]):
        off = (0, 2048, 4096)[i]
        bl = p.dma("gpsimd", rows[0:1, off:off + 1024],
                   bass.AP(W[nm], 0, [[1024, 1], [1, 1024]]), g.s_rw)
    p.wait("vector", g.s_rw, bl)
    for src_off, dst_off in ((0, 1024), (2048, 3072)):
        dop("tensor_scalar_mul", rows[0:1, dst_off:dst_off + 1024:2],
            rows[0:1, src_off + 1:src_off + 1024:2], -1.0)
        dop("tensor_copy", rows[0:1, dst_off + 1:dst_off + 1024:2],
            rows[0:1, src_off:src_off + 1024:2])
    rotb_lv = last

    a_ready = max(g.a_lv)
    negm8p = g.a1_negm8
    last_proj_mm = 0
    pspill = [0, 0]
    for pr in range(8):
        sw = g.s_wt[pr % 2]
        p.wait("gpsimd", g.s_pe, last_proj_mm)
        wl = 0
        for mat, dst in (("L0_attn_Wq", wqs), ("L0_attn_Wk", wks),
                         ("L0_attn_Wv", wvs)):
            wl = p.dma("gpsimd", dst[:],
                       bass.AP(W[mat], pr * 128,
                               [[1024, 128], [128 * 1024, 8], [1, 128]]), sw)
        p.wait("vector", sw, wl)
        p.wait("vector", g.s_pe, last_proj_mm)
        p.wait("vector", g.s_st[(pr + 1) % 2], pspill[(pr + 1) % 2])
        for src, dst in ((wqs, wqr), (wks, wkr)):
            dop("tensor_scalar_mul", dst[:, 0:1024:2], src[:, 1:1024:2], -1.0)
            dop("tensor_copy", dst[:, 1:1024:2], src[:, 0:1024:2])
        wrot_lv = last
        mm_mx = dop("memset", mm8[:], -3e38)

        p.wait("tensor", g.s_dve, wrot_lv)
        if pr == 0:
            p.wait("tensor", g.s_dve, a_ready)
            p.wait("tensor", g.s_act, cs_act)
            p.wait("tensor", g.s_dve, rotb_lv)
        rope_ev = {}
        if pr == 0:
            red_ev = {}
        for qk in range(2):
            wsl, wrl, dstbuf, boff = ((wqs, wqr, qb, 0) if qk == 0
                                      else (wks, wkr, kb, 2048))
            for j in range(NT5):
                b0 = bank(j % 2)
                b1 = bank(2 + j % 2)
                if (qk, j) not in ((0, 0), (0, 1)):
                    prev = (qk, j - 2) if j >= 2 else (qk - 1, NT5 - 2 + j)
                    p.wait("tensor", g.s_dve, rope_ev[prev])
                for dc in range(NCK):
                    p.op("tensor", "matmul", b0,
                         wsl[:, dc * 128:(dc + 1) * 128],
                         a_reg[:, dc * T + j * 512:dc * T + (j + 1) * 512],
                         start=(dc == 0), stop=False)
                p.op("tensor", "matmul", b0,
                     rows[0:1, boff + pr * 128:boff + (pr + 1) * 128],
                     g.ones_bf[0:1, 0:512], start=False, stop=True)
                for dc in range(NCK):
                    p.op("tensor", "matmul", b1,
                         wrl[:, dc * 128:(dc + 1) * 128],
                         a_reg[:, dc * T + j * 512:dc * T + (j + 1) * 512],
                         start=(dc == 0), stop=False)
                pj = p.op("tensor", "matmul", b1,
                          rows[0:1, boff + 1024 + pr * 128:
                               boff + 1024 + (pr + 1) * 128],
                          g.ones_bf[0:1, 0:512], start=False, stop=True,
                          sem=g.s_pe)
                p.wait("vector", g.s_pe, pj)
                dsl = dstbuf[:, j * 512:(j + 1) * 512]
                dop("tensor_tensor", t1[:], b0,
                    cosb[:, j * 512:(j + 1) * 512], ALU.mult)
                dop("tensor_tensor", dsl, b1,
                    sinb[:, j * 512:(j + 1) * 512], ALU.mult)
                rope_ev[(qk, j)] = dop("tensor_tensor", dsl, dsl, t1[:],
                                       ALU.add)
                p.wait("scalar", g.s_dve, rope_ev[(qk, j)])
                sq = p.op("scalar", "activation", sqb[:], dsl, AF.Square,
                          sem=g.s_act)
                for h in range(2):
                    si = qk * 16 + j * 2 + h
                    bsq = bank(6 + si % 2)
                    p.wait("tensor", g.s_act, sq)
                    if si >= 2:
                        p.wait("tensor", g.s_dve, red_ev[si - 2])
                    elif pr > 0:
                        p.wait("tensor", g.s_dve, red_ev[30 + si])
                    mmh = p.op("tensor", "matmul", bsq[0:1, :],
                               g.onescol_bf[h * 64:h * 64 + 64, 0:1],
                               sqb[h * 64:h * 64 + 64, :],
                               start=True, stop=True, sem=g.s_pe)
                    p.wait("vector", g.s_pe, mmh)
                    dop("reduce_max", mm8[0:1, 15:16], bsq[0:1, :],
                        axis=AX.X)
                    red_ev[si] = dop(
                        "tensor_tensor", mm8[0:1, qk * 2 + h:qk * 2 + h + 1],
                        mm8[0:1, qk * 2 + h:qk * 2 + h + 1],
                        mm8[0:1, 15:16], ALU.max)
        p.wait("tensor", g.s_gp, vb1_lv)
        vp_ev = {}
        for tt in range(NTT):
            bv_ = bank(4 + tt % 2)
            if tt >= 2:
                p.wait("tensor", g.s_dve, vp_ev[tt - 2])
            for dc in range(NCK):
                p.op("tensor", "matmul", bv_[:, 0:128],
                     a_reg[:, dc * T + tt * 128:dc * T + (tt + 1) * 128],
                     wvs[:, dc * 128:(dc + 1) * 128],
                     start=(dc == 0), stop=False)
            pv = p.op("tensor", "matmul", bv_[:, 0:128], g.ones_bf[0:1, 0:128],
                      rows[0:1, 4096 + pr * 128:4096 + (pr + 1) * 128],
                      start=False, stop=True, sem=g.s_pe)
            p.wait("vector", g.s_pe, pv)
            dop("tensor_copy", vb[:, tt * 130:tt * 130 + 64], bv_[:, 0:64])
            vp_ev[tt] = dop("tensor_copy",
                            vb[:, tt * 130 + 65:tt * 130 + 129],
                            bv_[:, 64:128])
        last_proj_mm = g.s_pe.n
        p.wait("scalar", g.s_dve, last)
        alv = p.op("scalar", "activation", mm8[0:1, 8:12], mm8[0:1, 0:4],
                   AF.Ln, bias=g.epscol[0:1, 0:1], sem=g.s_act)
        p.wait("vector", g.s_act, alv)
        dop("tensor_tensor", mm8[0:1, 12:14], mm8[0:1, 8:10],
            mm8[0:1, 10:12], ALU.add)
        p.wait("scalar", g.s_dve, last)
        alv = p.op("scalar", "activation", mm8[0:1, 12:14], mm8[0:1, 12:14],
                   AF.Exp, scale=0.5, bias=g.negln8[0:1, 0:1], sem=g.s_act)
        p.wait("vector", g.s_act, alv)
        dop("reduce_max", mm8[0:1, 14:15], mm8[0:1, 12:14], axis=AX.X,
            negate=True)
        dop("tensor_copy", g.a0_negbf[0:1, :], mm8[0:1, 14:15])
        p.wait("tensor", g.s_dve, last)
        bmm = p.op("tensor", "matmul", bank(6)[:, 0:1], g.ones_bf[0:1, 0:128],
                   g.a0_negbf[0:1, 0:1], start=True, stop=True, sem=g.s_pe)
        p.wait("vector", g.s_pe, bmm)
        dop("tensor_copy", negm8p[:, pr:pr + 1], bank(6)[:, 0:1])
        p.wait("sync", g.s_dve, last)
        p.dma("sync", g.QS[pr * 128:(pr + 1) * 128, :], qb[:], g.s_st[pr % 2])
        p.dma("sync", g.KS[pr * 128:(pr + 1) * 128, :], kb[:], g.s_st[pr % 2])
        pspill[pr % 2] = p.dma(
            "sync", g.VS[pr * 128:(pr + 1) * 128, :], vb[:], g.s_st[pr % 2])
    spill_done = (g.s_st[0].n, g.s_st[1].n)

    qsl2 = (qb, g.a0_qs1)
    ksl2 = (kb, g.a0_ks1)
    vsl2 = (vb, g.a0_vs1)
    last_exp = 0
    av_ev = {}
    bc_ev = {}
    prev_b_dma = 0
    exp_lv = {}
    av_lv = {}
    pair_ld = [0, 0]
    pair_att_end = [0, 0]
    for pr in range(8):
        par = pr % 2
        sem_ld = g.s_ld[par]
        qbuf, kbuf, vbuf = qsl2[par], ksl2[par], vsl2[par]
        p.wait("sync", g.s_st[0], spill_done[0])
        p.wait("sync", g.s_st[1], spill_done[1])
        p.wait("sync", g.s_pe, pair_att_end[par])
        p.dma("sync", qbuf[:], g.QS[pr * 128:(pr + 1) * 128, :], sem_ld)
        p.dma("sync", kbuf[:], g.KS[pr * 128:(pr + 1) * 128, :], sem_ld)
        pair_ld[par] = p.dma("sync", vbuf[:],
                             g.VS[pr * 128:(pr + 1) * 128, :], sem_ld)
        p.wait("tensor", sem_ld, pair_ld[par])
        opair = oreg[:, pr * 4096:(pr + 1) * 4096]
        for qc in range(NT5):
            avA = bank(4)
            avB = bank(5)

            def emit_av(kt, qcv):
                p.wait("tensor", g.s_act, exp_lv[kt])
                if kt == 0:
                    p.wait("tensor", g.s_dve, av_ev.get((pr, qcv - 1), 0))
                    if qcv == 0 and pr > 0:
                        p.wait("tensor", g.s_dve,
                               av_ev.get((pr - 1, NT5 - 1), 0))
                pbsl = pb[:, (kt % 2) * 1024:(kt % 2) * 1024 + 1024]
                p.op("tensor", "matmul", avA[0:65, :],
                     vbuf[:, kt * 130:kt * 130 + 65], pbsl[:, 0:512],
                     start=(kt == 0), stop=(kt == NTT - 1))
                av_lv[kt] = p.op("tensor", "matmul", avB[0:65, :],
                                 vbuf[:, kt * 130 + 65:kt * 130 + 130],
                                 pbsl[:, 512:1024],
                                 start=(kt == 0), stop=(kt == NTT - 1),
                                 sem=g.s_pe)

            for kt in range(NTT):
                sb0 = bank(2 * (kt % 2))
                sb1 = bank(2 * (kt % 2) + 1)
                if kt >= 2:
                    p.wait("tensor", g.s_act, exp_lv[kt - 2])
                elif qc > 0 or pr > 0:
                    p.wait("tensor", g.s_act,
                           exp_lv.get(NTT - 2 + kt, last_exp))
                p.op("tensor", "matmul", sb0,
                     kbuf[0:64, kt * 128:(kt + 1) * 128],
                     qbuf[0:64, qc * 512:(qc + 1) * 512], start=True,
                     stop=True, tile_position=(0, 0))
                smm = p.op("tensor", "matmul", sb1,
                           kbuf[64:128, kt * 128:(kt + 1) * 128],
                           qbuf[64:128, qc * 512:(qc + 1) * 512],
                           start=True, stop=True, tile_position=(64, 0),
                           sem=g.s_pe)
                if kt >= 1:
                    emit_av(kt - 1, qc)
                p.wait("scalar", g.s_pe, smm)
                if kt >= 2:
                    p.wait("scalar", g.s_pe, av_lv[kt - 2])
                elif qc > 0 or pr > 0:
                    p.wait("scalar", g.s_pe, av_lv.get(NTT - 2 + kt, 0))
                pbsl = pb[:, (kt % 2) * 1024:(kt % 2) * 1024 + 1024]
                psp = ps[:, 2 * (kt % 2) * 512:2 * (kt % 2) * 512 + 1024]
                exp_lv[kt] = p.op("scalar", "activation", pbsl, psp, AF.Exp,
                                  bias=negm8p[:, pr:pr + 1], scale=0.125,
                                  sem=g.s_act)
            emit_av(NTT - 1, qc)
            last_exp = exp_lv[NTT - 1]
            p.wait("vector", g.s_pe, av_lv[NTT - 1])
            for h, avb_ in ((0, avA), (1, avB)):
                dop("reciprocal", rlbf[64:65, 0:512], avb_[64:65, :],
                    low_prec=True)
                p.wait("tensor", g.s_dve, last)
                prevk = (pr, qc - 1, h) if qc > 0 else (pr - 1, NT5 - 1, h)
                p.wait("tensor", g.s_dve, bc_ev.get(prevk, 0))
                bmm = p.op("tensor", "matmul", bank(6 + h)[0:64, :],
                           g.ones_bf[64:65, 0:64], rlbf[64:65, 0:512],
                           start=True, stop=True, sem=g.s_pe)
                p.wait("vector", g.s_pe, bmm)
                dop("tensor_copy", bcb[0:64, :], bank(6 + h)[0:64, :])
                bc_ev[(pr, qc, h)] = last
                if h == 0:
                    dop("tensor_tensor", opair[0:64, qc * 512:(qc + 1) * 512],
                        avb_[0:64, :], bcb[0:64, :], ALU.mult)
                else:
                    p.wait("vector", g.s_st[1], prev_b_dma)
                    dop("tensor_tensor", tmpB[0:64, :], avb_[0:64, :],
                        bcb[0:64, :], ALU.mult)
                    p.wait("sync", g.s_dve, last)
                    prev_b_dma = p.dma(
                        "sync", opair[64:128, qc * 512:(qc + 1) * 512],
                        tmpB[0:64, :], g.s_st[1])
            av_ev[(pr, qc)] = last
        pair_att_end[par] = g.s_pe.n

    wpf = g.a0_wpf
    bprow = g.a0_bprow
    p.wait("gpsimd", g.s_pe, g.s_pe.n)
    p.wait("gpsimd", g.s_dve, g.s_dve.n)
    p.wait("gpsimd", g.s_act, g.s_act.n)
    wl = 0
    for dc in range(NCK):
        wl = p.dma("gpsimd", wpf[:, dc * 1024:(dc + 1) * 1024],
                   bass.AP(W["L0_attn_Wp"], dc * 128 * 1024,
                           [[1024, 128], [1, 1024]]), g.s_wt[0])
    blr = p.dma("gpsimd", bprow[0:1, :],
                bass.AP(W["L0_attn_bp"], 0, [[1024, 1], [1, 1024]]), g.s_rw)
    p.wait("tensor", g.s_wt[0], wl)
    p.wait("tensor", g.s_rw, blr)
    p.wait("tensor", g.s_st[1], prev_b_dma)
    p.wait("tensor", g.s_dve, g.s_dve.n)
    p.wait("tensor", g.s_act, g.s_act.n)
    yev = {}
    for tau in range(NT5):
        for oc in range(NCK):
            i = tau * NCK + oc
            b = bank(i % 4)
            if i >= 4:
                p.wait("tensor", g.s_dve, yev[i - 4])
            for pr2 in range(8):
                p.op("tensor", "matmul", b,
                     wpf[:, pr2 * 1024 + oc * 128:pr2 * 1024 + (oc + 1) * 128],
                     oreg[:, pr2 * 4096 + tau * 512:
                          pr2 * 4096 + (tau + 1) * 512],
                     start=(pr2 == 0), stop=False)
            mm = p.op("tensor", "matmul", b,
                      bprow[0:1, oc * 128:(oc + 1) * 128],
                      g.ones_bf[0:1, 0:512], start=False, stop=True,
                      sem=g.s_pe)
            p.wait("vector", g.s_pe, mm)
            yev[i] = p.op("vector", "tensor_copy",
                          a_reg[:, oc * T + tau * 512:oc * T + (tau + 1) * 512],
                          b, sem=g.s_dve)
    g.attn0_done = g.s_dve.n


def emit_attn1(p, g, a_reg, decay_reg):
    ps = g.ps
    bank = g.bank
    W = g.weights
    dec = g.a1_decA if decay_reg is g.REGA else g.a1_decB

    ntile, ebuf = g.a1_nt, g.a1_eb
    wo = g.a1_wo
    gcol, bcol, borow = g.a1_gcol, g.a1_bcol, g.a1_borow
    thr = g.a1_th
    wq_, wk_, wv_ = g.a1_wq, g.a1_wk, g.a1_wv
    qh, kh, vh = g.a1_qh, g.a1_kh, g.a1_vh
    ksl, vsl, qsl = g.a1_qh, g.a1_kh, g.a1_qsl
    sd, pbuf = g.a1_sd, g.a1_pb
    lnb = g.a1_ln
    sqb = g.a1_sq
    bcb, rlbf = g.a1_bcb, g.a1_rlbf
    r1, r2 = g.a1_r1, g.a1_r2
    mrow, rrow = g.a1_mrow, g.a1_rrow
    negmball = g.a1_negm8
    mm8 = g.a0_mm

    last = g.s_dve.n

    def dop(meth, *a, **k):
        nonlocal last
        p.wait("vector", g.s_dve, last)
        last = p.op("vector", meth, *a, sem=g.s_dve, **k)
        return last

    for eng in ("gpsimd", "tensor"):
        p.wait(eng, g.s_pe, g.s_pe.n)
        p.wait(eng, g.s_dve, g.s_dve.n)
        p.wait(eng, g.s_gp, g.s_gp.n)
        for _s in g.s_st + g.s_ld:
            p.wait(eng, _s, _s.n)
    p.wait("vector", g.s_gp, g.s_gp.n)

    p.op("gpsimd", "memset", g.onesf32[:], 1.0)
    tl = p.dma("gpsimd", thr[0:1, 0:8],
               bass.AP(W["L1_attn_theta"], 0, [[8, 1], [1, 8]]), g.s_rw)
    grow = g.a1_grow
    tl = p.dma("gpsimd", grow[0:1, 0:1024],
               bass.AP(W["L1_attn_gamma"], 0, [[1024, 1], [1, 1024]]), g.s_rw)
    tl = p.dma("gpsimd", grow[0:1, 1024:2048],
               bass.AP(W["L1_attn_beta"], 0, [[1024, 1], [1, 1024]]), g.s_rw)
    bl = p.dma("gpsimd", borow[0:1, :],
               bass.AP(W["L1_attn_bo"], 0, [[1024, 1], [1, 1024]]), g.s_rw)
    wl = 0
    for dc in range(NCK):
        wl = p.dma("gpsimd", wo[:, dc * 1024:(dc + 1) * 1024],
                   bass.AP(W["L1_attn_Wo"], dc * 128 * 1024,
                           [[1024, 128], [1, 1024]]), g.s_wt[0])
    p.wait("tensor", g.s_rw, bl)
    p.wait("tensor", g.s_gp, g.const_lvl)
    for i in range(16):
        p.op("tensor", "transpose", g.psbf[:, 2 * i:2 * i + 1],
             grow[0:1, i * 128:(i + 1) * 128], g.ident_bf[0:1, 0:1],
             sem=(g.s_pe if i == 15 else None))
    p.wait("vector", g.s_pe, g.s_pe.n)
    for dc in range(NCK):
        dop("tensor_copy", gcol[:, dc:dc + 1], g.psbf[:, 2 * dc:2 * dc + 1])
        dop("tensor_copy", bcol[:, dc:dc + 1],
            g.psbf[:, 16 + 2 * dc:17 + 2 * dc])
    gcol_lv = last
    p.wait("tensor", g.s_wt[0], wl)
    p.wait("tensor", g.s_dve, gcol_lv)
    b4 = ps[:, 4 * 512:4 * 512 + 1024]
    bo_mm = None
    for half in range(2):
        for dc in range(NCK):
            bo_mm = p.op("tensor", "matmul",
                         ps[:, (4 + half) * 512:(5 + half) * 512][0:1, :],
                         bcol[:, dc:dc + 1],
                         wo[:, dc * 1024 + half * 512:
                            dc * 1024 + half * 512 + 512],
                         start=(dc == 0), stop=(dc == NCK - 1),
                         sem=(g.s_pe if dc == NCK - 1 else None))
    p.wait("vector", g.s_pe, bo_mm)
    dop("tensor_tensor", borow[0:1, :], borow[0:1, :], b4[0:1, :], ALU.add)
    for dc in range(NCK):
        dop("tensor_scalar_mul", wo[:, dc * 1024:(dc + 1) * 1024],
            wo[:, dc * 1024:(dc + 1) * 1024], gcol[:, dc:dc + 1])
    wo_lv = last
    p.wait("vector", g.s_rw, g.s_rw.n)
    dop("tensor_scalar_mul", thr[0:1, 8:16], thr[0:1, 0:8], -1.0)
    th_lv = last
    p.wait("tensor", g.s_dve, th_lv)
    thc = g.a1_thc
    for h in range(8):
        p.op("tensor", "matmul", ps[:, 5 * 512 + h:5 * 512 + h + 1],
             g.onesf32[0:1, :], thr[0:1, 8 + h:9 + h], start=True, stop=True,
             sem=(g.s_pe if h == 7 else None))
    p.wait("vector", g.s_pe, g.s_pe.n)
    dop("tensor_copy", thc[:], ps[:, 5 * 512:5 * 512 + 8])
    thc_lv = last
    mtile = g.a1_mt
    for ch in range(8):
        p.wait("gpsimd", g.s_dve, last)
        p.op("gpsimd", "iota", ntile[:], pattern=[[128, 4], [1, 512]],
             base=-384 + ch * 4 * 128, channel_multiplier=-1,
             allow_small_or_imprecise_dtypes=True, sem=g.s_gp)
        p.wait("vector", g.s_gp, g.s_gp.n)
        dop("tensor_single_scalar", mtile[:], ntile[:], 0.0, ALU.is_ge)
        dop("tensor_single_scalar", ntile[:], ntile[:], 0.0, ALU.max)
        ntl = last
        dsl = dec[:, ch * 2048:(ch + 1) * 2048]
        for h in range(8):
            p.wait("scalar", g.s_dve, ntl if h == 0 else last)
            alv = p.op("scalar", "activation", (dsl if h == 0 else ebuf[:]),
                       ntile[:], AF.Exp, scale=thc[:, h:h + 1], sem=g.s_act)
            if h > 0:
                p.wait("vector", g.s_act, alv)
                dop("tensor_tensor", dsl, dsl, ebuf[:], ALU.add)
        dop("scalar_tensor_tensor", dsl, dsl, 0.125, mtile[:], ALU.mult,
            ALU.mult)
    ngm = g.a1_ngm
    p.wait("gpsimd", g.s_act, g.s_act.n)
    p.wait("gpsimd", g.s_dve, g.s_dve.n)
    nmz = p.op("gpsimd", "memset", ngm[:], 0.0, sem=g.s_gp)
    p.wait("gpsimd", g.s_gp, nmz)
    for di in range(4):
        p.op("gpsimd", "affine_select", ngm[:, di * 512:(di + 1) * 512],
             ngm[:, di * 512:(di + 1) * 512], pattern=[[1, 512]],
             compare_op=ALU.is_ge, fill=-1e30, base=-128 * di,
             channel_multiplier=-1, sem=(g.s_gp if di == 3 else None))
    dec_lv = g.s_gp.n
    p.wait("tensor", g.s_dve, last)

    a_ready = max(g.a_lv)
    hspill = [0, 0]
    last_pj = 0
    ev_lv = {}
    for h in range(8):
        sw = g.s_wt[h % 2]
        p.wait("gpsimd", g.s_pe, last_pj)
        wl = 0
        for mat, dst in (("L1_attn_Wq", wq_), ("L1_attn_Wk", wk_),
                         ("L1_attn_Wv", wv_)):
            wl = p.dma("gpsimd", dst[:],
                       bass.AP(W[mat], h * 128,
                               [[1024, 128], [128 * 1024, 8], [1, 128]]), sw)
        p.wait("tensor", sw, wl)
        if h == 0:
            p.wait("tensor", g.s_dve, a_ready)
        p.wait("vector", g.s_st[(h + 1) % 2], hspill[(h + 1) % 2])
        mm_mx = dop("memset", mm8[:], -3e38)
        for qk in range(2):
            wsl = wq_ if qk == 0 else wk_
            buf = qh if qk == 0 else kh
            for j in range(NT5):
                b0 = bank(j % 2)
                if (qk, j) not in ((0, 0), (0, 1)):
                    prev_lv = ev_lv[(qk, j - 2) if j >= 2
                                    else (qk - 1, NT5 - 2 + j)]
                    p.wait("tensor", g.s_dve, prev_lv)
                elif h > 0:
                    p.wait("tensor", g.s_dve, ev_lv[(1, NT5 - 2 + j)])
                mmq = None
                for dc in range(NCK):
                    mmq = p.op("tensor", "matmul", b0,
                               wsl[:, dc * 128:(dc + 1) * 128],
                               a_reg[:, dc * T + j * 512:
                                     dc * T + (j + 1) * 512],
                               start=(dc == 0), stop=(dc == NCK - 1),
                               sem=(g.s_pe if dc == NCK - 1 else None))
                p.wait("vector", g.s_pe, mmq)
                dsl_ = buf[:, j * 512:(j + 1) * 512]
                ev_lv[(qk, j)] = dop("tensor_copy", dsl_, b0)
                p.wait("scalar", g.s_dve, ev_lv[(qk, j)])
                sq = p.op("scalar", "activation", sqb[:], dsl_, AF.Square,
                          sem=g.s_act)
                p.wait("tensor", g.s_act, sq)
                if not (h == 0 and qk == 0 and j == 0):
                    p.wait("tensor", g.s_dve, last)
                mmh = p.op("tensor", "matmul", bank(6)[0:1, :],
                           g.onescol_bf[:, 0:1], sqb[:],
                           start=True, stop=True, sem=g.s_pe)
                p.wait("vector", g.s_pe, mmh)
                dop("reduce_max", mm8[0:1, 15:16], bank(6)[0:1, :], axis=AX.X)
                dop("tensor_tensor", mm8[0:1, qk:qk + 1],
                    mm8[0:1, qk:qk + 1], mm8[0:1, 15:16], ALU.max)
        vp = {}
        for tt in range(NTT):
            bv_ = bank(2 + tt % 2)
            if tt >= 2:
                p.wait("tensor", g.s_dve, vp[tt - 2])
            mmv = None
            for dc in range(NCK):
                mmv = p.op("tensor", "matmul", bv_[:, 0:128],
                           a_reg[:, dc * T + tt * 128:dc * T + (tt + 1) * 128],
                           wv_[:, dc * 128:(dc + 1) * 128],
                           start=(dc == 0), stop=(dc == NCK - 1),
                           sem=(g.s_pe if dc == NCK - 1 else None))
            p.wait("vector", g.s_pe, mmv)
            vp[tt] = dop("tensor_copy", vh[:, tt * 128:(tt + 1) * 128],
                         bv_[:, 0:128])
        last_pj = g.s_pe.n
        p.wait("scalar", g.s_dve, last)
        alv = p.op("scalar", "activation", mm8[0:1, 8:10], mm8[0:1, 0:2],
                   AF.Ln, bias=g.epscol[0:1, 0:1], sem=g.s_act)
        p.wait("vector", g.s_act, alv)
        dop("tensor_tensor", mm8[0:1, 12:13], mm8[0:1, 8:9],
            mm8[0:1, 9:10], ALU.add)
        p.wait("scalar", g.s_dve, last)
        alv = p.op("scalar", "activation", mm8[0:1, 12:13], mm8[0:1, 12:13],
                   AF.Exp, scale=0.5, bias=g.negln128h[0:1, 0:1],
                   sem=g.s_act)
        p.wait("vector", g.s_act, alv)
        dop("tensor_scalar_mul", mm8[0:1, 14:15], mm8[0:1, 12:13], -1.0)
        dop("tensor_copy", g.a0_negbf[0:1, :], mm8[0:1, 14:15])
        p.wait("tensor", g.s_dve, last)
        bmm = p.op("tensor", "matmul", bank(6)[:, 1:2], g.ones_bf[0:1, 0:128],
                   g.a0_negbf[0:1, 0:1], start=True, stop=True, sem=g.s_pe)
        p.wait("vector", g.s_pe, bmm)
        dop("tensor_copy", negmball[:, h:h + 1], bank(6)[:, 1:2])
        p.wait("sync", g.s_dve, last)
        p.dma("sync", g.QS[h * 128:(h + 1) * 128, :], qh[:], g.s_st[h % 2])
        p.dma("sync", g.KS[h * 128:(h + 1) * 128, :], kh[:], g.s_st[h % 2])
        hspill[h % 2] = p.dma("sync", g.VS[h * 128:(h + 1) * 128, 0:4096],
                              vh[:], g.s_st[h % 2])
    spill_all = (g.s_st[0].n, g.s_st[1].n)

    exp_prev = 0
    av_done = 0
    ln_ev = {}
    yev = {}
    ld_lv = 0
    g.prev_sdtt = 0
    for qc in range(NT5):
        nkt = qc * 4 + 4
        for h in range(8):
            sem_ld = g.s_ld[h % 2]
            p.wait("sync", g.s_st[0], spill_all[0])
            p.wait("sync", g.s_st[1], spill_all[1])
            p.wait("sync", g.s_pe, av_done)
            p.dma("sync", ksl[:, 0:nkt * 128],
                  g.KS[h * 128:(h + 1) * 128, 0:nkt * 128], sem_ld)
            p.dma("sync", vsl[:, 0:nkt * 128],
                  g.VS[h * 128:(h + 1) * 128, 0:nkt * 128], sem_ld)
            ld_lv = p.dma("sync", qsl[:],
                          g.QS[h * 128:(h + 1) * 128,
                               qc * 512:(qc + 1) * 512], sem_ld)
            p.wait("tensor", sem_ld, ld_lv)
            if qc == 0 and h == 0:
                p.wait("tensor", g.s_gp, dec_lv)
            avb = bank(4)
            ell = bank(6)[0:1, :]
            sdtt = {}
            exp2 = {}
            av2 = {}
            npair = nkt // 2
            p.wait("tensor", g.s_dve, g.prev_sdtt)
            if h == 0 and qc > 0:
                p.wait("tensor", g.s_dve, yev[(qc - 1, NCK - 2)])
                p.wait("tensor", g.s_dve, yev[(qc - 1, NCK - 1)])

            def emit_av1(pi):
                p.wait("tensor", g.s_act, exp2[pi])
                pbsl = pbuf[:, (pi % 2) * 1024:(pi % 2) * 1024 + 1024]
                for u in range(2):
                    kt = 2 * pi + u
                    lastk = (kt == nkt - 1)
                    p.op("tensor", "matmul", avb,
                         vsl[:, kt * 128:(kt + 1) * 128],
                         pbsl[:, u * 512:(u + 1) * 512],
                         start=(kt == 0), stop=lastk)
                    av2[pi] = p.op("tensor", "matmul", ell,
                                   g.onescol_bf[:, 0:1],
                                   pbsl[:, u * 512:(u + 1) * 512],
                                   start=(kt == 0), stop=lastk,
                                   sem=(g.s_pe if u == 1 else None))

            for pi in range(npair):
                for u in range(2):
                    kt = 2 * pi + u
                    sb = bank(2 * (pi % 2) + u)
                    if pi >= 2 and u == 0:
                        p.wait("tensor", g.s_dve, sdtt[pi - 2])
                    smm = p.op("tensor", "matmul", sb,
                               ksl[:, kt * 128:(kt + 1) * 128], qsl[:],
                               start=True, stop=True,
                               sem=(g.s_pe if u == 1 else None))
                if pi >= 1:
                    emit_av1(pi - 1)
                p.wait("vector", g.s_pe, smm)
                if pi >= 2:
                    p.wait("vector", g.s_act, exp2[pi - 2])
                msk = None
                for u in range(2):
                    kt = 2 * pi + u
                    d = 4 * qc - kt + 3
                    sb = bank(2 * (pi % 2) + u)
                    sdh = sd[:, (pi % 2) * 1024 + u * 512:
                             (pi % 2) * 1024 + (u + 1) * 512]
                    dop("tensor_tensor", sdh, sb,
                        dec[:, d * 512:(d + 1) * 512], ALU.mult)
                    if kt >= 4 * qc:
                        di = kt - 4 * qc
                        p.wait("gpsimd", g.s_dve, last)
                        msk = p.op("gpsimd", "tensor_tensor", sdh, sdh,
                                   ngm[:, di * 512:(di + 1) * 512], ALU.add,
                                   sem=g.s_gp)
                sd_lv = last
                sdtt[pi] = sd_lv
                g.prev_sdtt = sd_lv
                p.wait("scalar", g.s_dve, sd_lv)
                if msk is not None:
                    p.wait("scalar", g.s_gp, msk)
                if pi >= 2:
                    p.wait("scalar", g.s_pe, av2[pi - 2])
                sdp = sd[:, (pi % 2) * 1024:(pi % 2) * 1024 + 1024]
                pbsl = pbuf[:, (pi % 2) * 1024:(pi % 2) * 1024 + 1024]
                exp2[pi] = p.op("scalar", "activation", pbsl, sdp, AF.Exp,
                                bias=negmball[:, h:h + 1],
                                scale=float(1.0 / math.sqrt(128.0)),
                                sem=g.s_act)
            emit_av1(npair - 1)
            av_done = av2[npair - 1]
            p.wait("vector", g.s_pe, av_done)
            dop("reciprocal", rlbf[0:1, :], ell, low_prec=True)
            p.wait("tensor", g.s_dve, last)
            bmm = p.op("tensor", "matmul", bank(7), g.ones_bf[0:1, 0:128],
                       rlbf[0:1, :], start=True, stop=True, sem=g.s_pe)
            p.wait("vector", g.s_pe, bmm)
            dop("tensor_copy", bcb[:], bank(7))
            onb_lv = dop("tensor_tensor", g.a1_onb[:], avb, bcb[:], ALU.mult)
            p.wait("scalar", g.s_dve, onb_lv)
            sq = p.op("scalar", "activation", sqb[:], g.a1_onb[:], AF.Square,
                      sem=g.s_act)
            p.wait("tensor", g.s_dve, onb_lv)
            p.wait("tensor", g.s_act, sq)
            s1m = p.op("tensor", "matmul", bank(6)[0:1, :],
                       g.onescol_bf[:, 0:1], g.a1_onb[:],
                       start=True, stop=True)
            s2m = p.op("tensor", "matmul", bank(5)[0:1, :],
                       g.onescol_bf[:, 0:1], sqb[:],
                       start=True, stop=True, sem=g.s_pe)
            p.wait("vector", g.s_pe, s2m)
            dop("tensor_scalar_mul", r1[0:1, :], bank(6)[0:1, :], 1.0 / 128)
            dop("tensor_scalar_mul", r2[0:1, :], bank(5)[0:1, :], 1.0 / 128)
            dop("tensor_tensor", g.a1_rl[0:1, :], r1[0:1, :], r1[0:1, :],
                ALU.mult)
            dop("tensor_tensor", r2[0:1, :], r2[0:1, :], g.a1_rl[0:1, :],
                ALU.subtract)
            p.wait("scalar", g.s_dve, last)
            alv = p.op("scalar", "activation", r2[0:1, :], r2[0:1, :],
                       AF.Ln, bias=g.epscol[0:1, 0:1], sem=g.s_act)
            p.wait("scalar", g.s_act, alv)
            alv = p.op("scalar", "activation", r2[0:1, :], r2[0:1, :],
                       AF.Exp, scale=-0.5, sem=g.s_act)
            p.wait("vector", g.s_act, alv)
            dop("tensor_copy", rrow[0:1, :], r2[0:1, :])
            dop("tensor_tensor", g.a1_rl[0:1, :], r1[0:1, :], r2[0:1, :],
                ALU.mult)
            dop("tensor_copy", mrow[0:1, :], g.a1_rl[0:1, :])
            mr_lv = last
            p.wait("tensor", g.s_dve, mr_lv)
            bm1 = p.op("tensor", "matmul", bank(7), g.ones_bf[0:1, 0:128],
                       rrow[0:1, :], start=True, stop=True, sem=g.s_pe)
            p.wait("vector", g.s_pe, bm1)
            dop("tensor_copy", bcb[:], bank(7))
            lnsl = lnb[:, h * 512:(h + 1) * 512]
            dop("tensor_tensor", lnsl, g.a1_onb[:], bcb[:], ALU.mult)
            p.wait("tensor", g.s_dve, last)
            bm2 = p.op("tensor", "matmul", bank(7), g.ones_bf[0:1, 0:128],
                       mrow[0:1, :], start=True, stop=True, sem=g.s_pe)
            p.wait("vector", g.s_pe, bm2)
            dop("tensor_copy", bcb[:], bank(7))
            ln_ev[(qc, h)] = dop("tensor_tensor", lnsl, lnsl, bcb[:],
                                 ALU.subtract)
        p.wait("tensor", g.s_dve, ln_ev[(qc, 7)])
        p.wait("tensor", g.s_dve, wo_lv)
        for oc in range(NCK):
            b = bank(2 + oc % 2)
            if oc >= 2:
                p.wait("tensor", g.s_dve, yev[(qc, oc - 2)])
            elif qc > 0:
                p.wait("tensor", g.s_dve, yev[(qc - 1, NCK - 2 + oc)])
            for h2 in range(8):
                p.op("tensor", "matmul", b,
                     wo[:, h2 * 1024 + oc * 128:h2 * 1024 + (oc + 1) * 128],
                     lnb[:, h2 * 512:(h2 + 1) * 512],
                     start=(h2 == 0), stop=False)
            mm = p.op("tensor", "matmul", b,
                      borow[0:1, oc * 128:(oc + 1) * 128],
                      g.ones_bf[0:1, 0:512], start=False, stop=True,
                      sem=g.s_pe)
            p.wait("vector", g.s_pe, mm)
            yev[(qc, oc)] = p.op(
                "vector", "tensor_copy",
                a_reg[:, oc * T + qc * 512:oc * T + (qc + 1) * 512], b,
                sem=g.s_dve)
    g.attn1_done = g.s_dve.n


def emit_ffn1(p, g, a_reg, y_reg):
    ps = g.ps
    bank = g.bank
    W = g.weights
    wt = W["L1_ffn_weight"]
    dis = W["L1_ffn_disha"]

    weff = g.f_weff
    wstg = g.f_wstg
    dish = g.f_dish

    p.wait("gpsimd", g.s_pe, g.s_pe.n)
    p.wait("gpsimd", g.s_dve, g.s_dve.n)
    p.wait("gpsimd", g.s_gp, g.s_gp.n)
    for _s in g.s_st + g.s_ld:
        p.wait("gpsimd", _s, _s.n)

    wl = 0
    for oc in range(NCK):
        wl = p.dma("gpsimd", wstg[:, oc * 1024:(oc + 1) * 1024],
                   bass.AP(wt, oc * 128 * 1024, [[1024, 128], [1, 1024]]),
                   g.s_wt[0])
    dl = p.dma("gpsimd", dish[0:64, :], bass.AP(dis, 0, [[1024, 64], [1, 1024]]),
               g.s_rw)
    dl = p.dma("gpsimd", dish[64:128, :], bass.AP(dis, 0, [[1024, 64], [1, 1024]]),
               g.s_rw)
    p.wait("vector", g.s_rw, dl)
    dsc = p.op("vector", "tensor_scalar_mul", dish[:], dish[:], 0.5,
               sem=g.s_dve)

    p.wait("tensor", g.s_wt[0], wl)
    p.wait("tensor", g.s_gp, g.const_lvl)
    ev = {}
    for dc in range(NCK):
        for oc in range(NCK):
            i = dc * NCK + oc
            tb = g.psbf[:, (7 if i % 2 == 0 else 5) * 1024:
                        (7 if i % 2 == 0 else 5) * 1024 + 128]
            if i >= 2:
                p.wait("tensor", g.s_dve, ev[i - 2])
            lv = p.op("tensor", "transpose", tb,
                      wstg[:, oc * 1024 + dc * 128:oc * 1024 + (dc + 1) * 128],
                      g.ident_bf[:], sem=g.s_pe)
            p.wait("vector", g.s_pe, lv)
            if i == 0:
                p.wait("vector", g.s_dve, dsc)
            ev[i] = p.op("vector", "scalar_tensor_tensor",
                         weff[:, dc * 1024 + oc * 128:dc * 1024 + (oc + 1) * 128],
                         tb, 0.5, dish[:, oc * 128:(oc + 1) * 128],
                         ALU.mult, ALU.add, sem=g.s_dve)
    weff_lv = ev[NCK * NCK - 2]
    weff_lv2 = ev[NCK * NCK - 1]

    a_ready = max(g.a_lv)
    p.wait("tensor", g.s_dve, max(weff_lv, weff_lv2))
    p.wait("tensor", g.s_dve, a_ready)
    yev = {}
    for tau in range(NT5):
        for oc in range(NCK):
            i = tau * NCK + oc
            b = bank(i % 4)
            if i >= 4:
                p.wait("tensor", g.s_dve, yev[i - 4])
            mm = None
            for dc in range(NCK):
                mm = p.op("tensor", "matmul", b,
                          weff[:, dc * 1024 + oc * 128:dc * 1024 + (oc + 1) * 128],
                          a_reg[:, dc * T + tau * 512:dc * T + (tau + 1) * 512],
                          start=(dc == 0), stop=(dc == NCK - 1),
                          sem=(g.s_pe if dc == NCK - 1 else None))
            p.wait("vector", g.s_pe, mm)
            yev[i] = p.op("vector", "tensor_copy",
                          y_reg[:, oc * T + tau * 512:oc * T + (tau + 1) * 512],
                          b, sem=g.s_dve)
    g.ffn_done = g.s_dve.n


SEQ_FULL = ["moe", "attn0", "moe", "ffn1", "attn1", "ffn1"]

WNAMES = [
    "L0_attn_Wq", "L0_attn_bq", "L0_attn_Wk", "L0_attn_bk", "L0_attn_Wv",
    "L0_attn_bv", "L0_attn_Wp", "L0_attn_bp", "L0_ffn_router_w", "L0_ffn_w1",
    "L0_ffn_w3", "L0_ffn_w2", "L1_attn_Wq", "L1_attn_Wk", "L1_attn_Wv",
    "L1_attn_Wo", "L1_attn_bo", "L1_attn_theta", "L1_attn_gamma",
    "L1_attn_beta", "L1_ffn_weight", "L1_ffn_disha",
]

WSHAPES = {
    "L0_attn_Wq": [D, D], "L0_attn_bq": [D], "L0_attn_Wk": [D, D],
    "L0_attn_bk": [D], "L0_attn_Wv": [D, D], "L0_attn_bv": [D],
    "L0_attn_Wp": [D, D], "L0_attn_bp": [D], "L0_ffn_router_w": [D, 8],
    "L0_ffn_w1": [8, D, 2048], "L0_ffn_w3": [8, D, 2048],
    "L0_ffn_w2": [8, 2048, D], "L1_attn_Wq": [D, D], "L1_attn_Wk": [D, D],
    "L1_attn_Wv": [D, D], "L1_attn_Wo": [D, D], "L1_attn_bo": [D],
    "L1_attn_theta": [8], "L1_attn_gamma": [D], "L1_attn_beta": [D],
    "L1_ffn_weight": [D, D], "L1_ffn_disha": [64, D],
}


def build_nc(stages, dbg=False):
    from contextlib import ExitStack
    nc = bass.Bass("TRN2", target_bir_lowering=False,
                   dynamic_dma_scratch_size=16384)
    g = St()
    g.weights = {}
    for n in WNAMES:
        g.weights[n] = nc.dram_tensor(n, WSHAPES[n], F32, kind="ExternalInput")
    g.x_in = nc.dram_tensor("x_in", [T, D], F32, kind="ExternalInput")
    g.x_out = nc.dram_tensor("x_out", [T, D], F32, kind="ExternalOutput")
    dkind = "ExternalOutput" if dbg else "Internal"
    g.Rbuf = nc.dram_tensor("Rbuf", [D, T], F32, kind=dkind)
    g.Ybuf = nc.dram_tensor("Ybuf", [D, T], F32, kind=dkind)
    g.QS = nc.dram_tensor("QS", [D, T], BF16, kind=dkind)
    g.KS = nc.dram_tensor("KS", [D, T], BF16, kind=dkind)
    g.VS = nc.dram_tensor("VS", [1024, 4160], BF16, kind=dkind)

    with ExitStack() as ctx:
        E = ctx.enter_context
        ra = Arena(nc, ctx, "regA", 65536)
        rb_ = Arena(nc, ctx, "regB", 65536)
        rw = Arena(nc, ctx, "regW", 72 * KB)
        g.REGA = ra.view([128, 32768], BF16, 0)
        g.REGB = rb_.view([128, 32768], BF16, 0)
        g.stag = rw.view([128, 2 * T], F32, 0)
        g.sqbuf = rw.view([128, T], BF16, 32 * KB)
        g.rstd_row = rw.view([1, T], BF16, 40 * KB)
        g.mr_row = rw.view([1, T], BF16, 48 * KB)
        g.rstd_b = rw.view([128, T], BF16, 56 * KB)
        g.mr_b = rw.view([128, T], BF16, 64 * KB)
        g.ystag = rw.view([128, T], F32, 40 * KB)
        g.ystag2 = rw.view([128, T], F32, 56 * KB)
        g.xstage = rw.view([128, 2 * T], F32, 32 * KB)
        g.sq_x = None
        g.sqbuf_x = rw.view([128, T], BF16, 64 * KB)
        g.rstd_row_x = rw.view([1, T], BF16, 32 * KB)
        g.mr_row_x = rw.view([1, T], BF16, 40 * KB)
        g.rstd_b_x = rw.view([128, T], BF16, 48 * KB)
        g.mr_b_x = rw.view([128, T], BF16, 56 * KB)
        g.m_w1 = ra.view([128, 16 * 1024], BF16, 0)
        g.m_w3 = ra.view([128, 16 * 1024], BF16, 32 * KB)
        g.m_w2 = rb_.view([128, 16 * 1024], BF16, 0)
        g.m_gbuf = rb_.view([128, 16 * 512], BF16, 32 * KB)
        g.m_asc = rb_.view([128, 8 * 512], BF16, 48 * KB)
        g.m_asl = rb_.view([128, 8 * 512], BF16, 56 * KB)
        KBf = 1024
        g.a0_q = rw.view([128, T], BF16, 0)
        g.a0_k = rw.view([128, T], BF16, 8 * KBf)
        g.a0_v = rw.view([128, 32 * 130], BF16, 16 * KBf)
        g.a0_cos = rw.view([128, T], BF16, 24 * KBf + 512)
        g.a0_sin = rw.view([128, T], BF16, 32 * KBf + 512)
        g.a0_wq = rw.view([128, 1024], BF16, 40 * KBf + 512)
        g.a0_wqr = rw.view([128, 1024], BF16, 42 * KBf + 512)
        g.a0_wk = rw.view([128, 1024], BF16, 44 * KBf + 512)
        g.a0_wkr = rw.view([128, 1024], BF16, 46 * KBf + 512)
        g.a0_wv = rw.view([128, 1024], BF16, 48 * KBf + 512)
        g.a0_p = rw.view([128, 2048], BF16, 50 * KBf + 512)
        g.a0_t1 = rw.view([128, 512], F32, 54 * KBf + 512)
        g.a0_rows = rw.view([1, 5120], BF16, 56 * KBf + 512)
        g.a0_t2 = rw.view([128, 512], F32, 56 * KBf + 512)
        g.a0_ntile = rw.view([128, 512], F32, 58 * KBf + 512)
        g.a0_i32b = rw.view([128, 512], I32, 60 * KBf + 512)
        g.a0_tang = rw.view([128, 512], F32, 62 * KBf + 512)
        g.a0_rlbf = rw.view([128, 512], BF16, 66 * KBf + 512)
        g.a0_bcb = rw.view([128, 512], F32, 67 * KBf + 512)
        g.a0_tmpB = rw.view([128, 512], BF16, 69 * KBf + 512)
        g.a0_sq = rw.view([128, 512], BF16, 70 * KBf + 512)
        g.a0_invf = rw.view([1, 160], F32, 64 * KBf + 512)
        g.onesf32 = rw.view([1, 128], F32, 71 * KBf + 512)
        g.a0_qs1 = rw.view([128, T], BF16, 24 * KBf + 512)
        g.a0_ks1 = rw.view([128, T], BF16, 32 * KBf + 512)
        g.a0_vs1 = rw.view([128, 32 * 130], BF16, 40 * KBf + 512)
        g.a0_wpf = rw.view([128, 8192], BF16, 0)
        g.a0_bprow = rw.view([1, 1024], BF16, 16 * KBf)
        g.a1_decA = ra.view([128, 16384], F32, 0)
        g.a1_decB = rb_.view([128, 16384], F32, 0)
        g.a1_nt = rw.view([128, 2048], F32, 0)
        g.a1_sd = rw.view([128, 2048], F32, 0)
        g.a1_pb = rw.view([128, 2048], BF16, 8 * KBf)
        g.a1_ngm = rw.view([128, 2048], BF16, 12 * KBf)
        g.a1_onb = rw.view([128, 512], BF16, 0)
        g.a1_eb = rw.view([128, 2048], F32, 8 * KBf)
        g.a1_ln = rw.view([128, 4096], BF16, 60 * KBf + 512)
        g.a1_wo = rw.view([128, 8192], BF16, 16 * KBf)
        g.a1_grow = rw.view([1, 2048], BF16, 32 * KBf)
        g.a1_rl = rw.view([128, 512], F32, 32 * KBf)
        g.a1_bcb = rw.view([128, 512], F32, 34 * KBf)
        g.a1_borow = rw.view([1, 1024], BF16, 36 * KBf)
        g.a1_wq = rw.view([128, 1024], BF16, 38 * KBf + 512)
        g.a1_r1 = rw.view([1, 512], F32, 38 * KBf + 512)
        g.a1_wk = rw.view([128, 1024], BF16, 40 * KBf + 512)
        g.a1_r2 = rw.view([1, 512], F32, 40 * KBf + 512)
        g.a1_wv = rw.view([128, 1024], BF16, 42 * KBf + 512)
        g.a1_mrow = rw.view([1, 512], BF16, 42 * KBf + 512)
        g.a1_rrow = rw.view([1, 512], BF16, 43 * KBf + 512)
        g.a1_qh = rw.view([128, 4096], BF16, 44 * KBf + 512)
        g.a1_mt = rw.view([128, 2048], BF16, 44 * KBf + 512)
        g.a1_kh = rw.view([128, 4096], BF16, 52 * KBf + 512)
        g.a1_vh = rw.view([128, 4096], BF16, 60 * KBf + 512)
        g.a1_sq = rw.view([128, 512], BF16, 68 * KBf + 512)
        g.a1_qsl = rw.view([128, 512], BF16, 69 * KBf + 512)
        g.a1_rlbf = rw.view([128, 512], BF16, 70 * KBf + 512)
        g.f_weff = rw.view([128, 8 * 1024], BF16, 0)
        g.f_wstg = rw.view([128, 8 * 1024], BF16, 16 * KB)
        g.f_dish = rw.view([128, 1024], BF16, 32 * KB)
        g.m_cbb = rw.view([128, 2 * 512], BF16, 0)
        g.m_ystg = rw.view([128, 2 * 512], F32, 2 * KB)
        g.m_crow = rw.view([1, T], BF16, 6 * KB)
        g.Abuf = nc.dram_tensor("Abuf", [D, T], BF16, kind=dkind)
        g.ones_bf = E(nc.sbuf_tensor("ones_bf", [128, 512], BF16))
        g.onescol_bf = E(nc.sbuf_tensor("onescol_bf", [128, 1], BF16))
        g.epscol = E(nc.sbuf_tensor("epscol", [128, 1], F32))
        g.ident_bf = E(nc.sbuf_tensor("ident_bf", [128, 128], BF16))
        g.ident_f32 = E(nc.sbuf_tensor("ident_f32", [128, 128], F32))
        g.statsb = E(nc.sbuf_tensor("statsb", [128, 64], F32))
        g.stat_m = E(nc.sbuf_tensor("stat_m", [128, 32], F32))
        g.stat_v = E(nc.sbuf_tensor("stat_v", [128, 32], F32))
        g.stat_t = E(nc.sbuf_tensor("stat_t", [128, 32], F32))
        g.stat_pk = E(nc.sbuf_tensor("stat_pk", [128, 64], BF16))
        g.lg_sb = E(nc.sbuf_tensor("lg_sb", [128, 256], F32))
        g.r_tmp = E(nc.sbuf_tensor("r_tmp", [128, 256], F32))
        g.r_tmp2 = E(nc.sbuf_tensor("r_tmp2", [128, 256], F32))
        g.r_m1 = E(nc.sbuf_tensor("r_m1", [128, 32], F32))
        g.r_m2 = E(nc.sbuf_tensor("r_m2", [128, 32], F32))
        g.r_el = E(nc.sbuf_tensor("r_el", [128, 32], F32))
        g.c_tm = E(nc.sbuf_tensor("c_tm", [128, 256], BF16))
        g.rw_sb = E(nc.sbuf_tensor("rw_sb", [128, 64], BF16))
        g.a0_mm = E(nc.sbuf_tensor("a0_mm", [1, 16], F32))
        g.a0_negbf = E(nc.sbuf_tensor("a0_negbf", [1, 1], BF16))
        g.a0_invfc = E(nc.sbuf_tensor("a0_invfc", [128, 1], F32))
        g.a0_negmb = E(nc.sbuf_tensor("a0_negmb", [128, 1], F32))
        g.negln8 = E(nc.sbuf_tensor("negln8", [1, 1], F32))
        g.neg1e30 = E(nc.sbuf_tensor("neg1e30", [128, 1], F32))
        nc.const_aps.register(F32, -1e30, g.neg1e30[:, 0:1])
        g.negpi = E(nc.sbuf_tensor("negpi", [128, 1], F32))
        g.negln128h = E(nc.sbuf_tensor("negln128h", [1, 1], F32))
        g.a1_gcol = E(nc.sbuf_tensor("a1_gcol", [128, 8], F32))
        g.a1_bcol = E(nc.sbuf_tensor("a1_bcol", [128, 8], BF16))
        g.a1_thc = E(nc.sbuf_tensor("a1_thc", [128, 8], F32))
        g.a1_th = E(nc.sbuf_tensor("a1_th", [1, 16], F32))
        g.a1_negm8 = E(nc.sbuf_tensor("a1_negm8", [128, 8], F32))
        g.ps = E(nc.psum_tensor("ps", [128, 4096], F32))[:]
        g.psbf = g.ps.bitcast(BF16)
        g.bank = lambda b: g.ps[:, 512 * b:512 * (b + 1)]
        sems = {}
        for nm in ("pe", "act", "dve", "gp"):
            sems[nm] = Sem(E(nc.semaphore(f"s_{nm}")))
        g.s_pe, g.s_act, g.s_dve, g.s_gp = (sems[n] for n in
                                            ("pe", "act", "dve", "gp"))
        g.s_ld = [Sem(E(nc.semaphore(f"s_ld{i}"))) for i in range(2)]
        g.s_st = [Sem(E(nc.semaphore(f"s_st{i}"))) for i in range(2)]
        g.s_wt = [Sem(E(nc.semaphore(f"s_wt{i}"))) for i in range(2)]
        g.s_ya = [Sem(E(nc.semaphore(f"s_ya{i}"))) for i in range(2)]
        g.s_rw = Sem(E(nc.semaphore("s_rw")))
        g.s_w2 = Sem(E(nc.semaphore("s_w2")))
        g.xtp_lv = [0] * NCK

        block = E(nc.Block())
        p = Prog(nc)
        emit_const_setup(p, g)

        other = lambda r: g.REGB if r is g.REGA else g.REGA
        cur_a = g.REGA
        moe_kw = dict()
        if isinstance(stages, dict):
            moe_kw = stages.get("moe_kw", {})
            seq = stages["seq"]
        else:
            seq = stages
        emit_bridge(p, g, yspec=None, a_reg=cur_a, rin_x=True,
                    router=(seq and seq[0] == "moe"),
                    spill_A=(seq and seq[0] == "moe"))
        cur_y = None
        for i, body in enumerate(seq):
            nxt = seq[i + 1] if i + 1 < len(seq) else None
            if body == "moe":
                emit_moe(p, g, 0, True, **moe_kw)
                cur_y = ("hbm", g.Ybuf)
            elif body == "ffn1":
                yreg = other(cur_a)
                emit_ffn1(p, g, cur_a, yreg)
                cur_y = ("sbuf", yreg)
            elif body == "attn0":
                emit_attn0(p, g, cur_a)
                cur_y = ("sbuf", cur_a)
            elif body == "attn1":
                emit_attn1(p, g, cur_a, other(cur_a))
                cur_y = ("sbuf", cur_a)
            else:
                raise NotImplementedError(body)
            if nxt is None:
                emit_bridge(p, g, yspec=cur_y, a_reg=None, out_mode=True)
            else:
                a_new = cur_y[1] if cur_y[0] == "sbuf" else cur_a
                emit_bridge(p, g, yspec=cur_y, a_reg=a_new,
                            router=(nxt == "moe"), spill_A=(nxt == "moe"))
                cur_a = a_new
        if not seq:
            emit_bridge(p, g, yspec=None, a_reg=None, out_mode=True)

        for eng in ENGINES:
            for s in (g.s_st + g.s_ya + g.s_ld + g.s_wt):
                p.wait(eng, s, s.n)
        p.replay(block)
    return nc, g


def kernel(**inputs):
    x = np.ascontiguousarray(np.asarray(inputs["L0_attn_x"], dtype=np.float32))
    nc, g = build_nc(SEQ_FULL)
    wmap = {n: np.ascontiguousarray(np.asarray(inputs[n], dtype=np.float32))
            for n in WNAMES}
    in_maps = [dict(wmap, x_in=x[c]) for c in range(8)]
    res = run_bass_kernel_spmd(nc, in_maps, core_ids=list(range(8)))
    return np.stack([r["x_out"] for r in res.results], axis=0)


if __name__ == "__main__":
    pass


# version 06
# speedup vs baseline: 1.0504x; 1.0048x over previous
import math
import numpy as np
import ml_dtypes
import concourse.bass as bass
import concourse.mybir as mybir
from concourse.bass_utils import run_bass_kernel_spmd

F32 = mybir.dt.float32
BF16 = mybir.dt.bfloat16
I32 = mybir.dt.int32
U8 = mybir.dt.uint8
AF = mybir.ActivationFunctionType
ALU = mybir.AluOpType
AX = mybir.AxisListType

T = 4096
D = 1024
NCK = 8
NTT = 32
NT5 = 8
KB = 1024

ENGINES = ("tensor", "vector", "scalar", "gpsimd", "sync")


class Sem:
    __slots__ = ("h", "n")

    def __init__(self, h):
        self.h = h
        self.n = 0

    def inc(self, k=1):
        self.n += k
        return self.n


class Prog:
    def __init__(self, nc):
        self.nc = nc
        self.q = {e: [] for e in ENGINES}

    def wait(self, eng, sem, lvl):
        if lvl is None or lvl <= 0:
            return
        self.q[eng].append(lambda E, s=sem.h, v=lvl: E.wait_ge(s, v))

    def op(self, eng, meth, *args, sem=None, inc=1, low_prec=False, **kw):
        lvl = sem.inc(inc) if sem is not None else None
        nc = self.nc

        def fn(E, meth=meth, args=args, kw=kw, sem=sem, inc=inc):
            try:
                if low_prec:
                    with nc.allow_low_precision(reason="sanctioned bf16"):
                        inst = getattr(E, meth)(*args, **kw)
                else:
                    inst = getattr(E, meth)(*args, **kw)
            except Exception as e:
                raise RuntimeError(
                    f"op {meth} failed: {e} args0={str(args[0])[:160]}") from e
            if sem is not None:
                inst.then_inc(sem.h, inc)
        self.q[eng].append(fn)
        return lvl

    def dma(self, eng, out, in_, sem, **kw):
        return self.op(eng, "dma_start", sem=sem, inc=16, out=out, in_=in_, **kw)

    def replay(self, block):
        for eng in ENGINES:
            q = self.q[eng]

            def section(E, q=q):
                for fn in q:
                    fn(E)
            getattr(block, eng)(section)


class Arena:
    def __init__(self, nc, ctx, name, nbytes):
        self.nc = nc
        self.backing = ctx.enter_context(nc.sbuf_tensor(name, [128, nbytes], U8))
        self.base = nc.lookup_mloc(self.backing).addr
        self.name = name
        self.n = 0

    def view(self, shape, dtype, offset_bytes):
        name = f"{self.name}_v{self.n}"
        self.n += 1
        return self.nc.alloc_sbuf_tensor_at(
            name, list(shape), dtype, offset=self.base + offset_bytes)


class St:
    pass


def emit_const_setup(p, g):
    p.op("gpsimd", "memset", g.ones_bf[:], 1.0)
    p.op("gpsimd", "memset", g.onescol_bf[:], 1.0)
    p.op("gpsimd", "memset", g.epscol[:], 1e-5)
    p.op("gpsimd", "memset", g.negln8[:], -math.log(8.0))
    p.op("gpsimd", "memset", g.negpi[:], -math.pi)
    p.op("gpsimd", "memset", g.neg1e30[:], -1e30)
    p.op("gpsimd", "memset", g.negln128h[:], -0.5 * math.log(128.0))
    p.op("gpsimd", "memset", g.onesf32[:], 1.0)
    lv = p.op("gpsimd", "memset", g.ident_bf[:], 0.0, sem=g.s_gp)
    p.wait("gpsimd", g.s_gp, lv)
    p.op("gpsimd", "affine_select", g.ident_bf[:], g.ident_bf[:],
         pattern=[[-1, 128]], compare_op=ALU.not_equal, fill=1.0,
         base=0, channel_multiplier=1)
    lv = p.op("gpsimd", "memset", g.ident_f32[:], 0.0, sem=g.s_gp)
    p.wait("gpsimd", g.s_gp, lv)
    p.op("gpsimd", "affine_select", g.ident_f32[:], g.ident_f32[:],
         pattern=[[-1, 128]], compare_op=ALU.not_equal, fill=1.0,
         base=0, channel_multiplier=1, sem=g.s_gp)
    g.const_lvl = g.s_gp.n
    rwa = bass.AP(g.weights["L0_ffn_router_w"], 0,
                  [[8, 128], [128 * 8, 8], [1, 8]])
    g.rw_lvl = p.dma("gpsimd", g.rw_sb[:], rwa, g.s_rw)


def emit_bridge(p, g, *, yspec=None, a_reg=None, rin_x=False, out_mode=False,
                router=False, spill_A=False):
    if rin_x:
        sqbuf, rstd_row, mr_row, rstd_b, mr_b = (g.sqbuf_x, g.rstd_row_x,
                                                 g.mr_row_x, g.rstd_b_x,
                                                 g.mr_b_x)
    else:
        sqbuf, rstd_row, mr_row, rstd_b, mr_b = (g.sqbuf, g.rstd_row,
                                                 g.mr_row, g.rstd_b, g.mr_b)
    """Stream R: R += Y -> Rbuf; A = LN(R) -> a_reg; optional router logits.

    yspec: None | ("sbuf", bf16 [128x32768] handle) | ("hbm", f32 dram handle)
    """
    stag = g.stag
    ps = g.ps
    psbf = g.psbf
    bank = g.bank

    ld_seq = 0
    add_lv = [None] * NCK
    rb_lv = [None] * NCK
    ev_lv = {}
    st_lv = [None] * NCK
    rst_lv = [None] * NCK
    g.a_spill_lv = [0] * NCK
    stat_mm = [None] * NCK
    stat_ev = [None] * NCK
    stats_bank = bank(6)[:, 0:64]

    dve_entry = g.s_dve.n
    act_entry = g.s_act.n
    p.wait("sync", g.s_dve, dve_entry)
    p.wait("sync", g.s_gp, g.s_gp.n)
    for _s in g.s_st + g.s_ya:
        p.wait("sync", _s, _s.n)
    z_lv = p.op("vector", "memset", g.statsb[:], 0.0, sem=g.s_dve)

    for c in range(NCK):
        sl = stag[:, (c % 2) * T:(c % 2) * T + T]
        sem_ld = g.s_ld[c % 2]
        if out_mode:
            sl = stag[:, 0:T]
            if c >= 1:
                p.wait("sync", g.s_dve, ev_lv[(c - 1, NTT - 1)])
        elif c >= 2:
            p.wait("sync", g.s_dve, rb_lv[c - 2])
            p.wait("sync", g.s_st[c % 2], rst_lv[c - 2])
        if rin_x:
            xa = bass.AP(g.x_in, c * 128,
                         [[1024, 128], [128 * 1024, 32], [1, 128]])
            xsl = g.xstage[:, (c % 2) * T:(c % 2) * T + T]
            if c >= 2:
                p.wait("sync", g.s_pe, g.xtp_lv[c - 2])
            lv = p.dma("sync", xsl, xa, sem_ld)
            p.wait("tensor", sem_ld, lv)
            if c == 0:
                p.wait("tensor", g.s_gp, g.const_lvl)
            if c >= 2:
                p.wait("vector", g.s_st[c % 2], rst_lv[c - 2])
            for tt in range(NTT):
                tb = bank(7 if tt % 2 == 0 else 5)[:, 0:128]
                prev = (c, tt - 2) if tt >= 2 else \
                    ((c - 1, NTT - 2 + tt) if c >= 1 else None)
                if prev is not None:
                    p.wait("tensor", g.s_dve, ev_lv[prev])
                xlv = p.op("tensor", "transpose", tb,
                           xsl[:, tt * 128:(tt + 1) * 128], g.ident_f32[:],
                           sem=g.s_pe)
                p.wait("vector", g.s_pe, xlv)
                ev_lv[(c, tt)] = p.op(
                    "vector", "tensor_copy", sl[:, tt * 128:(tt + 1) * 128],
                    tb, sem=g.s_dve)
            g.xtp_lv[c] = xlv
        else:
            lv = p.dma("sync", sl, g.Rbuf[c * 128:(c + 1) * 128, :], sem_ld)
            if yspec is None or yspec[0] == "sbuf":
                p.wait("vector", sem_ld, lv)
            if out_mode and yspec is None:
                p.wait("tensor", sem_ld, lv)

        if yspec is not None:
            kind, yh = yspec
            if kind == "sbuf":
                ysl = yh[:, c * T:(c + 1) * T]
                add_lv[c] = p.op("vector", "tensor_tensor", sl, sl, ysl,
                                 ALU.add, sem=g.s_dve)
            elif out_mode:
                ysl = stag[:, T:2 * T]
                ylv = p.dma("sync", ysl,
                            bass.AP(yh, c * 128 * T, [[T, 128], [1, T]]),
                            sem_ld)
                p.wait("vector", sem_ld, ylv)
                add_lv[c] = p.op("vector", "tensor_tensor", sl, sl, ysl,
                                 ALU.add, sem=g.s_dve)
            else:
                ysl = g.ystag[:, (c % 2) * 2048:(c % 2) * 2048 + 2048]
                ylv = p.dma("sync", ysl,
                            bass.AP(yh, c * 128 * T, [[T, 128], [1, 2048]]),
                            sem_ld)
                ylv = p.dma("sync", g.ystag2[:, (c % 2) * 2048:(c % 2) * 2048 + 2048],
                            bass.AP(yh, c * 128 * T + 2048, [[T, 128], [1, 2048]]),
                            sem_ld)
                p.wait("vector", sem_ld, ylv)
                p.op("vector", "tensor_tensor", sl[:, 0:2048], sl[:, 0:2048],
                     ysl, ALU.add)
                add_lv[c] = p.op("vector", "tensor_tensor", sl[:, 2048:4096],
                                 sl[:, 2048:4096],
                                 g.ystag2[:, (c % 2) * 2048:(c % 2) * 2048 + 2048],
                                 ALU.add, sem=g.s_dve)

        if out_mode:
            if c >= 2:
                p.wait("vector", g.s_st[c % 2], st_lv[c - 2])
            for tt in range(NTT):
                tb = bank(7 if tt % 2 == 0 else 5)[:, 0:128]
                p.wait("tensor", g.s_dve, add_lv[c])
                if add_lv[c] is None and tt == 0 and c == 0:
                    p.wait("tensor", g.s_dve, dve_entry)
                    p.wait("tensor", g.s_act, act_entry)
                prev = (c, tt - 2) if tt >= 2 else \
                    ((c - 1, NTT - 2 + tt) if c >= 1 else None)
                if prev is not None:
                    p.wait("tensor", g.s_dve, ev_lv[prev])
                xlv = p.op("tensor", "transpose", tb,
                           sl[:, tt * 128:(tt + 1) * 128], g.ident_f32[:],
                           sem=g.s_pe)
                p.wait("vector", g.s_pe, xlv)
                ev_lv[(c, tt)] = p.op(
                    "vector", "tensor_copy",
                    g.xstage[:, (c % 2) * T + tt * 128:(c % 2) * T + (tt + 1) * 128],
                    tb, sem=g.s_dve)
            p.wait("sync", g.s_dve, ev_lv[(c, NTT - 1)])
            oa = bass.AP(g.x_out, c * 128,
                         [[1024, 128], [128 * 1024, 32], [1, 128]])
            st_lv[c] = p.dma("sync", oa,
                             g.xstage[:, (c % 2) * T:(c % 2) * T + T],
                             g.s_st[c % 2])
            continue

        if yspec is not None or rin_x:
            p.wait("sync", g.s_dve, add_lv[c] if yspec else ev_lv[(c, NTT - 1)])
            rst_lv[c] = p.dma("sync", g.Rbuf[c * 128:(c + 1) * 128, :], sl,
                              g.s_st[c % 2])

        rb = a_reg[:, c * T:(c + 1) * T]
        if yspec is not None:
            p.wait("vector", g.s_dve, add_lv[c])
        elif rin_x:
            p.wait("vector", g.s_dve, ev_lv[(c, NTT - 1)])
        rb_lv[c] = p.op("vector", "tensor_copy", rb, sl, sem=g.s_dve)
        p.wait("scalar", g.s_dve, rb_lv[c])
        if c >= 1:
            p.wait("scalar", g.s_pe, stat_mm[c - 1])
        sq_lv = p.op("scalar", "activation", sqbuf[:], rb, AF.Square,
                     sem=g.s_act)
        p.wait("tensor", g.s_dve, rb_lv[c])
        p.wait("tensor", g.s_act, sq_lv)
        if c == 0:
            p.wait("tensor", g.s_gp, g.const_lvl)
        else:
            p.wait("tensor", g.s_dve, stat_ev[c - 1])
        for tt in range(NTT):
            p.op("tensor", "matmul", stats_bank[:, 2 * tt:2 * tt + 1],
                 rb[:, tt * 128:(tt + 1) * 128], g.onescol_bf[:, 0:1],
                 start=True, stop=True)
            stat_mm[c] = p.op(
                "tensor", "matmul", stats_bank[:, 2 * tt + 1:2 * tt + 2],
                sqbuf[:, tt * 128:(tt + 1) * 128], g.onescol_bf[:, 0:1],
                start=True, stop=True,
                sem=(g.s_pe if tt == NTT - 1 else None))
        p.wait("vector", g.s_pe, stat_mm[c])
        p.wait("vector", g.s_dve, z_lv if c == 0 else stat_ev[c - 1])
        stat_ev[c] = p.op("vector", "tensor_tensor", g.statsb[:], g.statsb[:],
                          stats_bank, ALU.add, sem=g.s_dve)

    if out_mode:
        for c in range(NCK):
            p.wait("sync", g.s_st[c % 2], st_lv[c])
        return

    sb = g.statsb
    p.wait("vector", g.s_dve, stat_ev[NCK - 1])
    lv = p.op("vector", "tensor_scalar_mul", g.stat_m[:], sb[:, 0:64:2],
              1.0 / D, sem=g.s_dve)
    p.op("vector", "tensor_scalar_mul", g.stat_v[:], sb[:, 1:64:2], 1.0 / D)
    p.wait("vector", g.s_dve, lv)
    lv = p.op("vector", "tensor_tensor", g.stat_t[:], g.stat_m[:], g.stat_m[:],
              ALU.mult, sem=g.s_dve)
    p.wait("vector", g.s_dve, lv)
    lv = p.op("vector", "tensor_tensor", g.stat_v[:], g.stat_v[:], g.stat_t[:],
              ALU.subtract, sem=g.s_dve)
    p.wait("scalar", g.s_dve, lv)
    lv = p.op("scalar", "activation", g.stat_v[:], g.stat_v[:], AF.Ln,
              bias=g.epscol[:, 0:1], sem=g.s_act)
    p.wait("scalar", g.s_act, lv)
    lv = p.op("scalar", "activation", g.stat_v[:], g.stat_v[:], AF.Exp,
              scale=-0.5, sem=g.s_act)
    p.wait("vector", g.s_act, lv)
    lv = p.op("vector", "tensor_tensor", g.stat_t[:], g.stat_m[:], g.stat_v[:],
              ALU.mult, sem=g.s_dve)
    p.op("vector", "tensor_copy", g.stat_pk[:, 0:32], g.stat_v[:])
    p.wait("vector", g.s_dve, lv)
    lv = p.op("vector", "tensor_copy", g.stat_pk[:, 32:64], g.stat_t[:],
              sem=g.s_dve)
    p.wait("tensor", g.s_dve, lv)
    p.wait("tensor", g.s_dve, stat_ev[NCK - 1])
    for tt in range(NTT):
        p.op("tensor", "transpose", psbf[0:1, tt * 128:(tt + 1) * 128],
             g.stat_pk[:, tt:tt + 1], g.ident_bf[:])
    for tt in range(NTT):
        p.op("tensor", "transpose",
             psbf[0:1, 4096 + tt * 128:4096 + (tt + 1) * 128],
             g.stat_pk[:, 32 + tt:32 + tt + 1], g.ident_bf[:],
             sem=(g.s_pe if tt == NTT - 1 else None))
    p.wait("vector", g.s_pe, g.s_pe.n)
    p.op("vector", "tensor_copy", rstd_row[:], psbf[0:1, 0:4096])
    lv = p.op("vector", "tensor_copy", mr_row[:], psbf[0:1, 4096:8192],
              sem=g.s_dve)
    p.wait("tensor", g.s_dve, lv)
    bc_lv = {}
    for j in range(2 * NT5):
        row = rstd_row if j < NT5 else mr_row
        dstt = rstd_b if j < NT5 else mr_b
        jj = j % NT5
        b = bank(6 + (j % 2))
        if j >= 2:
            p.wait("tensor", g.s_dve, bc_lv[j - 2])
        p.op("tensor", "matmul", b, g.ones_bf[0:1, 0:128],
             row[:, jj * 512:(jj + 1) * 512], start=True, stop=True, sem=g.s_pe)
        p.wait("vector", g.s_pe, g.s_pe.n)
        bc_lv[j] = p.op("vector", "tensor_copy",
                        dstt[:, jj * 512:(jj + 1) * 512], b, sem=g.s_dve)

    if router:
        lgb = bank(4)[:, 0:256]
        zl = p.op("vector", "memset", g.lg_sb[:], 0.0, sem=g.s_dve)
        lg_mm = None
        lg_ev = None
    ap_lv = [None] * NCK
    for c in range(NCK):
        rb = a_reg[:, c * T:(c + 1) * T]
        if c == 0:
            p.wait("vector", g.s_dve, bc_lv[2 * NT5 - 1])
        lv = p.op("vector", "tensor_tensor", rb, rb, rstd_b[:], ALU.mult,
                  sem=g.s_dve)
        p.wait("vector", g.s_dve, lv)
        ap_lv[c] = p.op("vector", "tensor_tensor", rb, rb, mr_b[:],
                        ALU.subtract, sem=g.s_dve)
        if spill_A:
            p.wait("sync", g.s_dve, ap_lv[c])
            g.a_spill_lv[c] = p.dma("sync", g.Abuf[c * 128:(c + 1) * 128, :],
                                    rb, g.s_st[c % 2])
        if router:
            p.wait("tensor", g.s_dve, ap_lv[c])
            if c == 0:
                p.wait("tensor", g.s_rw, g.rw_lvl)
            else:
                p.wait("tensor", g.s_dve, lg_ev)
            for tt in range(NTT):
                lg_mm = p.op("tensor", "matmul", lgb[:, tt * 8:(tt + 1) * 8],
                             rb[:, tt * 128:(tt + 1) * 128],
                             g.rw_sb[:, c * 8:(c + 1) * 8],
                             start=True, stop=True,
                             sem=(g.s_pe if tt == NTT - 1 else None))
            p.wait("vector", g.s_pe, lg_mm)
            p.wait("vector", g.s_dve, zl if c == 0 else lg_ev)
            lg_ev = p.op("vector", "tensor_tensor", g.lg_sb[:], g.lg_sb[:],
                         lgb, ALU.add, sem=g.s_dve)
    g.a_lv = list(ap_lv)
    g.a_spilled = spill_A
    if router:
        emit_router_combine(p, g)


def emit_router_combine(p, g):
    L = g.lg_sb
    last = g.s_dve.n

    def dop(meth, *a, **k):
        nonlocal last
        p.wait("vector", g.s_dve, last)
        last = p.op("vector", meth, *a, sem=g.s_dve, **k)
        return last

    for tt in range(NTT):
        l8 = L[:, tt * 8:(tt + 1) * 8]
        t8 = g.r_tmp[:, tt * 8:(tt + 1) * 8]
        w8 = g.r_tmp2[:, tt * 8:(tt + 1) * 8]
        nm1 = g.r_m1[:, tt:tt + 1]
        nm2 = g.r_m2[:, tt:tt + 1]
        el = g.r_el[:, tt:tt + 1]
        dop("reduce_max", nm1, l8, axis=AX.X, negate=True)
        dop("tensor_scalar", t8, l8, nm1, None, ALU.add)
        dop("tensor_single_scalar", w8, t8, 0.0, ALU.is_ge)
        dop("scalar_tensor_tensor", w8, w8, -1e30, l8, ALU.mult, ALU.add)
        dop("reduce_max", nm2, w8, axis=AX.X, negate=True)
        lv = dop("tensor_scalar", w8, l8, nm2, None, ALU.add)
        p.wait("scalar", g.s_dve, lv)
        elv = p.op("scalar", "activation", t8, t8, AF.Exp, sem=g.s_act)
        dop("tensor_single_scalar", w8, w8, 0.0, ALU.is_ge)
        p.wait("vector", g.s_act, elv)
        dop("tensor_tensor", w8, w8, t8, ALU.mult)
        dop("reduce_sum", el, w8, axis=AX.X)
        dop("reciprocal", el, el)
        dop("tensor_scalar", g.c_tm[:, tt * 8:(tt + 1) * 8], w8, el, 0.5,
            ALU.mult, ALU.mult)
    g.ctm_lv = g.s_dve.n


def emit_moe(p, g, L, first_call, n_experts=8, n_tau=NT5, prefetch=True, no_accum=False):
    ps = g.ps
    psbf = g.psbf
    bank = g.bank
    W = g.weights
    w1 = W["L0_ffn_w1"]
    w3 = W["L0_ffn_w3"]
    w2 = W["L0_ffn_w2"]

    gbuf = g.m_gbuf
    asc = g.m_asc
    asl = g.m_asl
    w1sb = g.m_w1
    w3sb = g.m_w3
    w2sb = g.m_w2
    cbb = g.m_cbb
    ystg = g.m_ystg
    crow = g.m_crow

    assert g.a_spilled, "MoE requires A spilled to HBM"
    p.wait("vector", g.s_gp, g.s_gp.n)
    p.wait("gpsimd", g.s_gp, g.s_gp.n)
    for _s in g.s_st + g.s_ld:
        p.wait("vector", _s, _s.n)
        p.wait("gpsimd", _s, _s.n)

    def w1ap(e, fc):
        return bass.AP(w1, e * 1024 * 2048 + fc * 128,
                       [[2048, 128], [128 * 2048, 8], [1, 128]])

    def w3ap(e, fc):
        return bass.AP(w3, e * 1024 * 2048 + fc * 128,
                       [[2048, 128], [128 * 2048, 8], [1, 128]])

    def w2ap(e, fc):
        return bass.AP(w2, e * 2048 * 1024 + fc * 128 * 1024,
                       [[1024, 128], [1, 1024]])

    p.wait("gpsimd", g.s_pe, g.s_pe.n)
    p.wait("gpsimd", g.s_dve, g.s_dve.n)
    w13_lv = 0
    w2_lv = 0
    for fc in range(16):
        p.dma("gpsimd", w1sb[:, fc * 1024:(fc + 1) * 1024], w1ap(0, fc),
              g.s_wt[0])
        w13_lv = p.dma("gpsimd", w3sb[:, fc * 1024:(fc + 1) * 1024],
                       w3ap(0, fc), g.s_wt[0])
        w2_lv = p.dma("gpsimd", w2sb[:, fc * 1024:(fc + 1) * 1024],
                      w2ap(0, fc), g.s_w2)
    w13_done = [w13_lv] + [None] * 8
    w2_done = [w2_lv] + [None] * 8

    cb_lv = [0, 0]
    ystg_dma = [0, 0]
    yev = [0] * 8
    sil_lv = {}
    g1_lv = {}
    h3_last = 0
    h3_tau2 = [0, 0]
    asl_lv2 = [0, 0]
    out_fc_lv = {}

    for e in range(n_experts):
        if not prefetch and e > 0:
            p.wait("gpsimd", g.s_pe, g.s_pe.n)
            p.wait("gpsimd", g.s_dve, g.s_dve.n)
            for fc in range(16):
                p.dma("gpsimd", w1sb[:, fc * 1024:(fc + 1) * 1024],
                      w1ap(e, fc), g.s_wt[e % 2])
                w13_done[e] = p.dma("gpsimd",
                                    w3sb[:, fc * 1024:(fc + 1) * 1024],
                                    w3ap(e, fc), g.s_wt[e % 2])
                w2_done[e] = p.dma("gpsimd", w2sb[:, fc * 1024:(fc + 1) * 1024],
                                   w2ap(e, fc), g.s_w2)
        p.wait("tensor", g.s_dve, g.ctm_lv)
        for dc in range(4):
            p.wait("tensor", g.s_dve, yev[dc])
        for tt in range(NTT):
            p.op("tensor", "transpose", psbf[0:1, tt * 128:(tt + 1) * 128],
                 g.c_tm[:, tt * 8 + e:tt * 8 + e + 1], g.ident_bf[:],
                 sem=(g.s_pe if tt == NTT - 1 else None))
        p.wait("vector", g.s_pe, g.s_pe.n)
        crow_lv = p.op("vector", "tensor_copy", crow[:], psbf[0:1, 0:4096],
                       sem=g.s_dve)

        for tau in range(n_tau):
            par = tau % 2
            p.wait("sync", g.s_pe, h3_tau2[par])
            aap = bass.AP(g.Abuf, tau * 512,
                          [[4096, 128], [128 * 4096, 8], [1, 512]])
            if e == 0 and tau == 0:
                p.wait("sync", g.s_st[0], g.a_spill_lv[NCK - 2])
                p.wait("sync", g.s_st[1], g.a_spill_lv[NCK - 1])
            aslv = asl[:, par * 4096:par * 4096 + 4096]
            asl_lv2[par] = p.dma("sync", aslv, aap, g.s_ld[par])
            p.wait("tensor", g.s_dve, crow_lv)
            p.wait("tensor", g.s_dve, yev[5])
            p.wait("tensor", g.s_dve, cb_lv[par])
            b5 = bank(5)
            lv = p.op("tensor", "matmul", b5, g.ones_bf[0:1, 0:128],
                      crow[:, tau * 512:(tau + 1) * 512], start=True,
                      stop=True, sem=g.s_pe)
            p.wait("vector", g.s_pe, lv)
            cbt = cbb[:, par * 512:(par + 1) * 512]
            cb_lv[par] = p.op("vector", "tensor_copy", cbt, b5, sem=g.s_dve)
            p.wait("vector", g.s_dve, cb_lv[par])
            p.wait("vector", g.s_ld[par], asl_lv2[par])
            p.wait("vector", g.s_pe, h3_last)
            asc_lv = None
            for dc in range(NCK):
                asc_lv = p.op("vector", "tensor_tensor",
                              asc[:, dc * 512:(dc + 1) * 512],
                              aslv[:, dc * 512:(dc + 1) * 512], cbt,
                              ALU.mult, sem=g.s_dve)
            if tau == 0:
                p.wait("tensor", g.s_wt[e % 2], w13_done[e])
            for fc in range(16):
                fp = fc % 2
                bh1 = bank(fp)
                bh3 = bank(2 + fp)
                key = fc - 2 if fc >= 2 else 14 + fc
                if (e + tau > 0) or fc >= 2:
                    p.wait("tensor", g.s_act, sil_lv.get(key, 0))
                    p.wait("tensor", g.s_dve, g1_lv.get(key, 0))
                if fc < 2:
                    p.wait("tensor", g.s_dve, yev[fp])
                    p.wait("tensor", g.s_dve, yev[2 + fp])
                if e > 0 and tau == 0 and fc < 2:
                    p.wait("tensor", g.s_dve, crow_lv)
                if fc == 0:
                    p.wait("tensor", g.s_ld[par], asl_lv2[par])
                h1_lv = None
                for dc in range(NCK):
                    h1_lv = p.op("tensor", "matmul", bh1,
                                 w1sb[:, fc * 1024 + dc * 128:
                                      fc * 1024 + (dc + 1) * 128],
                                 aslv[:, dc * 512:(dc + 1) * 512],
                                 start=(dc == 0), stop=(dc == NCK - 1),
                                 sem=(g.s_pe if dc == NCK - 1 else None))
                if fc == 0:
                    p.wait("tensor", g.s_dve, asc_lv)
                h3_lv = None
                for dc in range(NCK):
                    h3_lv = p.op("tensor", "matmul", bh3,
                                 w3sb[:, fc * 1024 + dc * 128:
                                      fc * 1024 + (dc + 1) * 128],
                                 asc[:, dc * 512:(dc + 1) * 512],
                                 start=(dc == 0), stop=(dc == NCK - 1),
                                 sem=(g.s_pe if dc == NCK - 1 else None))
                h3_last = h3_lv
                h3_tau2[par] = h3_lv
                if tau == n_tau - 1:
                    h13_final = h3_lv
                    if prefetch and e < n_experts - 1:
                        p.wait("gpsimd", g.s_pe, h3_lv)
                        p.dma("gpsimd", w1sb[:, fc * 1024:(fc + 1) * 1024],
                              w1ap(e + 1, fc), g.s_wt[(e + 1) % 2])
                        lvw = p.dma("gpsimd",
                                    w3sb[:, fc * 1024:(fc + 1) * 1024],
                                    w3ap(e + 1, fc), g.s_wt[(e + 1) % 2])
                        if fc == 15:
                            w13_done[e + 1] = lvw
                p.wait("scalar", g.s_pe, h1_lv)
                gslice = gbuf[:, fc * 512:(fc + 1) * 512]
                sil_lv[fc] = p.op("scalar", "activation", gslice, bh1, AF.Silu,
                                  sem=g.s_act)
                p.wait("vector", g.s_act, sil_lv[fc])
                p.wait("vector", g.s_pe, h3_lv)
                g1_lv[fc] = p.op("vector", "tensor_tensor", gslice, gslice,
                                 bh3, ALU.mult, sem=g.s_dve)
            if tau == 0:
                p.wait("tensor", g.s_w2, w2_done[e])
            out_mm = [None] * 8
            last_tau = (tau == n_tau - 1)
            for fc in range(16):
                p.wait("tensor", g.s_dve, g1_lv[fc])
                for dc in (4, 5, 6, 7, 0, 1, 2, 3):
                    if fc == 0:
                        p.wait("tensor", g.s_dve, yev[dc])
                        if dc == 0:
                            p.wait("tensor", g.s_act, sil_lv[15])
                            p.wait("tensor", g.s_dve, g1_lv[15])
                    want_sem = (fc == 15) or (last_tau and dc == 3)
                    out_mm[dc] = p.op(
                        "tensor", "matmul", bank(dc),
                        w2sb[:, fc * 1024 + dc * 128:fc * 1024 + (dc + 1) * 128],
                        gbuf[:, fc * 512:(fc + 1) * 512],
                        start=(fc == 0), stop=(fc == 15),
                        sem=(g.s_pe if want_sem else None))
                if last_tau:
                    out_fc_lv[fc] = out_mm[3]
            if last_tau and prefetch and e < n_experts - 1:
                for fc in range(16):
                    p.wait("gpsimd", g.s_pe, out_fc_lv[fc])
                    lvw = p.dma("gpsimd", w2sb[:, fc * 1024:(fc + 1) * 1024],
                                w2ap(e + 1, fc), g.s_w2)
                    if fc == 15:
                        w2_done[e + 1] = lvw
            for ei, dc in enumerate((5, 0, 2, 1, 3, 4, 6, 7)):
                p.wait("vector", g.s_pe, out_mm[dc])
                sp = ei % 2
                p.wait("vector", g.s_ya[sp], ystg_dma[sp])
                yev[dc] = p.op("vector", "tensor_copy",
                               ystg[:, sp * 512:(sp + 1) * 512], bank(dc),
                               sem=g.s_dve)
                p.wait("gpsimd", g.s_dve, yev[dc])
                ya = bass.AP(g.Ybuf, dc * 128 * T + tau * 512,
                             [[T, 128], [1, 512]])
                kw = {} if (e == 0 or no_accum) else {"accum_op": ALU.add}
                ystg_dma[sp] = p.dma("gpsimd", ya,
                                     ystg[:, sp * 512:(sp + 1) * 512],
                                     g.s_ya[sp], **kw)
    g.moe_done = True


def emit_attn0(p, g, a_reg):
    ps = g.ps
    bank = g.bank
    W = g.weights
    oreg = g.REGB if a_reg is g.REGA else g.REGA

    qb, kb, vb = g.a0_q, g.a0_k, g.a0_v
    cosb, sinb = g.a0_cos, g.a0_sin
    wqs, wqr, wks, wkr, wvs = g.a0_wq, g.a0_wqr, g.a0_wk, g.a0_wkr, g.a0_wv
    pb = g.a0_p
    t1 = g.a0_t1
    rl = g.a0_t1
    t2 = g.a0_t2
    rlbf, bcb, tmpB = g.a0_rlbf, g.a0_bcb, g.a0_tmpB
    rows = g.a0_rows
    mm8 = g.a0_mm
    negmb = g.a0_negmb
    sqb = g.a0_sq
    invf = g.a0_invf

    last = g.s_dve.n

    def dop(meth, *a, **k):
        nonlocal last
        p.wait("vector", g.s_dve, last)
        last = p.op("vector", meth, *a, sem=g.s_dve, **k)
        return last

    for eng in ("gpsimd", "tensor"):
        p.wait(eng, g.s_pe, g.s_pe.n)
        p.wait(eng, g.s_dve, g.s_dve.n)
        p.wait(eng, g.s_gp, g.s_gp.n)
        for _s in g.s_st + g.s_ld:
            p.wait(eng, _s, _s.n)
    p.wait("vector", g.s_ld[0], g.s_ld[0].n)
    p.wait("vector", g.s_ld[1], g.s_ld[1].n)
    p.wait("vector", g.s_gp, g.s_gp.n)

    i32 = g.a0_i32b
    ntile = g.a0_ntile
    tang = g.a0_tang
    p.op("gpsimd", "iota", invf[0:1, 0:32], pattern=[[1, 32]], base=0,
         channel_multiplier=0, allow_small_or_imprecise_dtypes=True,
         sem=g.s_gp)
    iota_lv = g.s_gp.n
    p.op("gpsimd", "memset", vb[:], 1.0, sem=g.s_gp)
    vb1_lv = g.s_gp.n
    p.wait("vector", g.s_gp, iota_lv)
    lnf = math.log(10000.0) * 2.0 / 64.0
    dop("tensor_scalar_mul", invf[0:1, 0:32], invf[0:1, 0:32], -lnf)
    p.wait("scalar", g.s_dve, last)
    alv = p.op("scalar", "activation", invf[0:1, 0:32], invf[0:1, 0:32],
               AF.Exp, sem=g.s_act)
    p.wait("vector", g.s_act, alv)
    for off in (32, 33, 96, 97):
        dop("tensor_copy", invf[0:1, off:min(off + 64, 160):2],
            invf[0:1, 0:32])
    invf_lv = last
    p.wait("tensor", g.s_dve, invf_lv)
    tl = p.op("tensor", "transpose", g.ps[:, 0:1],
              invf[0:1, 32:160], g.ident_f32[0:1, 0:1], sem=g.s_pe)
    p.wait("vector", g.s_pe, tl)
    invfc_lv = dop("tensor_copy", g.a0_invfc[:], g.ps[:, 0:1])
    for j in range(NT5):
        p.wait("gpsimd", g.s_dve, last)
        p.op("gpsimd", "iota", ntile[:], pattern=[[1, 512]], base=j * 512,
             channel_multiplier=0, allow_small_or_imprecise_dtypes=True,
             sem=g.s_gp)
        p.wait("vector", g.s_gp, g.s_gp.n)
        dop("tensor_scalar_mul", tang[:], ntile[:], g.a0_invfc[:, 0:1])
        for cw in range(2):
            dst = (sinb if cw == 0 else cosb)[:, j * 512:(j + 1) * 512]
            dop("tensor_scalar", t1[:], tang[:], 1.0 / (2 * math.pi),
                0.25 * cw, ALU.mult, ALU.add)
            dop("tensor_copy", i32[:], t1[:])
            dop("tensor_copy", t2[:], i32[:])
            dop("tensor_tensor", t2[:], t2[:], t1[:], ALU.is_gt)
            dop("tensor_copy", t1[:], i32[:])
            dop("tensor_tensor", t1[:], t1[:], t2[:], ALU.subtract)
            if cw == 0:
                dop("scalar_tensor_tensor", t1[:], t1[:], -2 * math.pi,
                    tang[:], ALU.mult, ALU.add)
            else:
                dop("tensor_scalar", t2[:], tang[:], 1.0, math.pi / 2,
                    ALU.mult, ALU.add)
                dop("scalar_tensor_tensor", t1[:], t1[:], -2 * math.pi,
                    t2[:], ALU.mult, ALU.add)
            p.wait("scalar", g.s_dve, last)
            a1 = p.op("scalar", "activation", dst, t1[:], AF.Sin,
                      bias=g.negpi[:, 0:1], sem=g.s_act)
            p.wait("vector", g.s_act, a1)
            dop("tensor_scalar_mul", dst, dst, -1.0)
        cs_act = g.s_act.n

    p.wait("gpsimd", g.s_dve, last)
    bl = 0
    for i, nm in enumerate(["L0_attn_bq", "L0_attn_bk", "L0_attn_bv"]):
        off = (0, 2048, 4096)[i]
        bl = p.dma("gpsimd", rows[0:1, off:off + 1024],
                   bass.AP(W[nm], 0, [[1024, 1], [1, 1024]]), g.s_rw)
    p.wait("vector", g.s_rw, bl)
    for src_off, dst_off in ((0, 1024), (2048, 3072)):
        dop("tensor_scalar_mul", rows[0:1, dst_off:dst_off + 1024:2],
            rows[0:1, src_off + 1:src_off + 1024:2], -1.0)
        dop("tensor_copy", rows[0:1, dst_off + 1:dst_off + 1024:2],
            rows[0:1, src_off:src_off + 1024:2])
    rotb_lv = last

    a_ready = max(g.a_lv)
    negm8p = g.a1_negm8
    last_proj_mm = 0
    pspill = [0, 0]
    for pr in range(8):
        sw = g.s_wt[pr % 2]
        p.wait("gpsimd", g.s_pe, last_proj_mm)
        wl = 0
        for mat, dst in (("L0_attn_Wq", wqs), ("L0_attn_Wk", wks),
                         ("L0_attn_Wv", wvs)):
            wl = p.dma("gpsimd", dst[:],
                       bass.AP(W[mat], pr * 128,
                               [[1024, 128], [128 * 1024, 8], [1, 128]]), sw)
        p.wait("vector", sw, wl)
        p.wait("vector", g.s_pe, last_proj_mm)
        p.wait("vector", g.s_st[(pr + 1) % 2], pspill[(pr + 1) % 2])
        for src, dst in ((wqs, wqr), (wks, wkr)):
            dop("tensor_scalar_mul", dst[:, 0:1024:2], src[:, 1:1024:2], -1.0)
            dop("tensor_copy", dst[:, 1:1024:2], src[:, 0:1024:2])
        wrot_lv = last
        mm_mx = dop("memset", mm8[:], -3e38)

        p.wait("tensor", g.s_dve, wrot_lv)
        if pr == 0:
            p.wait("tensor", g.s_dve, a_ready)
            p.wait("tensor", g.s_act, cs_act)
            p.wait("tensor", g.s_dve, rotb_lv)
        rope_ev = {}
        if pr == 0:
            red_ev = {}
        for qk in range(2):
            wsl, wrl, dstbuf, boff = ((wqs, wqr, qb, 0) if qk == 0
                                      else (wks, wkr, kb, 2048))
            for j in range(NT5):
                b0 = bank(j % 2)
                b1 = bank(2 + j % 2)
                if (qk, j) not in ((0, 0), (0, 1)):
                    prev = (qk, j - 2) if j >= 2 else (qk - 1, NT5 - 2 + j)
                    p.wait("tensor", g.s_dve, rope_ev[prev])
                for dc in range(NCK):
                    p.op("tensor", "matmul", b0,
                         wsl[:, dc * 128:(dc + 1) * 128],
                         a_reg[:, dc * T + j * 512:dc * T + (j + 1) * 512],
                         start=(dc == 0), stop=False)
                p.op("tensor", "matmul", b0,
                     rows[0:1, boff + pr * 128:boff + (pr + 1) * 128],
                     g.ones_bf[0:1, 0:512], start=False, stop=True)
                for dc in range(NCK):
                    p.op("tensor", "matmul", b1,
                         wrl[:, dc * 128:(dc + 1) * 128],
                         a_reg[:, dc * T + j * 512:dc * T + (j + 1) * 512],
                         start=(dc == 0), stop=False)
                pj = p.op("tensor", "matmul", b1,
                          rows[0:1, boff + 1024 + pr * 128:
                               boff + 1024 + (pr + 1) * 128],
                          g.ones_bf[0:1, 0:512], start=False, stop=True,
                          sem=g.s_pe)
                p.wait("vector", g.s_pe, pj)
                dsl = dstbuf[:, j * 512:(j + 1) * 512]
                dop("tensor_tensor", t1[:], b0,
                    cosb[:, j * 512:(j + 1) * 512], ALU.mult)
                dop("tensor_tensor", dsl, b1,
                    sinb[:, j * 512:(j + 1) * 512], ALU.mult)
                rope_ev[(qk, j)] = dop("tensor_tensor", dsl, dsl, t1[:],
                                       ALU.add)
                p.wait("scalar", g.s_dve, rope_ev[(qk, j)])
                sq = p.op("scalar", "activation", sqb[:], dsl, AF.Square,
                          sem=g.s_act)
                for h in range(2):
                    si = qk * 16 + j * 2 + h
                    bsq = bank(6 + si % 2)
                    p.wait("tensor", g.s_act, sq)
                    if si >= 2:
                        p.wait("tensor", g.s_dve, red_ev[si - 2])
                    elif pr > 0:
                        p.wait("tensor", g.s_dve, red_ev[30 + si])
                    mmh = p.op("tensor", "matmul", bsq[0:1, :],
                               g.onescol_bf[h * 64:h * 64 + 64, 0:1],
                               sqb[h * 64:h * 64 + 64, :],
                               start=True, stop=True, sem=g.s_pe)
                    p.wait("vector", g.s_pe, mmh)
                    dop("reduce_max", mm8[0:1, 15:16], bsq[0:1, :],
                        axis=AX.X)
                    red_ev[si] = dop(
                        "tensor_tensor", mm8[0:1, qk * 2 + h:qk * 2 + h + 1],
                        mm8[0:1, qk * 2 + h:qk * 2 + h + 1],
                        mm8[0:1, 15:16], ALU.max)
        p.wait("tensor", g.s_gp, vb1_lv)
        vp_ev = {}
        for tt in range(NTT):
            bv_ = bank(4 + tt % 2)
            if tt >= 2:
                p.wait("tensor", g.s_dve, vp_ev[tt - 2])
            for dc in range(NCK):
                p.op("tensor", "matmul", bv_[:, 0:128],
                     a_reg[:, dc * T + tt * 128:dc * T + (tt + 1) * 128],
                     wvs[:, dc * 128:(dc + 1) * 128],
                     start=(dc == 0), stop=False)
            pv = p.op("tensor", "matmul", bv_[:, 0:128], g.ones_bf[0:1, 0:128],
                      rows[0:1, 4096 + pr * 128:4096 + (pr + 1) * 128],
                      start=False, stop=True, sem=g.s_pe)
            p.wait("vector", g.s_pe, pv)
            dop("tensor_copy", vb[:, tt * 130:tt * 130 + 64], bv_[:, 0:64])
            vp_ev[tt] = dop("tensor_copy",
                            vb[:, tt * 130 + 65:tt * 130 + 129],
                            bv_[:, 64:128])
        last_proj_mm = g.s_pe.n
        p.wait("scalar", g.s_dve, last)
        alv = p.op("scalar", "activation", mm8[0:1, 8:12], mm8[0:1, 0:4],
                   AF.Ln, bias=g.epscol[0:1, 0:1], sem=g.s_act)
        p.wait("vector", g.s_act, alv)
        dop("tensor_tensor", mm8[0:1, 12:14], mm8[0:1, 8:10],
            mm8[0:1, 10:12], ALU.add)
        p.wait("scalar", g.s_dve, last)
        alv = p.op("scalar", "activation", mm8[0:1, 12:14], mm8[0:1, 12:14],
                   AF.Exp, scale=0.5, bias=g.negln8[0:1, 0:1], sem=g.s_act)
        p.wait("vector", g.s_act, alv)
        dop("reduce_max", mm8[0:1, 14:15], mm8[0:1, 12:14], axis=AX.X,
            negate=True)
        dop("tensor_copy", g.a0_negbf[0:1, :], mm8[0:1, 14:15])
        p.wait("tensor", g.s_dve, last)
        bmm = p.op("tensor", "matmul", bank(6)[:, 0:1], g.ones_bf[0:1, 0:128],
                   g.a0_negbf[0:1, 0:1], start=True, stop=True, sem=g.s_pe)
        p.wait("vector", g.s_pe, bmm)
        dop("tensor_copy", negm8p[:, pr:pr + 1], bank(6)[:, 0:1])
        p.wait("sync", g.s_dve, last)
        p.dma("sync", g.QS[pr * 128:(pr + 1) * 128, :], qb[:], g.s_st[pr % 2])
        p.dma("sync", g.KS[pr * 128:(pr + 1) * 128, :], kb[:], g.s_st[pr % 2])
        pspill[pr % 2] = p.dma(
            "sync", g.VS[pr * 128:(pr + 1) * 128, :], vb[:], g.s_st[pr % 2])
    spill_done = (g.s_st[0].n, g.s_st[1].n)

    qsl2 = (qb, g.a0_qs1)
    ksl2 = (kb, g.a0_ks1)
    vsl2 = (vb, g.a0_vs1)
    last_exp = 0
    av_ev = {}
    bc_ev = {}
    prev_b_dma = 0
    exp_lv = {}
    av_lv = {}
    pair_ld = [0, 0]
    pair_att_end = [0, 0]
    for pr in range(8):
        par = pr % 2
        sem_ld = g.s_ld[par]
        qbuf, kbuf, vbuf = qsl2[par], ksl2[par], vsl2[par]
        p.wait("sync", g.s_st[0], spill_done[0])
        p.wait("sync", g.s_st[1], spill_done[1])
        p.wait("sync", g.s_pe, pair_att_end[par])
        p.dma("sync", qbuf[:], g.QS[pr * 128:(pr + 1) * 128, :], sem_ld)
        p.dma("sync", kbuf[:], g.KS[pr * 128:(pr + 1) * 128, :], sem_ld)
        pair_ld[par] = p.dma("sync", vbuf[:],
                             g.VS[pr * 128:(pr + 1) * 128, :], sem_ld)
        p.wait("tensor", sem_ld, pair_ld[par])
        opair = oreg[:, pr * 4096:(pr + 1) * 4096]
        for qc in range(NT5):
            avA = bank(4)
            avB = bank(5)

            def emit_av(kt, qcv):
                p.wait("tensor", g.s_act, exp_lv[kt])
                if kt == 0:
                    p.wait("tensor", g.s_dve, av_ev.get((pr, qcv - 1), 0))
                    if qcv == 0 and pr > 0:
                        p.wait("tensor", g.s_dve,
                               av_ev.get((pr - 1, NT5 - 1), 0))
                pbsl = pb[:, (kt % 2) * 1024:(kt % 2) * 1024 + 1024]
                p.op("tensor", "matmul", avA[0:65, :],
                     vbuf[:, kt * 130:kt * 130 + 65], pbsl[:, 0:512],
                     start=(kt == 0), stop=(kt == NTT - 1))
                av_lv[kt] = p.op("tensor", "matmul", avB[0:65, :],
                                 vbuf[:, kt * 130 + 65:kt * 130 + 130],
                                 pbsl[:, 512:1024],
                                 start=(kt == 0), stop=(kt == NTT - 1),
                                 sem=g.s_pe)

            for kt in range(NTT):
                sb0 = bank(2 * (kt % 2))
                sb1 = bank(2 * (kt % 2) + 1)
                if kt >= 2:
                    p.wait("tensor", g.s_act, exp_lv[kt - 2])
                elif qc > 0 or pr > 0:
                    p.wait("tensor", g.s_act,
                           exp_lv.get(NTT - 2 + kt, last_exp))
                p.op("tensor", "matmul", sb0,
                     kbuf[0:64, kt * 128:(kt + 1) * 128],
                     qbuf[0:64, qc * 512:(qc + 1) * 512], start=True,
                     stop=True, tile_position=(0, 0))
                smm = p.op("tensor", "matmul", sb1,
                           kbuf[64:128, kt * 128:(kt + 1) * 128],
                           qbuf[64:128, qc * 512:(qc + 1) * 512],
                           start=True, stop=True, tile_position=(64, 0),
                           sem=g.s_pe)
                if kt >= 1:
                    emit_av(kt - 1, qc)
                p.wait("scalar", g.s_pe, smm)
                if kt >= 2:
                    p.wait("scalar", g.s_pe, av_lv[kt - 2])
                elif qc > 0 or pr > 0:
                    p.wait("scalar", g.s_pe, av_lv.get(NTT - 2 + kt, 0))
                pbsl = pb[:, (kt % 2) * 1024:(kt % 2) * 1024 + 1024]
                psp = ps[:, 2 * (kt % 2) * 512:2 * (kt % 2) * 512 + 1024]
                exp_lv[kt] = p.op("scalar", "activation", pbsl, psp, AF.Exp,
                                  bias=negm8p[:, pr:pr + 1], scale=0.125,
                                  sem=g.s_act)
            emit_av(NTT - 1, qc)
            last_exp = exp_lv[NTT - 1]
            p.wait("vector", g.s_pe, av_lv[NTT - 1])
            for h, avb_ in ((0, avA), (1, avB)):
                dop("reciprocal", rlbf[64:65, 0:512], avb_[64:65, :],
                    low_prec=True)
                p.wait("tensor", g.s_dve, last)
                prevk = (pr, qc - 1, h) if qc > 0 else (pr - 1, NT5 - 1, h)
                p.wait("tensor", g.s_dve, bc_ev.get(prevk, 0))
                bmm = p.op("tensor", "matmul", bank(6 + h)[0:64, :],
                           g.ones_bf[64:65, 0:64], rlbf[64:65, 0:512],
                           start=True, stop=True, sem=g.s_pe)
                p.wait("vector", g.s_pe, bmm)
                dop("tensor_copy", bcb[0:64, :], bank(6 + h)[0:64, :])
                bc_ev[(pr, qc, h)] = last
                if h == 0:
                    dop("tensor_tensor", opair[0:64, qc * 512:(qc + 1) * 512],
                        avb_[0:64, :], bcb[0:64, :], ALU.mult)
                else:
                    p.wait("vector", g.s_st[1], prev_b_dma)
                    dop("tensor_tensor", tmpB[0:64, :], avb_[0:64, :],
                        bcb[0:64, :], ALU.mult)
                    p.wait("sync", g.s_dve, last)
                    prev_b_dma = p.dma(
                        "sync", opair[64:128, qc * 512:(qc + 1) * 512],
                        tmpB[0:64, :], g.s_st[1])
            av_ev[(pr, qc)] = last
        pair_att_end[par] = g.s_pe.n

    wpf = g.a0_wpf
    bprow = g.a0_bprow
    p.wait("gpsimd", g.s_pe, g.s_pe.n)
    p.wait("gpsimd", g.s_dve, g.s_dve.n)
    p.wait("gpsimd", g.s_act, g.s_act.n)
    wl = 0
    for dc in range(NCK):
        wl = p.dma("gpsimd", wpf[:, dc * 1024:(dc + 1) * 1024],
                   bass.AP(W["L0_attn_Wp"], dc * 128 * 1024,
                           [[1024, 128], [1, 1024]]), g.s_wt[0])
    blr = p.dma("gpsimd", bprow[0:1, :],
                bass.AP(W["L0_attn_bp"], 0, [[1024, 1], [1, 1024]]), g.s_rw)
    p.wait("tensor", g.s_wt[0], wl)
    p.wait("tensor", g.s_rw, blr)
    p.wait("tensor", g.s_st[1], prev_b_dma)
    p.wait("tensor", g.s_dve, g.s_dve.n)
    p.wait("tensor", g.s_act, g.s_act.n)
    yev = {}
    for tau in range(NT5):
        for oc in range(NCK):
            i = tau * NCK + oc
            b = bank(i % 4)
            if i >= 4:
                p.wait("tensor", g.s_dve, yev[i - 4])
            for pr2 in range(8):
                p.op("tensor", "matmul", b,
                     wpf[:, pr2 * 1024 + oc * 128:pr2 * 1024 + (oc + 1) * 128],
                     oreg[:, pr2 * 4096 + tau * 512:
                          pr2 * 4096 + (tau + 1) * 512],
                     start=(pr2 == 0), stop=False)
            mm = p.op("tensor", "matmul", b,
                      bprow[0:1, oc * 128:(oc + 1) * 128],
                      g.ones_bf[0:1, 0:512], start=False, stop=True,
                      sem=g.s_pe)
            p.wait("vector", g.s_pe, mm)
            yev[i] = p.op("vector", "tensor_copy",
                          a_reg[:, oc * T + tau * 512:oc * T + (tau + 1) * 512],
                          b, sem=g.s_dve)
    g.attn0_done = g.s_dve.n


def emit_attn1(p, g, a_reg, decay_reg):
    ps = g.ps
    bank = g.bank
    W = g.weights
    dec = g.a1_decA if decay_reg is g.REGA else g.a1_decB

    ntile, ebuf = g.a1_nt, g.a1_eb
    wo = g.a1_wo
    gcol, bcol, borow = g.a1_gcol, g.a1_bcol, g.a1_borow
    thr = g.a1_th
    wq_, wk_, wv_ = g.a1_wq, g.a1_wk, g.a1_wv
    qh, kh, vh = g.a1_qh, g.a1_kh, g.a1_vh
    ksl, vsl, qsl = g.a1_qh, g.a1_kh, g.a1_qsl
    sd, pbuf = g.a1_sd, g.a1_pb
    lnb = g.a1_ln
    sqb = g.a1_sq
    bcb, rlbf = g.a1_bcb, g.a1_rlbf
    r1, r2 = g.a1_r1, g.a1_r2
    mrow, rrow = g.a1_mrow, g.a1_rrow
    negmball = g.a1_negm8
    mm8 = g.a0_mm

    last = g.s_dve.n

    def dop(meth, *a, **k):
        nonlocal last
        p.wait("vector", g.s_dve, last)
        last = p.op("vector", meth, *a, sem=g.s_dve, **k)
        return last

    for eng in ("gpsimd", "tensor"):
        p.wait(eng, g.s_pe, g.s_pe.n)
        p.wait(eng, g.s_dve, g.s_dve.n)
        p.wait(eng, g.s_gp, g.s_gp.n)
        for _s in g.s_st + g.s_ld:
            p.wait(eng, _s, _s.n)
    p.wait("vector", g.s_gp, g.s_gp.n)

    p.op("gpsimd", "memset", g.onesf32[:], 1.0)
    tl = p.dma("gpsimd", thr[0:1, 0:8],
               bass.AP(W["L1_attn_theta"], 0, [[8, 1], [1, 8]]), g.s_rw)
    grow = g.a1_grow
    tl = p.dma("gpsimd", grow[0:1, 0:1024],
               bass.AP(W["L1_attn_gamma"], 0, [[1024, 1], [1, 1024]]), g.s_rw)
    tl = p.dma("gpsimd", grow[0:1, 1024:2048],
               bass.AP(W["L1_attn_beta"], 0, [[1024, 1], [1, 1024]]), g.s_rw)
    bl = p.dma("gpsimd", borow[0:1, :],
               bass.AP(W["L1_attn_bo"], 0, [[1024, 1], [1, 1024]]), g.s_rw)
    wl = 0
    for dc in range(NCK):
        wl = p.dma("gpsimd", wo[:, dc * 1024:(dc + 1) * 1024],
                   bass.AP(W["L1_attn_Wo"], dc * 128 * 1024,
                           [[1024, 128], [1, 1024]]), g.s_wt[0])
    p.wait("tensor", g.s_rw, bl)
    p.wait("tensor", g.s_gp, g.const_lvl)
    for i in range(16):
        p.op("tensor", "transpose", g.psbf[:, 2 * i:2 * i + 1],
             grow[0:1, i * 128:(i + 1) * 128], g.ident_bf[0:1, 0:1],
             sem=(g.s_pe if i == 15 else None))
    p.wait("vector", g.s_pe, g.s_pe.n)
    for dc in range(NCK):
        dop("tensor_copy", gcol[:, dc:dc + 1], g.psbf[:, 2 * dc:2 * dc + 1])
        dop("tensor_copy", bcol[:, dc:dc + 1],
            g.psbf[:, 16 + 2 * dc:17 + 2 * dc])
    gcol_lv = last
    p.wait("tensor", g.s_wt[0], wl)
    p.wait("tensor", g.s_dve, gcol_lv)
    b4 = ps[:, 4 * 512:4 * 512 + 1024]
    bo_mm = None
    for half in range(2):
        for dc in range(NCK):
            bo_mm = p.op("tensor", "matmul",
                         ps[:, (4 + half) * 512:(5 + half) * 512][0:1, :],
                         bcol[:, dc:dc + 1],
                         wo[:, dc * 1024 + half * 512:
                            dc * 1024 + half * 512 + 512],
                         start=(dc == 0), stop=(dc == NCK - 1),
                         sem=(g.s_pe if dc == NCK - 1 else None))
    p.wait("vector", g.s_pe, bo_mm)
    dop("tensor_tensor", borow[0:1, :], borow[0:1, :], b4[0:1, :], ALU.add)
    for dc in range(NCK):
        dop("tensor_scalar_mul", wo[:, dc * 1024:(dc + 1) * 1024],
            wo[:, dc * 1024:(dc + 1) * 1024], gcol[:, dc:dc + 1])
    wo_lv = last
    p.wait("vector", g.s_rw, g.s_rw.n)
    dop("tensor_scalar_mul", thr[0:1, 8:16], thr[0:1, 0:8], -1.0)
    th_lv = last
    p.wait("tensor", g.s_dve, th_lv)
    thc = g.a1_thc
    for h in range(8):
        p.op("tensor", "matmul", ps[:, 5 * 512 + h:5 * 512 + h + 1],
             g.onesf32[0:1, :], thr[0:1, 8 + h:9 + h], start=True, stop=True,
             sem=(g.s_pe if h == 7 else None))
    p.wait("vector", g.s_pe, g.s_pe.n)
    dop("tensor_copy", thc[:], ps[:, 5 * 512:5 * 512 + 8])
    thc_lv = last
    mtile = g.a1_mt
    for ch in range(8):
        p.wait("gpsimd", g.s_dve, last)
        p.op("gpsimd", "iota", ntile[:], pattern=[[128, 4], [1, 512]],
             base=-384 + ch * 4 * 128, channel_multiplier=-1,
             allow_small_or_imprecise_dtypes=True, sem=g.s_gp)
        p.wait("vector", g.s_gp, g.s_gp.n)
        dop("tensor_single_scalar", mtile[:], ntile[:], 0.0, ALU.is_ge)
        dop("tensor_single_scalar", ntile[:], ntile[:], 0.0, ALU.max)
        ntl = last
        dsl = dec[:, ch * 2048:(ch + 1) * 2048]
        for h in range(8):
            p.wait("scalar", g.s_dve, ntl if h == 0 else last)
            alv = p.op("scalar", "activation", (dsl if h == 0 else ebuf[:]),
                       ntile[:], AF.Exp, scale=thc[:, h:h + 1], sem=g.s_act)
            if h > 0:
                p.wait("vector", g.s_act, alv)
                dop("tensor_tensor", dsl, dsl, ebuf[:], ALU.add)
        dop("scalar_tensor_tensor", dsl, dsl, 0.125, mtile[:], ALU.mult,
            ALU.mult)
    ngm = g.a1_ngm
    p.wait("gpsimd", g.s_act, g.s_act.n)
    p.wait("gpsimd", g.s_dve, g.s_dve.n)
    nmz = p.op("gpsimd", "memset", ngm[:], 0.0, sem=g.s_gp)
    p.wait("gpsimd", g.s_gp, nmz)
    for di in range(4):
        p.op("gpsimd", "affine_select", ngm[:, di * 512:(di + 1) * 512],
             ngm[:, di * 512:(di + 1) * 512], pattern=[[1, 512]],
             compare_op=ALU.is_ge, fill=-1e30, base=-128 * di,
             channel_multiplier=-1, sem=(g.s_gp if di == 3 else None))
    dec_lv = g.s_gp.n
    p.wait("tensor", g.s_dve, last)

    a_ready = max(g.a_lv)
    hspill = [0, 0]
    last_pj = 0
    ev_lv = {}
    for h in range(8):
        sw = g.s_wt[h % 2]
        p.wait("gpsimd", g.s_pe, last_pj)
        wl = 0
        for mat, dst in (("L1_attn_Wq", wq_), ("L1_attn_Wk", wk_),
                         ("L1_attn_Wv", wv_)):
            wl = p.dma("gpsimd", dst[:],
                       bass.AP(W[mat], h * 128,
                               [[1024, 128], [128 * 1024, 8], [1, 128]]), sw)
        p.wait("tensor", sw, wl)
        if h == 0:
            p.wait("tensor", g.s_dve, a_ready)
        p.wait("vector", g.s_st[(h + 1) % 2], hspill[(h + 1) % 2])
        mm_mx = dop("memset", mm8[:], -3e38)
        for qk in range(2):
            wsl = wq_ if qk == 0 else wk_
            buf = qh if qk == 0 else kh
            for j in range(NT5):
                b0 = bank(j % 2)
                if (qk, j) not in ((0, 0), (0, 1)):
                    prev_lv = ev_lv[(qk, j - 2) if j >= 2
                                    else (qk - 1, NT5 - 2 + j)]
                    p.wait("tensor", g.s_dve, prev_lv)
                elif h > 0:
                    p.wait("tensor", g.s_dve, ev_lv[(1, NT5 - 2 + j)])
                mmq = None
                for dc in range(NCK):
                    mmq = p.op("tensor", "matmul", b0,
                               wsl[:, dc * 128:(dc + 1) * 128],
                               a_reg[:, dc * T + j * 512:
                                     dc * T + (j + 1) * 512],
                               start=(dc == 0), stop=(dc == NCK - 1),
                               sem=(g.s_pe if dc == NCK - 1 else None))
                p.wait("vector", g.s_pe, mmq)
                dsl_ = buf[:, j * 512:(j + 1) * 512]
                ev_lv[(qk, j)] = dop("tensor_copy", dsl_, b0)
                p.wait("scalar", g.s_dve, ev_lv[(qk, j)])
                sq = p.op("scalar", "activation", sqb[:], dsl_, AF.Square,
                          sem=g.s_act)
                p.wait("tensor", g.s_act, sq)
                if not (h == 0 and qk == 0 and j == 0):
                    p.wait("tensor", g.s_dve, last)
                mmh = p.op("tensor", "matmul", bank(6)[0:1, :],
                           g.onescol_bf[:, 0:1], sqb[:],
                           start=True, stop=True, sem=g.s_pe)
                p.wait("vector", g.s_pe, mmh)
                dop("reduce_max", mm8[0:1, 15:16], bank(6)[0:1, :], axis=AX.X)
                dop("tensor_tensor", mm8[0:1, qk:qk + 1],
                    mm8[0:1, qk:qk + 1], mm8[0:1, 15:16], ALU.max)
        vp = {}
        for tt in range(NTT):
            bv_ = bank(2 + tt % 2)
            if tt >= 2:
                p.wait("tensor", g.s_dve, vp[tt - 2])
            mmv = None
            for dc in range(NCK):
                mmv = p.op("tensor", "matmul", bv_[:, 0:128],
                           a_reg[:, dc * T + tt * 128:dc * T + (tt + 1) * 128],
                           wv_[:, dc * 128:(dc + 1) * 128],
                           start=(dc == 0), stop=(dc == NCK - 1),
                           sem=(g.s_pe if dc == NCK - 1 else None))
            p.wait("vector", g.s_pe, mmv)
            vp[tt] = dop("tensor_copy", vh[:, tt * 128:(tt + 1) * 128],
                         bv_[:, 0:128])
        last_pj = g.s_pe.n
        p.wait("scalar", g.s_dve, last)
        alv = p.op("scalar", "activation", mm8[0:1, 8:10], mm8[0:1, 0:2],
                   AF.Ln, bias=g.epscol[0:1, 0:1], sem=g.s_act)
        p.wait("vector", g.s_act, alv)
        dop("tensor_tensor", mm8[0:1, 12:13], mm8[0:1, 8:9],
            mm8[0:1, 9:10], ALU.add)
        p.wait("scalar", g.s_dve, last)
        alv = p.op("scalar", "activation", mm8[0:1, 12:13], mm8[0:1, 12:13],
                   AF.Exp, scale=0.5, bias=g.negln128h[0:1, 0:1],
                   sem=g.s_act)
        p.wait("vector", g.s_act, alv)
        dop("tensor_scalar_mul", mm8[0:1, 14:15], mm8[0:1, 12:13], -1.0)
        dop("tensor_copy", g.a0_negbf[0:1, :], mm8[0:1, 14:15])
        p.wait("tensor", g.s_dve, last)
        bmm = p.op("tensor", "matmul", bank(6)[:, 1:2], g.ones_bf[0:1, 0:128],
                   g.a0_negbf[0:1, 0:1], start=True, stop=True, sem=g.s_pe)
        p.wait("vector", g.s_pe, bmm)
        dop("tensor_copy", negmball[:, h:h + 1], bank(6)[:, 1:2])
        p.wait("sync", g.s_dve, last)
        p.dma("sync", g.QS[h * 128:(h + 1) * 128, :], qh[:], g.s_st[h % 2])
        p.dma("sync", g.KS[h * 128:(h + 1) * 128, :], kh[:], g.s_st[h % 2])
        hspill[h % 2] = p.dma("sync", g.VS[h * 128:(h + 1) * 128, 0:4096],
                              vh[:], g.s_st[h % 2])
    spill_all = (g.s_st[0].n, g.s_st[1].n)

    exp_prev = 0
    av_done = 0
    ln_ev = {}
    yev = {}
    ld_lv = 0
    g.prev_sdtt = 0
    for qc in range(NT5):
        nkt = qc * 4 + 4
        for h in range(8):
            sem_ld = g.s_ld[h % 2]
            p.wait("sync", g.s_st[0], spill_all[0])
            p.wait("sync", g.s_st[1], spill_all[1])
            p.wait("sync", g.s_pe, av_done)
            p.dma("sync", ksl[:, 0:nkt * 128],
                  g.KS[h * 128:(h + 1) * 128, 0:nkt * 128], sem_ld)
            p.dma("sync", vsl[:, 0:nkt * 128],
                  g.VS[h * 128:(h + 1) * 128, 0:nkt * 128], sem_ld)
            ld_lv = p.dma("sync", qsl[:],
                          g.QS[h * 128:(h + 1) * 128,
                               qc * 512:(qc + 1) * 512], sem_ld)
            p.wait("tensor", sem_ld, ld_lv)
            if qc == 0 and h == 0:
                p.wait("tensor", g.s_gp, dec_lv)
            avb = bank(4)
            ell = bank(6)[0:1, :]
            sdtt = {}
            exp2 = {}
            av2 = {}
            npair = nkt // 2
            p.wait("tensor", g.s_dve, g.prev_sdtt)
            if h == 0 and qc > 0:
                p.wait("tensor", g.s_dve, yev[(qc - 1, NCK - 2)])
                p.wait("tensor", g.s_dve, yev[(qc - 1, NCK - 1)])

            def emit_av1(pi):
                p.wait("tensor", g.s_act, exp2[pi])
                pbsl = pbuf[:, (pi % 2) * 1024:(pi % 2) * 1024 + 1024]
                for u in range(2):
                    kt = 2 * pi + u
                    lastk = (kt == nkt - 1)
                    p.op("tensor", "matmul", avb,
                         vsl[:, kt * 128:(kt + 1) * 128],
                         pbsl[:, u * 512:(u + 1) * 512],
                         start=(kt == 0), stop=lastk)
                    av2[pi] = p.op("tensor", "matmul", ell,
                                   g.onescol_bf[:, 0:1],
                                   pbsl[:, u * 512:(u + 1) * 512],
                                   start=(kt == 0), stop=lastk,
                                   sem=(g.s_pe if u == 1 else None))

            for pi in range(npair):
                for u in range(2):
                    kt = 2 * pi + u
                    sb = bank(2 * (pi % 2) + u)
                    if pi >= 2 and u == 0:
                        p.wait("tensor", g.s_dve, sdtt[pi - 2])
                    smm = p.op("tensor", "matmul", sb,
                               ksl[:, kt * 128:(kt + 1) * 128], qsl[:],
                               start=True, stop=True,
                               sem=(g.s_pe if u == 1 else None))
                if pi >= 1:
                    emit_av1(pi - 1)
                p.wait("vector", g.s_pe, smm)
                if pi >= 2:
                    p.wait("vector", g.s_act, exp2[pi - 2])
                msk = None
                for u in range(2):
                    kt = 2 * pi + u
                    d = 4 * qc - kt + 3
                    sb = bank(2 * (pi % 2) + u)
                    sdh = sd[:, (pi % 2) * 1024 + u * 512:
                             (pi % 2) * 1024 + (u + 1) * 512]
                    dop("tensor_tensor", sdh, sb,
                        dec[:, d * 512:(d + 1) * 512], ALU.mult)
                    if kt >= 4 * qc:
                        di = kt - 4 * qc
                        p.wait("gpsimd", g.s_dve, last)
                        msk = p.op("gpsimd", "tensor_tensor", sdh, sdh,
                                   ngm[:, di * 512:(di + 1) * 512], ALU.add,
                                   sem=g.s_gp)
                sd_lv = last
                sdtt[pi] = sd_lv
                g.prev_sdtt = sd_lv
                p.wait("scalar", g.s_dve, sd_lv)
                if msk is not None:
                    p.wait("scalar", g.s_gp, msk)
                if pi >= 2:
                    p.wait("scalar", g.s_pe, av2[pi - 2])
                sdp = sd[:, (pi % 2) * 1024:(pi % 2) * 1024 + 1024]
                pbsl = pbuf[:, (pi % 2) * 1024:(pi % 2) * 1024 + 1024]
                exp2[pi] = p.op("scalar", "activation", pbsl, sdp, AF.Exp,
                                bias=negmball[:, h:h + 1],
                                scale=float(1.0 / math.sqrt(128.0)),
                                sem=g.s_act)
            emit_av1(npair - 1)
            av_done = av2[npair - 1]
            p.wait("vector", g.s_pe, av_done)
            dop("reciprocal", rlbf[0:1, :], ell, low_prec=True)
            p.wait("tensor", g.s_dve, last)
            bmm = p.op("tensor", "matmul", bank(7), g.ones_bf[0:1, 0:128],
                       rlbf[0:1, :], start=True, stop=True, sem=g.s_pe)
            p.wait("vector", g.s_pe, bmm)
            dop("tensor_copy", bcb[:], bank(7))
            onb_lv = dop("tensor_tensor", g.a1_onb[:], avb, bcb[:], ALU.mult)
            p.wait("scalar", g.s_dve, onb_lv)
            sq = p.op("scalar", "activation", sqb[:], g.a1_onb[:], AF.Square,
                      sem=g.s_act)
            p.wait("tensor", g.s_dve, onb_lv)
            p.wait("tensor", g.s_act, sq)
            s1m = p.op("tensor", "matmul", bank(6)[0:1, :],
                       g.onescol_bf[:, 0:1], g.a1_onb[:],
                       start=True, stop=True)
            s2m = p.op("tensor", "matmul", bank(5)[0:1, :],
                       g.onescol_bf[:, 0:1], sqb[:],
                       start=True, stop=True, sem=g.s_pe)
            p.wait("vector", g.s_pe, s2m)
            dop("tensor_scalar_mul", r1[0:1, :], bank(6)[0:1, :], 1.0 / 128)
            dop("tensor_scalar_mul", r2[0:1, :], bank(5)[0:1, :], 1.0 / 128)
            dop("tensor_tensor", g.a1_rl[0:1, :], r1[0:1, :], r1[0:1, :],
                ALU.mult)
            dop("tensor_tensor", r2[0:1, :], r2[0:1, :], g.a1_rl[0:1, :],
                ALU.subtract)
            p.wait("scalar", g.s_dve, last)
            alv = p.op("scalar", "activation", r2[0:1, :], r2[0:1, :],
                       AF.Ln, bias=g.epscol[0:1, 0:1], sem=g.s_act)
            p.wait("scalar", g.s_act, alv)
            alv = p.op("scalar", "activation", r2[0:1, :], r2[0:1, :],
                       AF.Exp, scale=-0.5, sem=g.s_act)
            p.wait("vector", g.s_act, alv)
            dop("tensor_copy", rrow[0:1, :], r2[0:1, :])
            dop("tensor_tensor", g.a1_rl[0:1, :], r1[0:1, :], r2[0:1, :],
                ALU.mult)
            dop("tensor_copy", mrow[0:1, :], g.a1_rl[0:1, :])
            mr_lv = last
            p.wait("tensor", g.s_dve, mr_lv)
            bm1 = p.op("tensor", "matmul", bank(7), g.ones_bf[0:1, 0:128],
                       rrow[0:1, :], start=True, stop=True, sem=g.s_pe)
            p.wait("vector", g.s_pe, bm1)
            dop("tensor_copy", bcb[:], bank(7))
            lnsl = lnb[:, h * 512:(h + 1) * 512]
            dop("tensor_tensor", lnsl, g.a1_onb[:], bcb[:], ALU.mult)
            p.wait("tensor", g.s_dve, last)
            bm2 = p.op("tensor", "matmul", bank(7), g.ones_bf[0:1, 0:128],
                       mrow[0:1, :], start=True, stop=True, sem=g.s_pe)
            p.wait("vector", g.s_pe, bm2)
            dop("tensor_copy", bcb[:], bank(7))
            ln_ev[(qc, h)] = dop("tensor_tensor", lnsl, lnsl, bcb[:],
                                 ALU.subtract)
        p.wait("tensor", g.s_dve, ln_ev[(qc, 7)])
        p.wait("tensor", g.s_dve, wo_lv)
        for oc in range(NCK):
            b = bank(2 + oc % 2)
            if oc >= 2:
                p.wait("tensor", g.s_dve, yev[(qc, oc - 2)])
            elif qc > 0:
                p.wait("tensor", g.s_dve, yev[(qc - 1, NCK - 2 + oc)])
            for h2 in range(8):
                p.op("tensor", "matmul", b,
                     wo[:, h2 * 1024 + oc * 128:h2 * 1024 + (oc + 1) * 128],
                     lnb[:, h2 * 512:(h2 + 1) * 512],
                     start=(h2 == 0), stop=False)
            mm = p.op("tensor", "matmul", b,
                      borow[0:1, oc * 128:(oc + 1) * 128],
                      g.ones_bf[0:1, 0:512], start=False, stop=True,
                      sem=g.s_pe)
            p.wait("vector", g.s_pe, mm)
            yev[(qc, oc)] = p.op(
                "vector", "tensor_copy",
                a_reg[:, oc * T + qc * 512:oc * T + (qc + 1) * 512], b,
                sem=g.s_dve)
    g.attn1_done = g.s_dve.n


def emit_ffn1(p, g, a_reg, y_reg):
    ps = g.ps
    bank = g.bank
    W = g.weights
    wt = W["L1_ffn_weight"]
    dis = W["L1_ffn_disha"]

    weff = g.f_weff
    wstg = g.f_wstg
    dish = g.f_dish

    p.wait("gpsimd", g.s_pe, g.s_pe.n)
    p.wait("gpsimd", g.s_dve, g.s_dve.n)
    p.wait("gpsimd", g.s_gp, g.s_gp.n)
    for _s in g.s_st + g.s_ld:
        p.wait("gpsimd", _s, _s.n)

    wl = 0
    for oc in range(NCK):
        wl = p.dma("gpsimd", wstg[:, oc * 1024:(oc + 1) * 1024],
                   bass.AP(wt, oc * 128 * 1024, [[1024, 128], [1, 1024]]),
                   g.s_wt[0])
    dl = p.dma("gpsimd", dish[0:64, :], bass.AP(dis, 0, [[1024, 64], [1, 1024]]),
               g.s_rw)
    dl = p.dma("gpsimd", dish[64:128, :], bass.AP(dis, 0, [[1024, 64], [1, 1024]]),
               g.s_rw)
    p.wait("vector", g.s_rw, dl)
    dsc = p.op("vector", "tensor_scalar_mul", dish[:], dish[:], 0.5,
               sem=g.s_dve)

    p.wait("tensor", g.s_wt[0], wl)
    p.wait("tensor", g.s_gp, g.const_lvl)
    ev = {}
    for dc in range(NCK):
        for oc in range(NCK):
            i = dc * NCK + oc
            tb = g.psbf[:, (7 if i % 2 == 0 else 5) * 1024:
                        (7 if i % 2 == 0 else 5) * 1024 + 128]
            if i >= 2:
                p.wait("tensor", g.s_dve, ev[i - 2])
            lv = p.op("tensor", "transpose", tb,
                      wstg[:, oc * 1024 + dc * 128:oc * 1024 + (dc + 1) * 128],
                      g.ident_bf[:], sem=g.s_pe)
            p.wait("vector", g.s_pe, lv)
            if i == 0:
                p.wait("vector", g.s_dve, dsc)
            ev[i] = p.op("vector", "scalar_tensor_tensor",
                         weff[:, dc * 1024 + oc * 128:dc * 1024 + (oc + 1) * 128],
                         tb, 0.5, dish[:, oc * 128:(oc + 1) * 128],
                         ALU.mult, ALU.add, sem=g.s_dve)
    weff_lv = ev[NCK * NCK - 2]
    weff_lv2 = ev[NCK * NCK - 1]

    a_ready = max(g.a_lv)
    p.wait("tensor", g.s_dve, max(weff_lv, weff_lv2))
    p.wait("tensor", g.s_dve, a_ready)
    yev = {}
    for tau in range(NT5):
        for oc in range(NCK):
            i = tau * NCK + oc
            b = bank(i % 4)
            if i >= 4:
                p.wait("tensor", g.s_dve, yev[i - 4])
            mm = None
            for dc in range(NCK):
                mm = p.op("tensor", "matmul", b,
                          weff[:, dc * 1024 + oc * 128:dc * 1024 + (oc + 1) * 128],
                          a_reg[:, dc * T + tau * 512:dc * T + (tau + 1) * 512],
                          start=(dc == 0), stop=(dc == NCK - 1),
                          sem=(g.s_pe if dc == NCK - 1 else None))
            p.wait("vector", g.s_pe, mm)
            yev[i] = p.op("vector", "tensor_copy",
                          y_reg[:, oc * T + tau * 512:oc * T + (tau + 1) * 512],
                          b, sem=g.s_dve)
    g.ffn_done = g.s_dve.n


SEQ_FULL = ["moe", "attn0", "moe", "ffn1", "attn1", "ffn1"]

WNAMES = [
    "L0_attn_Wq", "L0_attn_bq", "L0_attn_Wk", "L0_attn_bk", "L0_attn_Wv",
    "L0_attn_bv", "L0_attn_Wp", "L0_attn_bp", "L0_ffn_router_w", "L0_ffn_w1",
    "L0_ffn_w3", "L0_ffn_w2", "L1_attn_Wq", "L1_attn_Wk", "L1_attn_Wv",
    "L1_attn_Wo", "L1_attn_bo", "L1_attn_theta", "L1_attn_gamma",
    "L1_attn_beta", "L1_ffn_weight", "L1_ffn_disha",
]

WSHAPES = {
    "L0_attn_Wq": [D, D], "L0_attn_bq": [D], "L0_attn_Wk": [D, D],
    "L0_attn_bk": [D], "L0_attn_Wv": [D, D], "L0_attn_bv": [D],
    "L0_attn_Wp": [D, D], "L0_attn_bp": [D], "L0_ffn_router_w": [D, 8],
    "L0_ffn_w1": [8, D, 2048], "L0_ffn_w3": [8, D, 2048],
    "L0_ffn_w2": [8, 2048, D], "L1_attn_Wq": [D, D], "L1_attn_Wk": [D, D],
    "L1_attn_Wv": [D, D], "L1_attn_Wo": [D, D], "L1_attn_bo": [D],
    "L1_attn_theta": [8], "L1_attn_gamma": [D], "L1_attn_beta": [D],
    "L1_ffn_weight": [D, D], "L1_ffn_disha": [64, D],
}


def build_nc(stages, dbg=False):
    from contextlib import ExitStack
    nc = bass.Bass("TRN2", target_bir_lowering=False,
                   dynamic_dma_scratch_size=16384)
    g = St()
    g.weights = {}
    for n in WNAMES:
        g.weights[n] = nc.dram_tensor(n, WSHAPES[n], F32, kind="ExternalInput")
    g.x_in = nc.dram_tensor("x_in", [T, D], F32, kind="ExternalInput")
    g.x_out = nc.dram_tensor("x_out", [T, D], F32, kind="ExternalOutput")
    dkind = "ExternalOutput" if dbg else "Internal"
    g.Rbuf = nc.dram_tensor("Rbuf", [D, T], F32, kind=dkind)
    g.Ybuf = nc.dram_tensor("Ybuf", [D, T], F32, kind=dkind)
    g.QS = nc.dram_tensor("QS", [D, T], BF16, kind=dkind)
    g.KS = nc.dram_tensor("KS", [D, T], BF16, kind=dkind)
    g.VS = nc.dram_tensor("VS", [1024, 4160], BF16, kind=dkind)

    with ExitStack() as ctx:
        E = ctx.enter_context
        ra = Arena(nc, ctx, "regA", 65536)
        rb_ = Arena(nc, ctx, "regB", 65536)
        rw = Arena(nc, ctx, "regW", 72 * KB)
        g.REGA = ra.view([128, 32768], BF16, 0)
        g.REGB = rb_.view([128, 32768], BF16, 0)
        g.stag = rw.view([128, 2 * T], F32, 0)
        g.sqbuf = rw.view([128, T], BF16, 32 * KB)
        g.rstd_row = rw.view([1, T], BF16, 40 * KB)
        g.mr_row = rw.view([1, T], BF16, 48 * KB)
        g.rstd_b = rw.view([128, T], BF16, 56 * KB)
        g.mr_b = rw.view([128, T], BF16, 64 * KB)
        g.ystag = rw.view([128, T], F32, 40 * KB)
        g.ystag2 = rw.view([128, T], F32, 56 * KB)
        g.xstage = rw.view([128, 2 * T], F32, 32 * KB)
        g.sq_x = None
        g.sqbuf_x = rw.view([128, T], BF16, 64 * KB)
        g.rstd_row_x = rw.view([1, T], BF16, 32 * KB)
        g.mr_row_x = rw.view([1, T], BF16, 40 * KB)
        g.rstd_b_x = rw.view([128, T], BF16, 48 * KB)
        g.mr_b_x = rw.view([128, T], BF16, 56 * KB)
        g.m_w1 = ra.view([128, 16 * 1024], BF16, 0)
        g.m_w3 = ra.view([128, 16 * 1024], BF16, 32 * KB)
        g.m_gbuf = rb_.view([128, 16 * 512], BF16, 0)
        g.m_asc = rb_.view([128, 8 * 512], BF16, 16 * KB)
        g.m_asl = rb_.view([128, 2 * 8 * 512], BF16, 24 * KB)
        g.m_w2 = rw.view([128, 16 * 1024], BF16, 14 * KB)
        KBf = 1024
        g.a0_q = rw.view([128, T], BF16, 0)
        g.a0_k = rw.view([128, T], BF16, 8 * KBf)
        g.a0_v = rw.view([128, 32 * 130], BF16, 16 * KBf)
        g.a0_cos = rw.view([128, T], BF16, 24 * KBf + 512)
        g.a0_sin = rw.view([128, T], BF16, 32 * KBf + 512)
        g.a0_wq = rw.view([128, 1024], BF16, 40 * KBf + 512)
        g.a0_wqr = rw.view([128, 1024], BF16, 42 * KBf + 512)
        g.a0_wk = rw.view([128, 1024], BF16, 44 * KBf + 512)
        g.a0_wkr = rw.view([128, 1024], BF16, 46 * KBf + 512)
        g.a0_wv = rw.view([128, 1024], BF16, 48 * KBf + 512)
        g.a0_p = rw.view([128, 2048], BF16, 50 * KBf + 512)
        g.a0_t1 = rw.view([128, 512], F32, 54 * KBf + 512)
        g.a0_rows = rw.view([1, 5120], BF16, 56 * KBf + 512)
        g.a0_t2 = rw.view([128, 512], F32, 56 * KBf + 512)
        g.a0_ntile = rw.view([128, 512], F32, 58 * KBf + 512)
        g.a0_i32b = rw.view([128, 512], I32, 60 * KBf + 512)
        g.a0_tang = rw.view([128, 512], F32, 62 * KBf + 512)
        g.a0_rlbf = rw.view([128, 512], BF16, 66 * KBf + 512)
        g.a0_bcb = rw.view([128, 512], F32, 67 * KBf + 512)
        g.a0_tmpB = rw.view([128, 512], BF16, 69 * KBf + 512)
        g.a0_sq = rw.view([128, 512], BF16, 70 * KBf + 512)
        g.a0_invf = rw.view([1, 160], F32, 64 * KBf + 512)
        g.onesf32 = rw.view([1, 128], F32, 71 * KBf + 512)
        g.a0_qs1 = rw.view([128, T], BF16, 24 * KBf + 512)
        g.a0_ks1 = rw.view([128, T], BF16, 32 * KBf + 512)
        g.a0_vs1 = rw.view([128, 32 * 130], BF16, 40 * KBf + 512)
        g.a0_wpf = rw.view([128, 8192], BF16, 0)
        g.a0_bprow = rw.view([1, 1024], BF16, 16 * KBf)
        g.a1_decA = ra.view([128, 16384], F32, 0)
        g.a1_decB = rb_.view([128, 16384], F32, 0)
        g.a1_nt = rw.view([128, 2048], F32, 0)
        g.a1_sd = rw.view([128, 2048], F32, 0)
        g.a1_pb = rw.view([128, 2048], BF16, 8 * KBf)
        g.a1_ngm = rw.view([128, 2048], BF16, 12 * KBf)
        g.a1_onb = rw.view([128, 512], BF16, 0)
        g.a1_eb = rw.view([128, 2048], F32, 8 * KBf)
        g.a1_ln = rw.view([128, 4096], BF16, 60 * KBf + 512)
        g.a1_wo = rw.view([128, 8192], BF16, 16 * KBf)
        g.a1_grow = rw.view([1, 2048], BF16, 32 * KBf)
        g.a1_rl = rw.view([128, 512], F32, 32 * KBf)
        g.a1_bcb = rw.view([128, 512], F32, 34 * KBf)
        g.a1_borow = rw.view([1, 1024], BF16, 36 * KBf)
        g.a1_wq = rw.view([128, 1024], BF16, 38 * KBf + 512)
        g.a1_r1 = rw.view([1, 512], F32, 38 * KBf + 512)
        g.a1_wk = rw.view([128, 1024], BF16, 40 * KBf + 512)
        g.a1_r2 = rw.view([1, 512], F32, 40 * KBf + 512)
        g.a1_wv = rw.view([128, 1024], BF16, 42 * KBf + 512)
        g.a1_mrow = rw.view([1, 512], BF16, 42 * KBf + 512)
        g.a1_rrow = rw.view([1, 512], BF16, 43 * KBf + 512)
        g.a1_qh = rw.view([128, 4096], BF16, 44 * KBf + 512)
        g.a1_mt = rw.view([128, 2048], BF16, 44 * KBf + 512)
        g.a1_kh = rw.view([128, 4096], BF16, 52 * KBf + 512)
        g.a1_vh = rw.view([128, 4096], BF16, 60 * KBf + 512)
        g.a1_sq = rw.view([128, 512], BF16, 68 * KBf + 512)
        g.a1_qsl = rw.view([128, 512], BF16, 69 * KBf + 512)
        g.a1_rlbf = rw.view([128, 512], BF16, 70 * KBf + 512)
        g.f_weff = rw.view([128, 8 * 1024], BF16, 0)
        g.f_wstg = rw.view([128, 8 * 1024], BF16, 16 * KB)
        g.f_dish = rw.view([128, 1024], BF16, 32 * KB)
        g.m_cbb = rw.view([128, 2 * 512], BF16, 0)
        g.m_ystg = rw.view([128, 2 * 512], F32, 2 * KB)
        g.m_crow = rw.view([1, T], BF16, 6 * KB)
        g.Abuf = nc.dram_tensor("Abuf", [D, T], BF16, kind=dkind)
        g.ones_bf = E(nc.sbuf_tensor("ones_bf", [128, 512], BF16))
        g.onescol_bf = E(nc.sbuf_tensor("onescol_bf", [128, 1], BF16))
        g.epscol = E(nc.sbuf_tensor("epscol", [128, 1], F32))
        g.ident_bf = E(nc.sbuf_tensor("ident_bf", [128, 128], BF16))
        g.ident_f32 = E(nc.sbuf_tensor("ident_f32", [128, 128], F32))
        g.statsb = E(nc.sbuf_tensor("statsb", [128, 64], F32))
        g.stat_m = E(nc.sbuf_tensor("stat_m", [128, 32], F32))
        g.stat_v = E(nc.sbuf_tensor("stat_v", [128, 32], F32))
        g.stat_t = E(nc.sbuf_tensor("stat_t", [128, 32], F32))
        g.stat_pk = E(nc.sbuf_tensor("stat_pk", [128, 64], BF16))
        g.lg_sb = E(nc.sbuf_tensor("lg_sb", [128, 256], F32))
        g.r_tmp = E(nc.sbuf_tensor("r_tmp", [128, 256], F32))
        g.r_tmp2 = E(nc.sbuf_tensor("r_tmp2", [128, 256], F32))
        g.r_m1 = E(nc.sbuf_tensor("r_m1", [128, 32], F32))
        g.r_m2 = E(nc.sbuf_tensor("r_m2", [128, 32], F32))
        g.r_el = E(nc.sbuf_tensor("r_el", [128, 32], F32))
        g.c_tm = E(nc.sbuf_tensor("c_tm", [128, 256], BF16))
        g.rw_sb = E(nc.sbuf_tensor("rw_sb", [128, 64], BF16))
        g.a0_mm = E(nc.sbuf_tensor("a0_mm", [1, 16], F32))
        g.a0_negbf = E(nc.sbuf_tensor("a0_negbf", [1, 1], BF16))
        g.a0_invfc = E(nc.sbuf_tensor("a0_invfc", [128, 1], F32))
        g.a0_negmb = E(nc.sbuf_tensor("a0_negmb", [128, 1], F32))
        g.negln8 = E(nc.sbuf_tensor("negln8", [1, 1], F32))
        g.neg1e30 = E(nc.sbuf_tensor("neg1e30", [128, 1], F32))
        nc.const_aps.register(F32, -1e30, g.neg1e30[:, 0:1])
        g.negpi = E(nc.sbuf_tensor("negpi", [128, 1], F32))
        g.negln128h = E(nc.sbuf_tensor("negln128h", [1, 1], F32))
        g.a1_gcol = E(nc.sbuf_tensor("a1_gcol", [128, 8], F32))
        g.a1_bcol = E(nc.sbuf_tensor("a1_bcol", [128, 8], BF16))
        g.a1_thc = E(nc.sbuf_tensor("a1_thc", [128, 8], F32))
        g.a1_th = E(nc.sbuf_tensor("a1_th", [1, 16], F32))
        g.a1_negm8 = E(nc.sbuf_tensor("a1_negm8", [128, 8], F32))
        g.ps = E(nc.psum_tensor("ps", [128, 4096], F32))[:]
        g.psbf = g.ps.bitcast(BF16)
        g.bank = lambda b: g.ps[:, 512 * b:512 * (b + 1)]
        sems = {}
        for nm in ("pe", "act", "dve", "gp"):
            sems[nm] = Sem(E(nc.semaphore(f"s_{nm}")))
        g.s_pe, g.s_act, g.s_dve, g.s_gp = (sems[n] for n in
                                            ("pe", "act", "dve", "gp"))
        g.s_ld = [Sem(E(nc.semaphore(f"s_ld{i}"))) for i in range(2)]
        g.s_st = [Sem(E(nc.semaphore(f"s_st{i}"))) for i in range(2)]
        g.s_wt = [Sem(E(nc.semaphore(f"s_wt{i}"))) for i in range(2)]
        g.s_ya = [Sem(E(nc.semaphore(f"s_ya{i}"))) for i in range(2)]
        g.s_rw = Sem(E(nc.semaphore("s_rw")))
        g.s_w2 = Sem(E(nc.semaphore("s_w2")))
        g.xtp_lv = [0] * NCK

        block = E(nc.Block())
        p = Prog(nc)
        emit_const_setup(p, g)

        other = lambda r: g.REGB if r is g.REGA else g.REGA
        cur_a = g.REGA
        moe_kw = dict()
        if isinstance(stages, dict):
            moe_kw = stages.get("moe_kw", {})
            seq = stages["seq"]
        else:
            seq = stages
        emit_bridge(p, g, yspec=None, a_reg=cur_a, rin_x=True,
                    router=(seq and seq[0] == "moe"),
                    spill_A=(seq and seq[0] == "moe"))
        cur_y = None
        for i, body in enumerate(seq):
            nxt = seq[i + 1] if i + 1 < len(seq) else None
            if body == "moe":
                emit_moe(p, g, 0, True, **moe_kw)
                cur_y = ("hbm", g.Ybuf)
            elif body == "ffn1":
                yreg = other(cur_a)
                emit_ffn1(p, g, cur_a, yreg)
                cur_y = ("sbuf", yreg)
            elif body == "attn0":
                emit_attn0(p, g, cur_a)
                cur_y = ("sbuf", cur_a)
            elif body == "attn1":
                emit_attn1(p, g, cur_a, other(cur_a))
                cur_y = ("sbuf", cur_a)
            else:
                raise NotImplementedError(body)
            if nxt is None:
                emit_bridge(p, g, yspec=cur_y, a_reg=None, out_mode=True)
            else:
                a_new = cur_y[1] if cur_y[0] == "sbuf" else cur_a
                emit_bridge(p, g, yspec=cur_y, a_reg=a_new,
                            router=(nxt == "moe"), spill_A=(nxt == "moe"))
                cur_a = a_new
        if not seq:
            emit_bridge(p, g, yspec=None, a_reg=None, out_mode=True)

        for eng in ENGINES:
            for s in (g.s_st + g.s_ya + g.s_ld + g.s_wt):
                p.wait(eng, s, s.n)
        p.replay(block)
    return nc, g


def kernel(**inputs):
    x = np.ascontiguousarray(np.asarray(inputs["L0_attn_x"], dtype=np.float32))
    nc, g = build_nc(SEQ_FULL)
    wmap = {n: np.ascontiguousarray(np.asarray(inputs[n], dtype=np.float32))
            for n in WNAMES}
    in_maps = [dict(wmap, x_in=x[c]) for c in range(8)]
    res = run_bass_kernel_spmd(nc, in_maps, core_ids=list(range(8)))
    return np.stack([r["x_out"] for r in res.results], axis=0)


if __name__ == "__main__":
    pass


# version 08 (final)
# speedup vs baseline: 1.1286x; 1.0744x over previous
import math
import numpy as np
import ml_dtypes
import concourse.bass as bass
import concourse.mybir as mybir
from concourse.bass_utils import run_bass_kernel_spmd

F32 = mybir.dt.float32
BF16 = mybir.dt.bfloat16
I32 = mybir.dt.int32
U8 = mybir.dt.uint8
AF = mybir.ActivationFunctionType
ALU = mybir.AluOpType
AX = mybir.AxisListType

T = 4096
D = 1024
NCK = 8
NTT = 32
NT5 = 8
KB = 1024

ENGINES = ("tensor", "vector", "scalar", "gpsimd", "sync")


class Sem:
    __slots__ = ("h", "n")

    def __init__(self, h):
        self.h = h
        self.n = 0

    def inc(self, k=1):
        self.n += k
        return self.n


class Prog:
    def __init__(self, nc):
        self.nc = nc
        self.q = {e: [] for e in ENGINES}

    def wait(self, eng, sem, lvl):
        if lvl is None or lvl <= 0:
            return
        self.q[eng].append(lambda E, s=sem.h, v=lvl: E.wait_ge(s, v))

    def op(self, eng, meth, *args, sem=None, inc=1, low_prec=False, **kw):
        lvl = sem.inc(inc) if sem is not None else None
        nc = self.nc

        def fn(E, meth=meth, args=args, kw=kw, sem=sem, inc=inc):
            try:
                if low_prec:
                    with nc.allow_low_precision(reason="sanctioned bf16"):
                        inst = getattr(E, meth)(*args, **kw)
                else:
                    inst = getattr(E, meth)(*args, **kw)
            except Exception as e:
                raise RuntimeError(
                    f"op {meth} failed: {e} args0={str(args[0])[:160]}") from e
            if sem is not None:
                inst.then_inc(sem.h, inc)
        self.q[eng].append(fn)
        return lvl

    def dma(self, eng, out, in_, sem, **kw):
        return self.op(eng, "dma_start", sem=sem, inc=16, out=out, in_=in_, **kw)

    def replay(self, block):
        for eng in ENGINES:
            q = self.q[eng]

            def section(E, q=q):
                for fn in q:
                    fn(E)
            getattr(block, eng)(section)


class Arena:
    def __init__(self, nc, ctx, name, nbytes):
        self.nc = nc
        self.backing = ctx.enter_context(nc.sbuf_tensor(name, [128, nbytes], U8))
        self.base = nc.lookup_mloc(self.backing).addr
        self.name = name
        self.n = 0

    def view(self, shape, dtype, offset_bytes):
        name = f"{self.name}_v{self.n}"
        self.n += 1
        return self.nc.alloc_sbuf_tensor_at(
            name, list(shape), dtype, offset=self.base + offset_bytes)


class St:
    pass


def emit_const_setup(p, g):
    p.op("gpsimd", "memset", g.ones_bf[:], 1.0)
    p.op("gpsimd", "memset", g.onescol_bf[:], 1.0)
    p.op("gpsimd", "memset", g.epscol[:], 1e-5)
    p.op("gpsimd", "memset", g.negln8[:], -math.log(8.0))
    p.op("gpsimd", "memset", g.negpi[:], -math.pi)
    p.op("gpsimd", "memset", g.neg1e30[:], -1e30)
    p.op("gpsimd", "memset", g.negln128h[:], -0.5 * math.log(128.0))
    p.op("gpsimd", "memset", g.onesf32[:], 1.0)
    lv = p.op("gpsimd", "memset", g.ident_bf[:], 0.0, sem=g.s_gp)
    p.wait("gpsimd", g.s_gp, lv)
    p.op("gpsimd", "affine_select", g.ident_bf[:], g.ident_bf[:],
         pattern=[[-1, 128]], compare_op=ALU.not_equal, fill=1.0,
         base=0, channel_multiplier=1)
    lv = p.op("gpsimd", "memset", g.ident_f32[:], 0.0, sem=g.s_gp)
    p.wait("gpsimd", g.s_gp, lv)
    p.op("gpsimd", "affine_select", g.ident_f32[:], g.ident_f32[:],
         pattern=[[-1, 128]], compare_op=ALU.not_equal, fill=1.0,
         base=0, channel_multiplier=1, sem=g.s_gp)
    g.const_lvl = g.s_gp.n
    rwa = bass.AP(g.weights["L0_ffn_router_w"], 0,
                  [[8, 128], [128 * 8, 8], [1, 8]])
    g.rw_lvl = p.dma("gpsimd", g.rw_sb[:], rwa, g.s_rw)


def emit_bridge(p, g, *, yspec=None, a_reg=None, rin_x=False, out_mode=False,
                router=False, spill_A=False):
    if rin_x:
        sqbuf, rstd_row, mr_row, rstd_b, mr_b = (g.sqbuf_x, g.rstd_row_x,
                                                 g.mr_row_x, g.rstd_b_x,
                                                 g.mr_b_x)
    else:
        sqbuf, rstd_row, mr_row, rstd_b, mr_b = (g.sqbuf, g.rstd_row,
                                                 g.mr_row, g.rstd_b, g.mr_b)
    """Stream R: R += Y -> Rbuf; A = LN(R) -> a_reg; optional router logits.

    yspec: None | ("sbuf", bf16 [128x32768] handle) | ("hbm", f32 dram handle)
    """
    stag = g.stag
    ps = g.ps
    psbf = g.psbf
    bank = g.bank

    ld_seq = 0
    add_lv = [None] * NCK
    rb_lv = [None] * NCK
    ev_lv = {}
    st_lv = [None] * NCK
    rst_lv = [None] * NCK
    g.a_spill_lv = [0] * NCK
    stat_mm = [None] * NCK
    stat_ev = [None] * NCK
    stats_bank = bank(6)[:, 0:64]

    dve_entry = g.s_dve.n
    act_entry = g.s_act.n
    p.wait("sync", g.s_dve, dve_entry)
    p.wait("sync", g.s_gp, g.s_gp.n)
    for _s in g.s_st + g.s_ya:
        p.wait("sync", _s, _s.n)
    z_lv = p.op("vector", "memset", g.statsb[:], 0.0, sem=g.s_dve)

    for c in range(NCK):
        sl = stag[:, (c % 2) * T:(c % 2) * T + T]
        sem_ld = g.s_ld[c % 2]
        if out_mode:
            sl = stag[:, 0:T]
            if c >= 1:
                p.wait("sync", g.s_dve, ev_lv[(c - 1, NTT - 1)])
        elif c >= 2:
            p.wait("sync", g.s_dve, rb_lv[c - 2])
            p.wait("sync", g.s_st[c % 2], rst_lv[c - 2])
        if rin_x:
            xa = bass.AP(g.x_in, c * 128,
                         [[1024, 128], [128 * 1024, 32], [1, 128]])
            xsl = g.xstage[:, (c % 2) * T:(c % 2) * T + T]
            if c >= 2:
                p.wait("sync", g.s_pe, g.xtp_lv[c - 2])
            lv = p.dma("sync", xsl, xa, sem_ld)
            p.wait("tensor", sem_ld, lv)
            if c == 0:
                p.wait("tensor", g.s_gp, g.const_lvl)
            if c >= 2:
                p.wait("vector", g.s_st[c % 2], rst_lv[c - 2])
            for tt in range(NTT):
                tb = bank(7 if tt % 2 == 0 else 5)[:, 0:128]
                prev = (c, tt - 2) if tt >= 2 else \
                    ((c - 1, NTT - 2 + tt) if c >= 1 else None)
                if prev is not None:
                    p.wait("tensor", g.s_dve, ev_lv[prev])
                xlv = p.op("tensor", "transpose", tb,
                           xsl[:, tt * 128:(tt + 1) * 128], g.ident_f32[:],
                           sem=g.s_pe)
                p.wait("vector", g.s_pe, xlv)
                ev_lv[(c, tt)] = p.op(
                    "vector", "tensor_copy", sl[:, tt * 128:(tt + 1) * 128],
                    tb, sem=g.s_dve)
            g.xtp_lv[c] = xlv
        else:
            lv = p.dma("sync", sl, g.Rbuf[c * 128:(c + 1) * 128, :], sem_ld)
            if yspec is None or yspec[0] == "sbuf":
                p.wait("vector", sem_ld, lv)
            if out_mode and yspec is None:
                p.wait("tensor", sem_ld, lv)

        if yspec is not None:
            kind, yh = yspec
            if kind == "sbuf":
                ysl = yh[:, c * T:(c + 1) * T]
                add_lv[c] = p.op("vector", "tensor_tensor", sl, sl, ysl,
                                 ALU.add, sem=g.s_dve)
            elif out_mode:
                ysl = stag[:, T:2 * T]
                ylv = p.dma("sync", ysl,
                            bass.AP(yh, c * 128 * T, [[T, 128], [1, T]]),
                            sem_ld)
                p.wait("vector", sem_ld, ylv)
                add_lv[c] = p.op("vector", "tensor_tensor", sl, sl, ysl,
                                 ALU.add, sem=g.s_dve)
            else:
                ysl = g.ystag[:, (c % 2) * 2048:(c % 2) * 2048 + 2048]
                ylv = p.dma("sync", ysl,
                            bass.AP(yh, c * 128 * T, [[T, 128], [1, 2048]]),
                            sem_ld)
                ylv = p.dma("sync", g.ystag2[:, (c % 2) * 2048:(c % 2) * 2048 + 2048],
                            bass.AP(yh, c * 128 * T + 2048, [[T, 128], [1, 2048]]),
                            sem_ld)
                p.wait("vector", sem_ld, ylv)
                p.op("vector", "tensor_tensor", sl[:, 0:2048], sl[:, 0:2048],
                     ysl, ALU.add)
                add_lv[c] = p.op("vector", "tensor_tensor", sl[:, 2048:4096],
                                 sl[:, 2048:4096],
                                 g.ystag2[:, (c % 2) * 2048:(c % 2) * 2048 + 2048],
                                 ALU.add, sem=g.s_dve)

        if out_mode:
            if c >= 2:
                p.wait("vector", g.s_st[c % 2], st_lv[c - 2])
            for tt in range(NTT):
                tb = bank(7 if tt % 2 == 0 else 5)[:, 0:128]
                p.wait("tensor", g.s_dve, add_lv[c])
                if add_lv[c] is None and tt == 0 and c == 0:
                    p.wait("tensor", g.s_dve, dve_entry)
                    p.wait("tensor", g.s_act, act_entry)
                prev = (c, tt - 2) if tt >= 2 else \
                    ((c - 1, NTT - 2 + tt) if c >= 1 else None)
                if prev is not None:
                    p.wait("tensor", g.s_dve, ev_lv[prev])
                xlv = p.op("tensor", "transpose", tb,
                           sl[:, tt * 128:(tt + 1) * 128], g.ident_f32[:],
                           sem=g.s_pe)
                p.wait("vector", g.s_pe, xlv)
                ev_lv[(c, tt)] = p.op(
                    "vector", "tensor_copy",
                    g.xstage[:, (c % 2) * T + tt * 128:(c % 2) * T + (tt + 1) * 128],
                    tb, sem=g.s_dve)
            p.wait("sync", g.s_dve, ev_lv[(c, NTT - 1)])
            oa = bass.AP(g.x_out, c * 128,
                         [[1024, 128], [128 * 1024, 32], [1, 128]])
            st_lv[c] = p.dma("sync", oa,
                             g.xstage[:, (c % 2) * T:(c % 2) * T + T],
                             g.s_st[c % 2])
            continue

        if yspec is not None or rin_x:
            p.wait("sync", g.s_dve, add_lv[c] if yspec else ev_lv[(c, NTT - 1)])
            rst_lv[c] = p.dma("sync", g.Rbuf[c * 128:(c + 1) * 128, :], sl,
                              g.s_st[c % 2])

        rb = a_reg[:, c * T:(c + 1) * T]
        if yspec is not None:
            p.wait("vector", g.s_dve, add_lv[c])
        elif rin_x:
            p.wait("vector", g.s_dve, ev_lv[(c, NTT - 1)])
        rb_lv[c] = p.op("vector", "tensor_copy", rb, sl, sem=g.s_dve)
        p.wait("scalar", g.s_dve, rb_lv[c])
        if c >= 1:
            p.wait("scalar", g.s_pe, stat_mm[c - 1])
        sq_lv = p.op("scalar", "activation", sqbuf[:], rb, AF.Square,
                     sem=g.s_act)
        p.wait("tensor", g.s_dve, rb_lv[c])
        p.wait("tensor", g.s_act, sq_lv)
        if c == 0:
            p.wait("tensor", g.s_gp, g.const_lvl)
        else:
            p.wait("tensor", g.s_dve, stat_ev[c - 1])
        for tt in range(NTT):
            p.op("tensor", "matmul", stats_bank[:, 2 * tt:2 * tt + 1],
                 rb[:, tt * 128:(tt + 1) * 128], g.onescol_bf[:, 0:1],
                 start=True, stop=True)
            stat_mm[c] = p.op(
                "tensor", "matmul", stats_bank[:, 2 * tt + 1:2 * tt + 2],
                sqbuf[:, tt * 128:(tt + 1) * 128], g.onescol_bf[:, 0:1],
                start=True, stop=True,
                sem=(g.s_pe if tt == NTT - 1 else None))
        p.wait("vector", g.s_pe, stat_mm[c])
        p.wait("vector", g.s_dve, z_lv if c == 0 else stat_ev[c - 1])
        stat_ev[c] = p.op("vector", "tensor_tensor", g.statsb[:], g.statsb[:],
                          stats_bank, ALU.add, sem=g.s_dve)

    if out_mode:
        for c in range(NCK):
            p.wait("sync", g.s_st[c % 2], st_lv[c])
        return

    sb = g.statsb
    p.wait("vector", g.s_dve, stat_ev[NCK - 1])
    lv = p.op("vector", "tensor_scalar_mul", g.stat_m[:], sb[:, 0:64:2],
              1.0 / D, sem=g.s_dve)
    p.op("vector", "tensor_scalar_mul", g.stat_v[:], sb[:, 1:64:2], 1.0 / D)
    p.wait("vector", g.s_dve, lv)
    lv = p.op("vector", "tensor_tensor", g.stat_t[:], g.stat_m[:], g.stat_m[:],
              ALU.mult, sem=g.s_dve)
    p.wait("vector", g.s_dve, lv)
    lv = p.op("vector", "tensor_tensor", g.stat_v[:], g.stat_v[:], g.stat_t[:],
              ALU.subtract, sem=g.s_dve)
    p.wait("scalar", g.s_dve, lv)
    lv = p.op("scalar", "activation", g.stat_v[:], g.stat_v[:], AF.Ln,
              bias=g.epscol[:, 0:1], sem=g.s_act)
    p.wait("scalar", g.s_act, lv)
    lv = p.op("scalar", "activation", g.stat_v[:], g.stat_v[:], AF.Exp,
              scale=-0.5, sem=g.s_act)
    p.wait("vector", g.s_act, lv)
    lv = p.op("vector", "tensor_tensor", g.stat_t[:], g.stat_m[:], g.stat_v[:],
              ALU.mult, sem=g.s_dve)
    p.op("vector", "tensor_copy", g.stat_pk[:, 0:32], g.stat_v[:])
    p.wait("vector", g.s_dve, lv)
    lv = p.op("vector", "tensor_copy", g.stat_pk[:, 32:64], g.stat_t[:],
              sem=g.s_dve)
    p.wait("tensor", g.s_dve, lv)
    p.wait("tensor", g.s_dve, stat_ev[NCK - 1])
    for tt in range(NTT):
        p.op("tensor", "transpose", psbf[0:1, tt * 128:(tt + 1) * 128],
             g.stat_pk[:, tt:tt + 1], g.ident_bf[:])
    for tt in range(NTT):
        p.op("tensor", "transpose",
             psbf[0:1, 4096 + tt * 128:4096 + (tt + 1) * 128],
             g.stat_pk[:, 32 + tt:32 + tt + 1], g.ident_bf[:],
             sem=(g.s_pe if tt == NTT - 1 else None))
    p.wait("vector", g.s_pe, g.s_pe.n)
    p.op("vector", "tensor_copy", rstd_row[:], psbf[0:1, 0:4096])
    lv = p.op("vector", "tensor_copy", mr_row[:], psbf[0:1, 4096:8192],
              sem=g.s_dve)
    p.wait("tensor", g.s_dve, lv)
    bc_lv = {}
    for j in range(2 * NT5):
        row = rstd_row if j < NT5 else mr_row
        dstt = rstd_b if j < NT5 else mr_b
        jj = j % NT5
        b = bank(6 + (j % 2))
        if j >= 2:
            p.wait("tensor", g.s_dve, bc_lv[j - 2])
        p.op("tensor", "matmul", b, g.ones_bf[0:1, 0:128],
             row[:, jj * 512:(jj + 1) * 512], start=True, stop=True, sem=g.s_pe)
        p.wait("vector", g.s_pe, g.s_pe.n)
        bc_lv[j] = p.op("vector", "tensor_copy",
                        dstt[:, jj * 512:(jj + 1) * 512], b, sem=g.s_dve)

    if router:
        lgb = bank(4)[:, 0:256]
        zl = p.op("vector", "memset", g.lg_sb[:], 0.0, sem=g.s_dve)
        lg_mm = None
        lg_ev = None
    ap_lv = [None] * NCK
    for c in range(NCK):
        rb = a_reg[:, c * T:(c + 1) * T]
        if c == 0:
            p.wait("vector", g.s_dve, bc_lv[2 * NT5 - 1])
        lv = p.op("vector", "tensor_tensor", rb, rb, rstd_b[:], ALU.mult,
                  sem=g.s_dve)
        p.wait("vector", g.s_dve, lv)
        ap_lv[c] = p.op("vector", "tensor_tensor", rb, rb, mr_b[:],
                        ALU.subtract, sem=g.s_dve)
        if spill_A:
            p.wait("sync", g.s_dve, ap_lv[c])
            g.a_spill_lv[c] = p.dma("sync", g.Abuf[c * 128:(c + 1) * 128, :],
                                    rb, g.s_st[c % 2])
        if router:
            p.wait("tensor", g.s_dve, ap_lv[c])
            if c == 0:
                p.wait("tensor", g.s_rw, g.rw_lvl)
            else:
                p.wait("tensor", g.s_dve, lg_ev)
            for tt in range(NTT):
                lg_mm = p.op("tensor", "matmul", lgb[:, tt * 8:(tt + 1) * 8],
                             rb[:, tt * 128:(tt + 1) * 128],
                             g.rw_sb[:, c * 8:(c + 1) * 8],
                             start=True, stop=True,
                             sem=(g.s_pe if tt == NTT - 1 else None))
            p.wait("vector", g.s_pe, lg_mm)
            p.wait("vector", g.s_dve, zl if c == 0 else lg_ev)
            lg_ev = p.op("vector", "tensor_tensor", g.lg_sb[:], g.lg_sb[:],
                         lgb, ALU.add, sem=g.s_dve)
    g.a_lv = list(ap_lv)
    g.a_spilled = spill_A
    if router:
        emit_router_combine(p, g)


def emit_router_combine(p, g):
    L = g.lg_sb
    last = g.s_dve.n

    def dop(meth, *a, **k):
        nonlocal last
        p.wait("vector", g.s_dve, last)
        last = p.op("vector", meth, *a, sem=g.s_dve, **k)
        return last

    for tt in range(NTT):
        l8 = L[:, tt * 8:(tt + 1) * 8]
        t8 = g.r_tmp[:, tt * 8:(tt + 1) * 8]
        w8 = g.r_tmp2[:, tt * 8:(tt + 1) * 8]
        nm1 = g.r_m1[:, tt:tt + 1]
        nm2 = g.r_m2[:, tt:tt + 1]
        el = g.r_el[:, tt:tt + 1]
        dop("reduce_max", nm1, l8, axis=AX.X, negate=True)
        dop("tensor_scalar", t8, l8, nm1, None, ALU.add)
        dop("tensor_single_scalar", w8, t8, 0.0, ALU.is_ge)
        dop("scalar_tensor_tensor", w8, w8, -1e30, l8, ALU.mult, ALU.add)
        dop("reduce_max", nm2, w8, axis=AX.X, negate=True)
        lv = dop("tensor_scalar", w8, l8, nm2, None, ALU.add)
        p.wait("scalar", g.s_dve, lv)
        elv = p.op("scalar", "activation", t8, t8, AF.Exp, sem=g.s_act)
        dop("tensor_single_scalar", w8, w8, 0.0, ALU.is_ge)
        p.wait("vector", g.s_act, elv)
        dop("tensor_tensor", w8, w8, t8, ALU.mult)
        dop("reduce_sum", el, w8, axis=AX.X)
        dop("reciprocal", el, el)
        dop("tensor_scalar", g.c_tm[:, tt * 8:(tt + 1) * 8], w8, el, 0.5,
            ALU.mult, ALU.mult)
    g.ctm_lv = g.s_dve.n


def emit_moe(p, g, L, first_call, n_experts=8, n_tau=NT5, prefetch=True, no_accum=False):
    ps = g.ps
    psbf = g.psbf
    bank = g.bank
    W = g.weights
    w1 = W["L0_ffn_w1"]
    w3 = W["L0_ffn_w3"]
    w2 = W["L0_ffn_w2"]

    gbuf = g.m_gbuf
    asc = g.m_asc
    asl = g.m_asl
    w1sb = g.m_w1
    w3sb = g.m_w3
    w2sb = g.m_w2
    cbb = g.m_cbb
    ystg = g.m_ystg
    crow = g.m_crow

    assert g.a_spilled, "MoE requires A spilled to HBM"
    p.wait("vector", g.s_gp, g.s_gp.n)
    p.wait("gpsimd", g.s_gp, g.s_gp.n)
    for _s in g.s_st + g.s_ld:
        p.wait("vector", _s, _s.n)
        p.wait("gpsimd", _s, _s.n)

    def w1ap(e, fc):
        return bass.AP(w1, e * 1024 * 2048 + fc * 128,
                       [[2048, 128], [128 * 2048, 8], [1, 128]])

    def w3ap(e, fc):
        return bass.AP(w3, e * 1024 * 2048 + fc * 128,
                       [[2048, 128], [128 * 2048, 8], [1, 128]])

    def w2ap(e, fc):
        return bass.AP(w2, e * 2048 * 1024 + fc * 128 * 1024,
                       [[1024, 128], [1, 1024]])

    p.wait("gpsimd", g.s_pe, g.s_pe.n)
    p.wait("gpsimd", g.s_dve, g.s_dve.n)
    w13_lv = 0
    w2_lv = 0
    for fc in range(16):
        p.dma("gpsimd", w1sb[:, fc * 1024:(fc + 1) * 1024], w1ap(0, fc),
              g.s_wt[0])
        w13_lv = p.dma("gpsimd", w3sb[:, fc * 1024:(fc + 1) * 1024],
                       w3ap(0, fc), g.s_wt[0])
        w2_lv = p.dma("gpsimd", w2sb[:, fc * 1024:(fc + 1) * 1024],
                      w2ap(0, fc), g.s_w2)
    w13_done = [w13_lv] + [None] * 8
    w2_done = [w2_lv] + [None] * 8

    cb_lv = [0, 0]
    ystg_dma = [0, 0, 0, 0]
    yev = [0] * 8
    sil_lv = {}
    g1_lv = {}
    h3_last = 0
    h3_tau2 = [0, 0]
    asl_lv2 = [0, 0]
    asc_lv2 = [0, 0]
    out_fc_lv = {}

    for e in range(n_experts):
        if not prefetch and e > 0:
            p.wait("gpsimd", g.s_pe, g.s_pe.n)
            p.wait("gpsimd", g.s_dve, g.s_dve.n)
            for fc in range(16):
                p.dma("gpsimd", w1sb[:, fc * 1024:(fc + 1) * 1024],
                      w1ap(e, fc), g.s_wt[e % 2])
                w13_done[e] = p.dma("gpsimd",
                                    w3sb[:, fc * 1024:(fc + 1) * 1024],
                                    w3ap(e, fc), g.s_wt[e % 2])
                w2_done[e] = p.dma("gpsimd", w2sb[:, fc * 1024:(fc + 1) * 1024],
                                   w2ap(e, fc), g.s_w2)
        p.wait("tensor", g.s_dve, g.ctm_lv)
        for dc in range(4):
            p.wait("tensor", g.s_dve, yev[dc])
        for tt in range(NTT):
            p.op("tensor", "transpose", psbf[0:1, tt * 128:(tt + 1) * 128],
                 g.c_tm[:, tt * 8 + e:tt * 8 + e + 1], g.ident_bf[:],
                 sem=(g.s_pe if tt == NTT - 1 else None))
        p.wait("vector", g.s_pe, g.s_pe.n)
        crow_lv = p.op("vector", "tensor_copy", crow[:], psbf[0:1, 0:4096],
                       sem=g.s_dve)

        def prep(tau):
            par = tau % 2
            p.wait("sync", g.s_pe, h3_tau2[par])
            aap = bass.AP(g.Abuf, tau * 512,
                          [[4096, 128], [128 * 4096, 8], [1, 512]])
            if e == 0 and tau == 0:
                p.wait("sync", g.s_st[0], g.a_spill_lv[NCK - 2])
                p.wait("sync", g.s_st[1], g.a_spill_lv[NCK - 1])
            aslv = asl[:, par * 4096:par * 4096 + 4096]
            asl_lv2[par] = p.dma("sync", aslv, aap, g.s_ld[par])
            p.wait("tensor", g.s_dve, crow_lv)
            p.wait("tensor", g.s_dve, yev[5])
            p.wait("tensor", g.s_dve, cb_lv[par])
            b5 = bank(5)
            lv = p.op("tensor", "matmul", b5, g.ones_bf[0:1, 0:128],
                      crow[:, tau * 512:(tau + 1) * 512], start=True,
                      stop=True, sem=g.s_pe)
            p.wait("vector", g.s_pe, lv)
            cbt = cbb[:, par * 512:(par + 1) * 512]
            cb_lv[par] = p.op("vector", "tensor_copy", cbt, b5, sem=g.s_dve)
            p.wait("vector", g.s_dve, cb_lv[par])
            p.wait("vector", g.s_ld[par], asl_lv2[par])
            p.wait("vector", g.s_pe, h3_tau2[par])
            for dc in range(NCK):
                asc_lv2[par] = p.op(
                    "vector", "tensor_tensor",
                    asc[:, par * 4096 + dc * 512:
                        par * 4096 + (dc + 1) * 512],
                    aslv[:, dc * 512:(dc + 1) * 512], cbt,
                    ALU.mult, sem=g.s_dve)

        prep(0)
        for tau in range(n_tau):
            par = tau % 2
            aslv = asl[:, par * 4096:par * 4096 + 4096]
            ascv = asc[:, par * 4096:par * 4096 + 4096]
            if tau == 0:
                p.wait("tensor", g.s_wt[e % 2], w13_done[e])
            for fc in range(16):
                fp = fc % 2
                bh1 = bank(fp)
                bh3 = bank(2 + fp)
                key = fc - 2 if fc >= 2 else 14 + fc
                if (e + tau > 0) or fc >= 2:
                    p.wait("tensor", g.s_act, sil_lv.get(key, 0))
                    p.wait("tensor", g.s_dve, g1_lv.get(key, 0))
                if fc < 2:
                    p.wait("tensor", g.s_dve, yev[fp])
                    p.wait("tensor", g.s_dve, yev[2 + fp])
                if e > 0 and tau == 0 and fc < 2:
                    p.wait("tensor", g.s_dve, crow_lv)
                if fc == 0:
                    p.wait("tensor", g.s_ld[par], asl_lv2[par])
                h1_lv = None
                for dc in range(NCK):
                    h1_lv = p.op("tensor", "matmul", bh1,
                                 w1sb[:, fc * 1024 + dc * 128:
                                      fc * 1024 + (dc + 1) * 128],
                                 aslv[:, dc * 512:(dc + 1) * 512],
                                 start=(dc == 0), stop=(dc == NCK - 1),
                                 sem=(g.s_pe if dc == NCK - 1 else None))
                if fc == 0:
                    p.wait("tensor", g.s_dve, asc_lv2[par])
                h3_lv = None
                for dc in range(NCK):
                    h3_lv = p.op("tensor", "matmul", bh3,
                                 w3sb[:, fc * 1024 + dc * 128:
                                      fc * 1024 + (dc + 1) * 128],
                                 ascv[:, dc * 512:(dc + 1) * 512],
                                 start=(dc == 0), stop=(dc == NCK - 1),
                                 sem=(g.s_pe if dc == NCK - 1 else None))
                h3_last = h3_lv
                h3_tau2[par] = h3_lv
                if tau == n_tau - 1:
                    h13_final = h3_lv
                    if prefetch and e < n_experts - 1:
                        p.wait("gpsimd", g.s_pe, h3_lv)
                        p.dma("gpsimd", w1sb[:, fc * 1024:(fc + 1) * 1024],
                              w1ap(e + 1, fc), g.s_wt[(e + 1) % 2])
                        lvw = p.dma("gpsimd",
                                    w3sb[:, fc * 1024:(fc + 1) * 1024],
                                    w3ap(e + 1, fc), g.s_wt[(e + 1) % 2])
                        if fc == 15:
                            w13_done[e + 1] = lvw
                p.wait("scalar", g.s_pe, h1_lv)
                gslice = gbuf[:, fc * 512:(fc + 1) * 512]
                sil_lv[fc] = p.op("scalar", "activation", gslice, bh1, AF.Silu,
                                  sem=g.s_act)
                p.wait("vector", g.s_act, sil_lv[fc])
                p.wait("vector", g.s_pe, h3_lv)
                g1_lv[fc] = p.op("vector", "tensor_tensor", gslice, gslice,
                                 bh3, ALU.mult, sem=g.s_dve)
            if tau + 1 < n_tau:
                prep(tau + 1)
                p.wait("tensor", g.s_dve, cb_lv[(tau + 1) % 2])
            if tau == 0:
                p.wait("tensor", g.s_w2, w2_done[e])
            out_mm = [None] * 8
            last_tau = (tau == n_tau - 1)
            for fc in range(16):
                p.wait("tensor", g.s_dve, g1_lv[fc])
                for dc in (4, 5, 6, 7, 0, 1, 2, 3):
                    if fc == 0:
                        p.wait("tensor", g.s_dve, yev[dc])
                        if dc == 0:
                            p.wait("tensor", g.s_act, sil_lv[15])
                            p.wait("tensor", g.s_dve, g1_lv[15])
                    want_sem = (fc == 15) or (last_tau and dc == 3)
                    out_mm[dc] = p.op(
                        "tensor", "matmul", bank(dc),
                        w2sb[:, fc * 1024 + dc * 128:fc * 1024 + (dc + 1) * 128],
                        gbuf[:, fc * 512:(fc + 1) * 512],
                        start=(fc == 0), stop=(fc == 15),
                        sem=(g.s_pe if want_sem else None))
                if last_tau:
                    out_fc_lv[fc] = out_mm[3]
            if last_tau and prefetch and e < n_experts - 1:
                for fc in range(16):
                    p.wait("gpsimd", g.s_pe, out_fc_lv[fc])
                    lvw = p.dma("gpsimd", w2sb[:, fc * 1024:(fc + 1) * 1024],
                                w2ap(e + 1, fc), g.s_w2)
                    if fc == 15:
                        w2_done[e + 1] = lvw
            for ei, dc in enumerate((5, 0, 2, 1, 3, 4, 6, 7)):
                p.wait("vector", g.s_pe, out_mm[dc])
                sp = ei % 4
                p.wait("vector", g.s_ya[sp], ystg_dma[sp])
                yev[dc] = p.op("vector", "tensor_copy",
                               ystg[:, sp * 512:(sp + 1) * 512], bank(dc),
                               sem=g.s_dve)
                p.wait("gpsimd", g.s_dve, yev[dc])
                ya = bass.AP(g.Ybuf, dc * 128 * T + tau * 512,
                             [[T, 128], [1, 512]])
                kw = {} if (e == 0 or no_accum) else {"accum_op": ALU.add}
                ystg_dma[sp] = p.dma("gpsimd", ya,
                                     ystg[:, sp * 512:(sp + 1) * 512],
                                     g.s_ya[sp], **kw)
    g.moe_done = True


def emit_attn0(p, g, a_reg):
    ps = g.ps
    bank = g.bank
    W = g.weights
    oreg = g.REGB if a_reg is g.REGA else g.REGA

    qb, kb, vb = g.a0_q, g.a0_k, g.a0_v
    cosb, sinb = g.a0_cos, g.a0_sin
    wqs, wqr, wks, wkr, wvs = g.a0_wq, g.a0_wqr, g.a0_wk, g.a0_wkr, g.a0_wv
    pb = g.a0_p
    t1 = g.a0_t1
    rl = g.a0_t1
    t2 = g.a0_t2
    rlbf, bcb, tmpB = g.a0_rlbf, g.a0_bcb, g.a0_tmpB
    rows = g.a0_rows
    mm8 = g.a0_mm
    negmb = g.a0_negmb
    sqb = g.a0_sq
    invf = g.a0_invf

    last = g.s_dve.n

    def dop(meth, *a, **k):
        nonlocal last
        p.wait("vector", g.s_dve, last)
        last = p.op("vector", meth, *a, sem=g.s_dve, **k)
        return last

    for eng in ("gpsimd", "tensor"):
        p.wait(eng, g.s_pe, g.s_pe.n)
        p.wait(eng, g.s_dve, g.s_dve.n)
        p.wait(eng, g.s_gp, g.s_gp.n)
        for _s in g.s_st + g.s_ld:
            p.wait(eng, _s, _s.n)
    p.wait("vector", g.s_ld[0], g.s_ld[0].n)
    p.wait("vector", g.s_ld[1], g.s_ld[1].n)
    p.wait("vector", g.s_gp, g.s_gp.n)

    i32 = g.a0_i32b
    ntile = g.a0_ntile
    tang = g.a0_tang
    p.op("gpsimd", "iota", invf[0:1, 0:32], pattern=[[1, 32]], base=0,
         channel_multiplier=0, allow_small_or_imprecise_dtypes=True,
         sem=g.s_gp)
    iota_lv = g.s_gp.n
    p.op("gpsimd", "memset", vb[:], 1.0, sem=g.s_gp)
    vb1_lv = g.s_gp.n
    p.wait("vector", g.s_gp, iota_lv)
    lnf = math.log(10000.0) * 2.0 / 64.0
    dop("tensor_scalar_mul", invf[0:1, 0:32], invf[0:1, 0:32], -lnf)
    p.wait("scalar", g.s_dve, last)
    alv = p.op("scalar", "activation", invf[0:1, 0:32], invf[0:1, 0:32],
               AF.Exp, sem=g.s_act)
    p.wait("vector", g.s_act, alv)
    for off in (32, 33, 96, 97):
        dop("tensor_copy", invf[0:1, off:min(off + 64, 160):2],
            invf[0:1, 0:32])
    invf_lv = last
    p.wait("tensor", g.s_dve, invf_lv)
    tl = p.op("tensor", "transpose", g.ps[:, 0:1],
              invf[0:1, 32:160], g.ident_f32[0:1, 0:1], sem=g.s_pe)
    p.wait("vector", g.s_pe, tl)
    invfc_lv = dop("tensor_copy", g.a0_invfc[:], g.ps[:, 0:1])
    for j in range(NT5):
        p.wait("gpsimd", g.s_dve, last)
        p.op("gpsimd", "iota", ntile[:], pattern=[[1, 512]], base=j * 512,
             channel_multiplier=0, allow_small_or_imprecise_dtypes=True,
             sem=g.s_gp)
        p.wait("vector", g.s_gp, g.s_gp.n)
        dop("tensor_scalar_mul", tang[:], ntile[:], g.a0_invfc[:, 0:1])
        for cw in range(2):
            dst = (sinb if cw == 0 else cosb)[:, j * 512:(j + 1) * 512]
            dop("tensor_scalar", t1[:], tang[:], 1.0 / (2 * math.pi),
                0.25 * cw, ALU.mult, ALU.add)
            dop("tensor_copy", i32[:], t1[:])
            dop("tensor_copy", t2[:], i32[:])
            dop("tensor_tensor", t2[:], t2[:], t1[:], ALU.is_gt)
            dop("tensor_copy", t1[:], i32[:])
            dop("tensor_tensor", t1[:], t1[:], t2[:], ALU.subtract)
            if cw == 0:
                dop("scalar_tensor_tensor", t1[:], t1[:], -2 * math.pi,
                    tang[:], ALU.mult, ALU.add)
            else:
                dop("tensor_scalar", t2[:], tang[:], 1.0, math.pi / 2,
                    ALU.mult, ALU.add)
                dop("scalar_tensor_tensor", t1[:], t1[:], -2 * math.pi,
                    t2[:], ALU.mult, ALU.add)
            p.wait("scalar", g.s_dve, last)
            a1 = p.op("scalar", "activation", dst, t1[:], AF.Sin,
                      bias=g.negpi[:, 0:1], sem=g.s_act)
            p.wait("vector", g.s_act, a1)
            dop("tensor_scalar_mul", dst, dst, -1.0)
        cs_act = g.s_act.n

    p.wait("gpsimd", g.s_dve, last)
    bl = 0
    for i, nm in enumerate(["L0_attn_bq", "L0_attn_bk", "L0_attn_bv"]):
        off = (0, 2048, 4096)[i]
        bl = p.dma("gpsimd", rows[0:1, off:off + 1024],
                   bass.AP(W[nm], 0, [[1024, 1], [1, 1024]]), g.s_rw)
    p.wait("vector", g.s_rw, bl)
    for src_off, dst_off in ((0, 1024), (2048, 3072)):
        dop("tensor_scalar_mul", rows[0:1, dst_off:dst_off + 1024:2],
            rows[0:1, src_off + 1:src_off + 1024:2], -1.0)
        dop("tensor_copy", rows[0:1, dst_off + 1:dst_off + 1024:2],
            rows[0:1, src_off:src_off + 1024:2])
    rotb_lv = last

    a_ready = max(g.a_lv)
    negm8p = g.a1_negm8
    last_proj_mm = 0
    pspill = [0, 0]
    for pr in range(8):
        sw = g.s_wt[pr % 2]
        p.wait("gpsimd", g.s_pe, last_proj_mm)
        wl = 0
        for mat, dst in (("L0_attn_Wq", wqs), ("L0_attn_Wk", wks),
                         ("L0_attn_Wv", wvs)):
            wl = p.dma("gpsimd", dst[:],
                       bass.AP(W[mat], pr * 128,
                               [[1024, 128], [128 * 1024, 8], [1, 128]]), sw)
        p.wait("vector", sw, wl)
        p.wait("vector", g.s_pe, last_proj_mm)
        p.wait("vector", g.s_st[(pr + 1) % 2], pspill[(pr + 1) % 2])
        for src, dst in ((wqs, wqr), (wks, wkr)):
            dop("tensor_scalar_mul", dst[:, 0:1024:2], src[:, 1:1024:2], -1.0)
            dop("tensor_copy", dst[:, 1:1024:2], src[:, 0:1024:2])
        wrot_lv = last
        mm_mx = dop("memset", mm8[:], -3e38)

        p.wait("tensor", g.s_dve, wrot_lv)
        if pr == 0:
            p.wait("tensor", g.s_dve, a_ready)
            p.wait("tensor", g.s_act, cs_act)
            p.wait("tensor", g.s_dve, rotb_lv)
        rope_ev = {}
        if pr == 0:
            red_ev = {}
        for qk in range(2):
            wsl, wrl, dstbuf, boff = ((wqs, wqr, qb, 0) if qk == 0
                                      else (wks, wkr, kb, 2048))
            for j in range(NT5):
                b0 = bank(j % 2)
                b1 = bank(2 + j % 2)
                if (qk, j) not in ((0, 0), (0, 1)):
                    prev = (qk, j - 2) if j >= 2 else (qk - 1, NT5 - 2 + j)
                    p.wait("tensor", g.s_dve, rope_ev[prev])
                for dc in range(NCK):
                    p.op("tensor", "matmul", b0,
                         wsl[:, dc * 128:(dc + 1) * 128],
                         a_reg[:, dc * T + j * 512:dc * T + (j + 1) * 512],
                         start=(dc == 0), stop=False)
                p.op("tensor", "matmul", b0,
                     rows[0:1, boff + pr * 128:boff + (pr + 1) * 128],
                     g.ones_bf[0:1, 0:512], start=False, stop=True)
                for dc in range(NCK):
                    p.op("tensor", "matmul", b1,
                         wrl[:, dc * 128:(dc + 1) * 128],
                         a_reg[:, dc * T + j * 512:dc * T + (j + 1) * 512],
                         start=(dc == 0), stop=False)
                pj = p.op("tensor", "matmul", b1,
                          rows[0:1, boff + 1024 + pr * 128:
                               boff + 1024 + (pr + 1) * 128],
                          g.ones_bf[0:1, 0:512], start=False, stop=True,
                          sem=g.s_pe)
                p.wait("vector", g.s_pe, pj)
                dsl = dstbuf[:, j * 512:(j + 1) * 512]
                dop("tensor_tensor", t1[:], b0,
                    cosb[:, j * 512:(j + 1) * 512], ALU.mult)
                dop("tensor_tensor", dsl, b1,
                    sinb[:, j * 512:(j + 1) * 512], ALU.mult)
                rope_ev[(qk, j)] = dop("tensor_tensor", dsl, dsl, t1[:],
                                       ALU.add)
                p.wait("scalar", g.s_dve, rope_ev[(qk, j)])
                sq = p.op("scalar", "activation", sqb[:], dsl, AF.Square,
                          sem=g.s_act)
                for h in range(2):
                    si = qk * 16 + j * 2 + h
                    bsq = bank(6 + si % 2)
                    p.wait("tensor", g.s_act, sq)
                    if si >= 2:
                        p.wait("tensor", g.s_dve, red_ev[si - 2])
                    elif pr > 0:
                        p.wait("tensor", g.s_dve, red_ev[30 + si])
                    mmh = p.op("tensor", "matmul", bsq[0:1, :],
                               g.onescol_bf[h * 64:h * 64 + 64, 0:1],
                               sqb[h * 64:h * 64 + 64, :],
                               start=True, stop=True, sem=g.s_pe)
                    p.wait("vector", g.s_pe, mmh)
                    dop("reduce_max", mm8[0:1, 15:16], bsq[0:1, :],
                        axis=AX.X)
                    red_ev[si] = dop(
                        "tensor_tensor", mm8[0:1, qk * 2 + h:qk * 2 + h + 1],
                        mm8[0:1, qk * 2 + h:qk * 2 + h + 1],
                        mm8[0:1, 15:16], ALU.max)
        p.wait("tensor", g.s_gp, vb1_lv)
        vp_ev = {}
        for tt in range(NTT):
            bv_ = bank(4 + tt % 2)
            if tt >= 2:
                p.wait("tensor", g.s_dve, vp_ev[tt - 2])
            for dc in range(NCK):
                p.op("tensor", "matmul", bv_[:, 0:128],
                     a_reg[:, dc * T + tt * 128:dc * T + (tt + 1) * 128],
                     wvs[:, dc * 128:(dc + 1) * 128],
                     start=(dc == 0), stop=False)
            pv = p.op("tensor", "matmul", bv_[:, 0:128], g.ones_bf[0:1, 0:128],
                      rows[0:1, 4096 + pr * 128:4096 + (pr + 1) * 128],
                      start=False, stop=True, sem=g.s_pe)
            p.wait("vector", g.s_pe, pv)
            dop("tensor_copy", vb[:, tt * 130:tt * 130 + 64], bv_[:, 0:64])
            vp_ev[tt] = dop("tensor_copy",
                            vb[:, tt * 130 + 65:tt * 130 + 129],
                            bv_[:, 64:128])
        last_proj_mm = g.s_pe.n
        p.wait("scalar", g.s_dve, last)
        alv = p.op("scalar", "activation", mm8[0:1, 8:12], mm8[0:1, 0:4],
                   AF.Ln, bias=g.epscol[0:1, 0:1], sem=g.s_act)
        p.wait("vector", g.s_act, alv)
        dop("tensor_tensor", mm8[0:1, 12:14], mm8[0:1, 8:10],
            mm8[0:1, 10:12], ALU.add)
        p.wait("scalar", g.s_dve, last)
        alv = p.op("scalar", "activation", mm8[0:1, 12:14], mm8[0:1, 12:14],
                   AF.Exp, scale=0.5, bias=g.negln8[0:1, 0:1], sem=g.s_act)
        p.wait("vector", g.s_act, alv)
        dop("reduce_max", mm8[0:1, 14:15], mm8[0:1, 12:14], axis=AX.X,
            negate=True)
        dop("tensor_copy", g.a0_negbf[0:1, :], mm8[0:1, 14:15])
        p.wait("tensor", g.s_dve, last)
        bmm = p.op("tensor", "matmul", bank(6)[:, 0:1], g.ones_bf[0:1, 0:128],
                   g.a0_negbf[0:1, 0:1], start=True, stop=True, sem=g.s_pe)
        p.wait("vector", g.s_pe, bmm)
        dop("tensor_copy", negm8p[:, pr:pr + 1], bank(6)[:, 0:1])
        p.wait("sync", g.s_dve, last)
        p.dma("sync", g.QS[pr * 128:(pr + 1) * 128, :], qb[:], g.s_st[pr % 2])
        p.dma("sync", g.KS[pr * 128:(pr + 1) * 128, :], kb[:], g.s_st[pr % 2])
        pspill[pr % 2] = p.dma(
            "sync", g.VS[pr * 128:(pr + 1) * 128, :], vb[:], g.s_st[pr % 2])
    spill_done = (g.s_st[0].n, g.s_st[1].n)

    qsl2 = (qb, g.a0_qs1)
    ksl2 = (kb, g.a0_ks1)
    vsl2 = (vb, g.a0_vs1)
    last_exp = 0
    av_ev = {}
    bc_ev = {}
    prev_b_dma = 0
    exp_lv = {}
    av_lv = {}
    pair_ld = [0, 0]
    pair_att_end = [0, 0]
    for pr in range(8):
        par = pr % 2
        sem_ld = g.s_ld[par]
        qbuf, kbuf, vbuf = qsl2[par], ksl2[par], vsl2[par]
        p.wait("sync", g.s_st[0], spill_done[0])
        p.wait("sync", g.s_st[1], spill_done[1])
        p.wait("sync", g.s_pe, pair_att_end[par])
        p.dma("sync", qbuf[:], g.QS[pr * 128:(pr + 1) * 128, :], sem_ld)
        p.dma("sync", kbuf[:], g.KS[pr * 128:(pr + 1) * 128, :], sem_ld)
        pair_ld[par] = p.dma("sync", vbuf[:],
                             g.VS[pr * 128:(pr + 1) * 128, :], sem_ld)
        p.wait("tensor", sem_ld, pair_ld[par])
        opair = oreg[:, pr * 4096:(pr + 1) * 4096]
        for qc in range(NT5):
            avA = bank(4)
            avB = bank(5)

            def emit_av(kt, qcv):
                p.wait("tensor", g.s_act, exp_lv[kt])
                if kt == 0:
                    p.wait("tensor", g.s_dve, av_ev.get((pr, qcv - 1), 0))
                    if qcv == 0 and pr > 0:
                        p.wait("tensor", g.s_dve,
                               av_ev.get((pr - 1, NT5 - 1), 0))
                pbsl = pb[:, (kt % 2) * 1024:(kt % 2) * 1024 + 1024]
                p.op("tensor", "matmul", avA[0:65, :],
                     vbuf[:, kt * 130:kt * 130 + 65], pbsl[:, 0:512],
                     start=(kt == 0), stop=(kt == NTT - 1))
                av_lv[kt] = p.op("tensor", "matmul", avB[0:65, :],
                                 vbuf[:, kt * 130 + 65:kt * 130 + 130],
                                 pbsl[:, 512:1024],
                                 start=(kt == 0), stop=(kt == NTT - 1),
                                 sem=g.s_pe)

            for kt in range(NTT):
                sb0 = bank(2 * (kt % 2))
                sb1 = bank(2 * (kt % 2) + 1)
                if kt >= 2:
                    p.wait("tensor", g.s_act, exp_lv[kt - 2])
                elif qc > 0 or pr > 0:
                    p.wait("tensor", g.s_act,
                           exp_lv.get(NTT - 2 + kt, last_exp))
                p.op("tensor", "matmul", sb0,
                     kbuf[0:64, kt * 128:(kt + 1) * 128],
                     qbuf[0:64, qc * 512:(qc + 1) * 512], start=True,
                     stop=True, tile_position=(0, 0))
                smm = p.op("tensor", "matmul", sb1,
                           kbuf[64:128, kt * 128:(kt + 1) * 128],
                           qbuf[64:128, qc * 512:(qc + 1) * 512],
                           start=True, stop=True, tile_position=(64, 0),
                           sem=g.s_pe)
                if kt >= 1:
                    emit_av(kt - 1, qc)
                p.wait("scalar", g.s_pe, smm)
                if kt >= 2:
                    p.wait("scalar", g.s_pe, av_lv[kt - 2])
                elif qc > 0 or pr > 0:
                    p.wait("scalar", g.s_pe, av_lv.get(NTT - 2 + kt, 0))
                pbsl = pb[:, (kt % 2) * 1024:(kt % 2) * 1024 + 1024]
                psp = ps[:, 2 * (kt % 2) * 512:2 * (kt % 2) * 512 + 1024]
                exp_lv[kt] = p.op("scalar", "activation", pbsl, psp, AF.Exp,
                                  bias=negm8p[:, pr:pr + 1], scale=0.125,
                                  sem=g.s_act)
            emit_av(NTT - 1, qc)
            last_exp = exp_lv[NTT - 1]
            p.wait("vector", g.s_pe, av_lv[NTT - 1])
            for h, avb_ in ((0, avA), (1, avB)):
                dop("reciprocal", rlbf[64:65, 0:512], avb_[64:65, :],
                    low_prec=True)
                p.wait("tensor", g.s_dve, last)
                prevk = (pr, qc - 1, h) if qc > 0 else (pr - 1, NT5 - 1, h)
                p.wait("tensor", g.s_dve, bc_ev.get(prevk, 0))
                bmm = p.op("tensor", "matmul", bank(6 + h)[0:64, :],
                           g.ones_bf[64:65, 0:64], rlbf[64:65, 0:512],
                           start=True, stop=True, sem=g.s_pe)
                p.wait("vector", g.s_pe, bmm)
                dop("tensor_copy", bcb[0:64, :], bank(6 + h)[0:64, :])
                bc_ev[(pr, qc, h)] = last
                if h == 0:
                    dop("tensor_tensor", opair[0:64, qc * 512:(qc + 1) * 512],
                        avb_[0:64, :], bcb[0:64, :], ALU.mult)
                else:
                    p.wait("vector", g.s_st[1], prev_b_dma)
                    dop("tensor_tensor", tmpB[0:64, :], avb_[0:64, :],
                        bcb[0:64, :], ALU.mult)
                    p.wait("sync", g.s_dve, last)
                    prev_b_dma = p.dma(
                        "sync", opair[64:128, qc * 512:(qc + 1) * 512],
                        tmpB[0:64, :], g.s_st[1])
            av_ev[(pr, qc)] = last
        pair_att_end[par] = g.s_pe.n

    wpf = g.a0_wpf
    bprow = g.a0_bprow
    p.wait("gpsimd", g.s_pe, g.s_pe.n)
    p.wait("gpsimd", g.s_dve, g.s_dve.n)
    p.wait("gpsimd", g.s_act, g.s_act.n)
    wl = 0
    for dc in range(NCK):
        wl = p.dma("gpsimd", wpf[:, dc * 1024:(dc + 1) * 1024],
                   bass.AP(W["L0_attn_Wp"], dc * 128 * 1024,
                           [[1024, 128], [1, 1024]]), g.s_wt[0])
    blr = p.dma("gpsimd", bprow[0:1, :],
                bass.AP(W["L0_attn_bp"], 0, [[1024, 1], [1, 1024]]), g.s_rw)
    p.wait("tensor", g.s_wt[0], wl)
    p.wait("tensor", g.s_rw, blr)
    p.wait("tensor", g.s_st[1], prev_b_dma)
    p.wait("tensor", g.s_dve, g.s_dve.n)
    p.wait("tensor", g.s_act, g.s_act.n)
    yev = {}
    for tau in range(NT5):
        for oc in range(NCK):
            i = tau * NCK + oc
            b = bank(i % 4)
            if i >= 4:
                p.wait("tensor", g.s_dve, yev[i - 4])
            for pr2 in range(8):
                p.op("tensor", "matmul", b,
                     wpf[:, pr2 * 1024 + oc * 128:pr2 * 1024 + (oc + 1) * 128],
                     oreg[:, pr2 * 4096 + tau * 512:
                          pr2 * 4096 + (tau + 1) * 512],
                     start=(pr2 == 0), stop=False)
            mm = p.op("tensor", "matmul", b,
                      bprow[0:1, oc * 128:(oc + 1) * 128],
                      g.ones_bf[0:1, 0:512], start=False, stop=True,
                      sem=g.s_pe)
            p.wait("vector", g.s_pe, mm)
            yev[i] = p.op("vector", "tensor_copy",
                          a_reg[:, oc * T + tau * 512:oc * T + (tau + 1) * 512],
                          b, sem=g.s_dve)
    g.attn0_done = g.s_dve.n


def emit_attn1(p, g, a_reg, decay_reg):
    ps = g.ps
    bank = g.bank
    W = g.weights
    dec = g.a1_decA if decay_reg is g.REGA else g.a1_decB

    ntile, ebuf = g.a1_nt, g.a1_eb
    wo = g.a1_wo
    gcol, bcol, borow = g.a1_gcol, g.a1_bcol, g.a1_borow
    thr = g.a1_th
    wq_, wk_, wv_ = g.a1_wq, g.a1_wk, g.a1_wv
    qh, kh, vh = g.a1_qh, g.a1_kh, g.a1_vh
    ksl, vsl, qsl = g.a1_qh, g.a1_kh, g.a1_qsl
    sd, pbuf = g.a1_sd, g.a1_pb
    lnb = g.a1_ln
    sqb = g.a1_sq
    bcb, rlbf = g.a1_bcb, g.a1_rlbf
    r1, r2 = g.a1_r1, g.a1_r2
    mrow, rrow = g.a1_mrow, g.a1_rrow
    negmball = g.a1_negm8
    mm8 = g.a0_mm

    last = g.s_dve.n

    def dop(meth, *a, **k):
        nonlocal last
        p.wait("vector", g.s_dve, last)
        last = p.op("vector", meth, *a, sem=g.s_dve, **k)
        return last

    for eng in ("gpsimd", "tensor"):
        p.wait(eng, g.s_pe, g.s_pe.n)
        p.wait(eng, g.s_dve, g.s_dve.n)
        p.wait(eng, g.s_gp, g.s_gp.n)
        for _s in g.s_st + g.s_ld:
            p.wait(eng, _s, _s.n)
    p.wait("vector", g.s_gp, g.s_gp.n)

    p.op("gpsimd", "memset", g.onesf32[:], 1.0)
    tl = p.dma("gpsimd", thr[0:1, 0:8],
               bass.AP(W["L1_attn_theta"], 0, [[8, 1], [1, 8]]), g.s_rw)
    grow = g.a1_grow
    tl = p.dma("gpsimd", grow[0:1, 0:1024],
               bass.AP(W["L1_attn_gamma"], 0, [[1024, 1], [1, 1024]]), g.s_rw)
    tl = p.dma("gpsimd", grow[0:1, 1024:2048],
               bass.AP(W["L1_attn_beta"], 0, [[1024, 1], [1, 1024]]), g.s_rw)
    bl = p.dma("gpsimd", borow[0:1, :],
               bass.AP(W["L1_attn_bo"], 0, [[1024, 1], [1, 1024]]), g.s_rw)
    wl = 0
    for dc in range(NCK):
        wl = p.dma("gpsimd", wo[:, dc * 1024:(dc + 1) * 1024],
                   bass.AP(W["L1_attn_Wo"], dc * 128 * 1024,
                           [[1024, 128], [1, 1024]]), g.s_wt[0])
    p.wait("tensor", g.s_rw, bl)
    p.wait("tensor", g.s_gp, g.const_lvl)
    for i in range(16):
        p.op("tensor", "transpose", g.psbf[:, 2 * i:2 * i + 1],
             grow[0:1, i * 128:(i + 1) * 128], g.ident_bf[0:1, 0:1],
             sem=(g.s_pe if i == 15 else None))
    p.wait("vector", g.s_pe, g.s_pe.n)
    for dc in range(NCK):
        dop("tensor_copy", gcol[:, dc:dc + 1], g.psbf[:, 2 * dc:2 * dc + 1])
        dop("tensor_copy", bcol[:, dc:dc + 1],
            g.psbf[:, 16 + 2 * dc:17 + 2 * dc])
    gcol_lv = last
    p.wait("tensor", g.s_wt[0], wl)
    p.wait("tensor", g.s_dve, gcol_lv)
    b4 = ps[:, 4 * 512:4 * 512 + 1024]
    bo_mm = None
    for half in range(2):
        for dc in range(NCK):
            bo_mm = p.op("tensor", "matmul",
                         ps[:, (4 + half) * 512:(5 + half) * 512][0:1, :],
                         bcol[:, dc:dc + 1],
                         wo[:, dc * 1024 + half * 512:
                            dc * 1024 + half * 512 + 512],
                         start=(dc == 0), stop=(dc == NCK - 1),
                         sem=(g.s_pe if dc == NCK - 1 else None))
    p.wait("vector", g.s_pe, bo_mm)
    dop("tensor_tensor", borow[0:1, :], borow[0:1, :], b4[0:1, :], ALU.add)
    for dc in range(NCK):
        dop("tensor_scalar_mul", wo[:, dc * 1024:(dc + 1) * 1024],
            wo[:, dc * 1024:(dc + 1) * 1024], gcol[:, dc:dc + 1])
    wo_lv = last
    p.wait("vector", g.s_rw, g.s_rw.n)
    dop("tensor_scalar_mul", thr[0:1, 8:16], thr[0:1, 0:8], -1.0)
    th_lv = last
    p.wait("tensor", g.s_dve, th_lv)
    thc = g.a1_thc
    for h in range(8):
        p.op("tensor", "matmul", ps[:, 5 * 512 + h:5 * 512 + h + 1],
             g.onesf32[0:1, :], thr[0:1, 8 + h:9 + h], start=True, stop=True,
             sem=(g.s_pe if h == 7 else None))
    p.wait("vector", g.s_pe, g.s_pe.n)
    dop("tensor_copy", thc[:], ps[:, 5 * 512:5 * 512 + 8])
    thc_lv = last
    mtile = g.a1_mt
    for ch in range(8):
        p.wait("gpsimd", g.s_dve, last)
        p.op("gpsimd", "iota", ntile[:], pattern=[[128, 4], [1, 512]],
             base=-384 + ch * 4 * 128, channel_multiplier=-1,
             allow_small_or_imprecise_dtypes=True, sem=g.s_gp)
        p.wait("vector", g.s_gp, g.s_gp.n)
        dop("tensor_single_scalar", mtile[:], ntile[:], 0.0, ALU.is_ge)
        dop("tensor_single_scalar", ntile[:], ntile[:], 0.0, ALU.max)
        ntl = last
        dsl = dec[:, ch * 2048:(ch + 1) * 2048]
        for h in range(8):
            p.wait("scalar", g.s_dve, ntl if h == 0 else last)
            alv = p.op("scalar", "activation", (dsl if h == 0 else ebuf[:]),
                       ntile[:], AF.Exp, scale=thc[:, h:h + 1], sem=g.s_act)
            if h > 0:
                p.wait("vector", g.s_act, alv)
                dop("tensor_tensor", dsl, dsl, ebuf[:], ALU.add)
        dop("scalar_tensor_tensor", dsl, dsl, 0.125, mtile[:], ALU.mult,
            ALU.mult)
    ngm = g.a1_ngm
    p.wait("gpsimd", g.s_act, g.s_act.n)
    p.wait("gpsimd", g.s_dve, g.s_dve.n)
    nmz = p.op("gpsimd", "memset", ngm[:], 0.0, sem=g.s_gp)
    p.wait("gpsimd", g.s_gp, nmz)
    for di in range(4):
        p.op("gpsimd", "affine_select", ngm[:, di * 512:(di + 1) * 512],
             ngm[:, di * 512:(di + 1) * 512], pattern=[[1, 512]],
             compare_op=ALU.is_ge, fill=-1e30, base=-128 * di,
             channel_multiplier=-1, sem=(g.s_gp if di == 3 else None))
    dec_lv = g.s_gp.n
    p.wait("tensor", g.s_dve, last)

    a_ready = max(g.a_lv)
    hspill = [0, 0]
    last_pj = 0
    ev_lv = {}
    for h in range(8):
        sw = g.s_wt[h % 2]
        p.wait("gpsimd", g.s_pe, last_pj)
        wl = 0
        for mat, dst in (("L1_attn_Wq", wq_), ("L1_attn_Wk", wk_),
                         ("L1_attn_Wv", wv_)):
            wl = p.dma("gpsimd", dst[:],
                       bass.AP(W[mat], h * 128,
                               [[1024, 128], [128 * 1024, 8], [1, 128]]), sw)
        p.wait("tensor", sw, wl)
        if h == 0:
            p.wait("tensor", g.s_dve, a_ready)
        p.wait("vector", g.s_st[(h + 1) % 2], hspill[(h + 1) % 2])
        mm_mx = dop("memset", mm8[:], -3e38)
        for qk in range(2):
            wsl = wq_ if qk == 0 else wk_
            buf = qh if qk == 0 else kh
            for j in range(NT5):
                b0 = bank(j % 2)
                if (qk, j) not in ((0, 0), (0, 1)):
                    prev_lv = ev_lv[(qk, j - 2) if j >= 2
                                    else (qk - 1, NT5 - 2 + j)]
                    p.wait("tensor", g.s_dve, prev_lv)
                elif h > 0:
                    p.wait("tensor", g.s_dve, ev_lv[(1, NT5 - 2 + j)])
                mmq = None
                for dc in range(NCK):
                    mmq = p.op("tensor", "matmul", b0,
                               wsl[:, dc * 128:(dc + 1) * 128],
                               a_reg[:, dc * T + j * 512:
                                     dc * T + (j + 1) * 512],
                               start=(dc == 0), stop=(dc == NCK - 1),
                               sem=(g.s_pe if dc == NCK - 1 else None))
                p.wait("vector", g.s_pe, mmq)
                dsl_ = buf[:, j * 512:(j + 1) * 512]
                ev_lv[(qk, j)] = dop("tensor_copy", dsl_, b0)
                p.wait("scalar", g.s_dve, ev_lv[(qk, j)])
                sq = p.op("scalar", "activation", sqb[:], dsl_, AF.Square,
                          sem=g.s_act)
                p.wait("tensor", g.s_act, sq)
                if not (h == 0 and qk == 0 and j == 0):
                    p.wait("tensor", g.s_dve, last)
                mmh = p.op("tensor", "matmul", bank(6)[0:1, :],
                           g.onescol_bf[:, 0:1], sqb[:],
                           start=True, stop=True, sem=g.s_pe)
                p.wait("vector", g.s_pe, mmh)
                dop("reduce_max", mm8[0:1, 15:16], bank(6)[0:1, :], axis=AX.X)
                dop("tensor_tensor", mm8[0:1, qk:qk + 1],
                    mm8[0:1, qk:qk + 1], mm8[0:1, 15:16], ALU.max)
        vp = {}
        for tt in range(NTT):
            bv_ = bank(2 + tt % 2)
            if tt >= 2:
                p.wait("tensor", g.s_dve, vp[tt - 2])
            mmv = None
            for dc in range(NCK):
                mmv = p.op("tensor", "matmul", bv_[:, 0:128],
                           a_reg[:, dc * T + tt * 128:dc * T + (tt + 1) * 128],
                           wv_[:, dc * 128:(dc + 1) * 128],
                           start=(dc == 0), stop=(dc == NCK - 1),
                           sem=(g.s_pe if dc == NCK - 1 else None))
            p.wait("vector", g.s_pe, mmv)
            vp[tt] = dop("tensor_copy", vh[:, tt * 128:(tt + 1) * 128],
                         bv_[:, 0:128])
        last_pj = g.s_pe.n
        p.wait("scalar", g.s_dve, last)
        alv = p.op("scalar", "activation", mm8[0:1, 8:10], mm8[0:1, 0:2],
                   AF.Ln, bias=g.epscol[0:1, 0:1], sem=g.s_act)
        p.wait("vector", g.s_act, alv)
        dop("tensor_tensor", mm8[0:1, 12:13], mm8[0:1, 8:9],
            mm8[0:1, 9:10], ALU.add)
        p.wait("scalar", g.s_dve, last)
        alv = p.op("scalar", "activation", mm8[0:1, 12:13], mm8[0:1, 12:13],
                   AF.Exp, scale=0.5, bias=g.negln128h[0:1, 0:1],
                   sem=g.s_act)
        p.wait("vector", g.s_act, alv)
        dop("tensor_scalar_mul", mm8[0:1, 14:15], mm8[0:1, 12:13], -1.0)
        dop("tensor_copy", g.a0_negbf[0:1, :], mm8[0:1, 14:15])
        p.wait("tensor", g.s_dve, last)
        bmm = p.op("tensor", "matmul", bank(6)[:, 1:2], g.ones_bf[0:1, 0:128],
                   g.a0_negbf[0:1, 0:1], start=True, stop=True, sem=g.s_pe)
        p.wait("vector", g.s_pe, bmm)
        dop("tensor_copy", negmball[:, h:h + 1], bank(6)[:, 1:2])
        p.wait("sync", g.s_dve, last)
        p.dma("sync", g.QS[h * 128:(h + 1) * 128, :], qh[:], g.s_st[h % 2])
        p.dma("sync", g.KS[h * 128:(h + 1) * 128, :], kh[:], g.s_st[h % 2])
        hspill[h % 2] = p.dma("sync", g.VS[h * 128:(h + 1) * 128, 0:4096],
                              vh[:], g.s_st[h % 2])
    spill_all = (g.s_st[0].n, g.s_st[1].n)

    exp_prev = 0
    av_done = 0
    ln_ev = {}
    yev = {}
    ld_lv = 0
    g.prev_sdtt = 0
    for qc in range(NT5):
        nkt = qc * 4 + 4
        for h in range(8):
            sem_ld = g.s_ld[h % 2]
            p.wait("sync", g.s_st[0], spill_all[0])
            p.wait("sync", g.s_st[1], spill_all[1])
            p.wait("sync", g.s_pe, av_done)
            p.dma("sync", ksl[:, 0:nkt * 128],
                  g.KS[h * 128:(h + 1) * 128, 0:nkt * 128], sem_ld)
            p.dma("sync", vsl[:, 0:nkt * 128],
                  g.VS[h * 128:(h + 1) * 128, 0:nkt * 128], sem_ld)
            ld_lv = p.dma("sync", qsl[:],
                          g.QS[h * 128:(h + 1) * 128,
                               qc * 512:(qc + 1) * 512], sem_ld)
            p.wait("tensor", sem_ld, ld_lv)
            if qc == 0 and h == 0:
                p.wait("tensor", g.s_gp, dec_lv)
            avb = bank(4)
            ell = bank(6)[0:1, :]
            sdtt = {}
            exp2 = {}
            av2 = {}
            npair = nkt // 2
            p.wait("tensor", g.s_dve, g.prev_sdtt)
            if h == 0 and qc > 0:
                p.wait("tensor", g.s_dve, yev[(qc - 1, NCK - 2)])
                p.wait("tensor", g.s_dve, yev[(qc - 1, NCK - 1)])

            def emit_av1(pi):
                p.wait("tensor", g.s_act, exp2[pi])
                pbsl = pbuf[:, (pi % 2) * 1024:(pi % 2) * 1024 + 1024]
                for u in range(2):
                    kt = 2 * pi + u
                    lastk = (kt == nkt - 1)
                    p.op("tensor", "matmul", avb,
                         vsl[:, kt * 128:(kt + 1) * 128],
                         pbsl[:, u * 512:(u + 1) * 512],
                         start=(kt == 0), stop=lastk)
                    av2[pi] = p.op("tensor", "matmul", ell,
                                   g.onescol_bf[:, 0:1],
                                   pbsl[:, u * 512:(u + 1) * 512],
                                   start=(kt == 0), stop=lastk,
                                   sem=(g.s_pe if u == 1 else None))

            for pi in range(npair):
                for u in range(2):
                    kt = 2 * pi + u
                    sb = bank(2 * (pi % 2) + u)
                    if pi >= 2 and u == 0:
                        p.wait("tensor", g.s_dve, sdtt[pi - 2])
                    smm = p.op("tensor", "matmul", sb,
                               ksl[:, kt * 128:(kt + 1) * 128], qsl[:],
                               start=True, stop=True,
                               sem=(g.s_pe if u == 1 else None))
                if pi >= 1:
                    emit_av1(pi - 1)
                p.wait("vector", g.s_pe, smm)
                if pi >= 2:
                    p.wait("vector", g.s_act, exp2[pi - 2])
                msk = None
                for u in range(2):
                    kt = 2 * pi + u
                    d = 4 * qc - kt + 3
                    sb = bank(2 * (pi % 2) + u)
                    sdh = sd[:, (pi % 2) * 1024 + u * 512:
                             (pi % 2) * 1024 + (u + 1) * 512]
                    dop("tensor_tensor", sdh, sb,
                        dec[:, d * 512:(d + 1) * 512], ALU.mult)
                    if kt >= 4 * qc:
                        di = kt - 4 * qc
                        p.wait("gpsimd", g.s_dve, last)
                        msk = p.op("gpsimd", "tensor_tensor", sdh, sdh,
                                   ngm[:, di * 512:(di + 1) * 512], ALU.add,
                                   sem=g.s_gp)
                sd_lv = last
                sdtt[pi] = sd_lv
                g.prev_sdtt = sd_lv
                p.wait("scalar", g.s_dve, sd_lv)
                if msk is not None:
                    p.wait("scalar", g.s_gp, msk)
                if pi >= 2:
                    p.wait("scalar", g.s_pe, av2[pi - 2])
                sdp = sd[:, (pi % 2) * 1024:(pi % 2) * 1024 + 1024]
                pbsl = pbuf[:, (pi % 2) * 1024:(pi % 2) * 1024 + 1024]
                exp2[pi] = p.op("scalar", "activation", pbsl, sdp, AF.Exp,
                                bias=negmball[:, h:h + 1],
                                scale=float(1.0 / math.sqrt(128.0)),
                                sem=g.s_act)
            emit_av1(npair - 1)
            av_done = av2[npair - 1]
            p.wait("vector", g.s_pe, av_done)
            dop("reciprocal", rlbf[0:1, :], ell, low_prec=True)
            p.wait("tensor", g.s_dve, last)
            bmm = p.op("tensor", "matmul", bank(7), g.ones_bf[0:1, 0:128],
                       rlbf[0:1, :], start=True, stop=True, sem=g.s_pe)
            p.wait("vector", g.s_pe, bmm)
            dop("tensor_copy", bcb[:], bank(7))
            onb_lv = dop("tensor_tensor", g.a1_onb[:], avb, bcb[:], ALU.mult)
            p.wait("scalar", g.s_dve, onb_lv)
            sq = p.op("scalar", "activation", sqb[:], g.a1_onb[:], AF.Square,
                      sem=g.s_act)
            p.wait("tensor", g.s_dve, onb_lv)
            p.wait("tensor", g.s_act, sq)
            s1m = p.op("tensor", "matmul", bank(6)[0:1, :],
                       g.onescol_bf[:, 0:1], g.a1_onb[:],
                       start=True, stop=True)
            s2m = p.op("tensor", "matmul", bank(5)[0:1, :],
                       g.onescol_bf[:, 0:1], sqb[:],
                       start=True, stop=True, sem=g.s_pe)
            p.wait("vector", g.s_pe, s2m)
            dop("tensor_scalar_mul", r1[0:1, :], bank(6)[0:1, :], 1.0 / 128)
            dop("tensor_scalar_mul", r2[0:1, :], bank(5)[0:1, :], 1.0 / 128)
            dop("tensor_tensor", g.a1_rl[0:1, :], r1[0:1, :], r1[0:1, :],
                ALU.mult)
            dop("tensor_tensor", r2[0:1, :], r2[0:1, :], g.a1_rl[0:1, :],
                ALU.subtract)
            p.wait("scalar", g.s_dve, last)
            alv = p.op("scalar", "activation", r2[0:1, :], r2[0:1, :],
                       AF.Ln, bias=g.epscol[0:1, 0:1], sem=g.s_act)
            p.wait("scalar", g.s_act, alv)
            alv = p.op("scalar", "activation", r2[0:1, :], r2[0:1, :],
                       AF.Exp, scale=-0.5, sem=g.s_act)
            p.wait("vector", g.s_act, alv)
            dop("tensor_copy", rrow[0:1, :], r2[0:1, :])
            dop("tensor_tensor", g.a1_rl[0:1, :], r1[0:1, :], r2[0:1, :],
                ALU.mult)
            dop("tensor_copy", mrow[0:1, :], g.a1_rl[0:1, :])
            mr_lv = last
            p.wait("tensor", g.s_dve, mr_lv)
            bm1 = p.op("tensor", "matmul", bank(7), g.ones_bf[0:1, 0:128],
                       rrow[0:1, :], start=True, stop=True, sem=g.s_pe)
            p.wait("vector", g.s_pe, bm1)
            dop("tensor_copy", bcb[:], bank(7))
            lnsl = lnb[:, h * 512:(h + 1) * 512]
            dop("tensor_tensor", lnsl, g.a1_onb[:], bcb[:], ALU.mult)
            p.wait("tensor", g.s_dve, last)
            bm2 = p.op("tensor", "matmul", bank(7), g.ones_bf[0:1, 0:128],
                       mrow[0:1, :], start=True, stop=True, sem=g.s_pe)
            p.wait("vector", g.s_pe, bm2)
            dop("tensor_copy", bcb[:], bank(7))
            ln_ev[(qc, h)] = dop("tensor_tensor", lnsl, lnsl, bcb[:],
                                 ALU.subtract)
        p.wait("tensor", g.s_dve, ln_ev[(qc, 7)])
        p.wait("tensor", g.s_dve, wo_lv)
        for oc in range(NCK):
            b = bank(2 + oc % 2)
            if oc >= 2:
                p.wait("tensor", g.s_dve, yev[(qc, oc - 2)])
            elif qc > 0:
                p.wait("tensor", g.s_dve, yev[(qc - 1, NCK - 2 + oc)])
            for h2 in range(8):
                p.op("tensor", "matmul", b,
                     wo[:, h2 * 1024 + oc * 128:h2 * 1024 + (oc + 1) * 128],
                     lnb[:, h2 * 512:(h2 + 1) * 512],
                     start=(h2 == 0), stop=False)
            mm = p.op("tensor", "matmul", b,
                      borow[0:1, oc * 128:(oc + 1) * 128],
                      g.ones_bf[0:1, 0:512], start=False, stop=True,
                      sem=g.s_pe)
            p.wait("vector", g.s_pe, mm)
            yev[(qc, oc)] = p.op(
                "vector", "tensor_copy",
                a_reg[:, oc * T + qc * 512:oc * T + (qc + 1) * 512], b,
                sem=g.s_dve)
    g.attn1_done = g.s_dve.n


def emit_ffn1(p, g, a_reg, y_reg):
    ps = g.ps
    bank = g.bank
    W = g.weights
    wt = W["L1_ffn_weight"]
    dis = W["L1_ffn_disha"]

    weff = g.f_weff
    wstg = g.f_wstg
    dish = g.f_dish

    p.wait("gpsimd", g.s_pe, g.s_pe.n)
    p.wait("gpsimd", g.s_dve, g.s_dve.n)
    p.wait("gpsimd", g.s_gp, g.s_gp.n)
    for _s in g.s_st + g.s_ld:
        p.wait("gpsimd", _s, _s.n)

    wl = 0
    for oc in range(NCK):
        wl = p.dma("gpsimd", wstg[:, oc * 1024:(oc + 1) * 1024],
                   bass.AP(wt, oc * 128 * 1024, [[1024, 128], [1, 1024]]),
                   g.s_wt[0])
    dl = p.dma("gpsimd", dish[0:64, :], bass.AP(dis, 0, [[1024, 64], [1, 1024]]),
               g.s_rw)
    dl = p.dma("gpsimd", dish[64:128, :], bass.AP(dis, 0, [[1024, 64], [1, 1024]]),
               g.s_rw)
    p.wait("vector", g.s_rw, dl)
    dsc = p.op("vector", "tensor_scalar_mul", dish[:], dish[:], 0.5,
               sem=g.s_dve)

    p.wait("tensor", g.s_wt[0], wl)
    p.wait("tensor", g.s_gp, g.const_lvl)
    ev = {}
    for dc in range(NCK):
        for oc in range(NCK):
            i = dc * NCK + oc
            tb = g.psbf[:, (7 if i % 2 == 0 else 5) * 1024:
                        (7 if i % 2 == 0 else 5) * 1024 + 128]
            if i >= 2:
                p.wait("tensor", g.s_dve, ev[i - 2])
            lv = p.op("tensor", "transpose", tb,
                      wstg[:, oc * 1024 + dc * 128:oc * 1024 + (dc + 1) * 128],
                      g.ident_bf[:], sem=g.s_pe)
            p.wait("vector", g.s_pe, lv)
            if i == 0:
                p.wait("vector", g.s_dve, dsc)
            ev[i] = p.op("vector", "scalar_tensor_tensor",
                         weff[:, dc * 1024 + oc * 128:dc * 1024 + (oc + 1) * 128],
                         tb, 0.5, dish[:, oc * 128:(oc + 1) * 128],
                         ALU.mult, ALU.add, sem=g.s_dve)
    weff_lv = ev[NCK * NCK - 2]
    weff_lv2 = ev[NCK * NCK - 1]

    a_ready = max(g.a_lv)
    p.wait("tensor", g.s_dve, max(weff_lv, weff_lv2))
    p.wait("tensor", g.s_dve, a_ready)
    yev = {}
    for tau in range(NT5):
        for oc in range(NCK):
            i = tau * NCK + oc
            b = bank(i % 4)
            if i >= 4:
                p.wait("tensor", g.s_dve, yev[i - 4])
            mm = None
            for dc in range(NCK):
                mm = p.op("tensor", "matmul", b,
                          weff[:, dc * 1024 + oc * 128:dc * 1024 + (oc + 1) * 128],
                          a_reg[:, dc * T + tau * 512:dc * T + (tau + 1) * 512],
                          start=(dc == 0), stop=(dc == NCK - 1),
                          sem=(g.s_pe if dc == NCK - 1 else None))
            p.wait("vector", g.s_pe, mm)
            yev[i] = p.op("vector", "tensor_copy",
                          y_reg[:, oc * T + tau * 512:oc * T + (tau + 1) * 512],
                          b, sem=g.s_dve)
    g.ffn_done = g.s_dve.n


SEQ_FULL = ["moe", "attn0", "moe", "ffn1", "attn1", "ffn1"]

WNAMES = [
    "L0_attn_Wq", "L0_attn_bq", "L0_attn_Wk", "L0_attn_bk", "L0_attn_Wv",
    "L0_attn_bv", "L0_attn_Wp", "L0_attn_bp", "L0_ffn_router_w", "L0_ffn_w1",
    "L0_ffn_w3", "L0_ffn_w2", "L1_attn_Wq", "L1_attn_Wk", "L1_attn_Wv",
    "L1_attn_Wo", "L1_attn_bo", "L1_attn_theta", "L1_attn_gamma",
    "L1_attn_beta", "L1_ffn_weight", "L1_ffn_disha",
]

WSHAPES = {
    "L0_attn_Wq": [D, D], "L0_attn_bq": [D], "L0_attn_Wk": [D, D],
    "L0_attn_bk": [D], "L0_attn_Wv": [D, D], "L0_attn_bv": [D],
    "L0_attn_Wp": [D, D], "L0_attn_bp": [D], "L0_ffn_router_w": [D, 8],
    "L0_ffn_w1": [8, D, 2048], "L0_ffn_w3": [8, D, 2048],
    "L0_ffn_w2": [8, 2048, D], "L1_attn_Wq": [D, D], "L1_attn_Wk": [D, D],
    "L1_attn_Wv": [D, D], "L1_attn_Wo": [D, D], "L1_attn_bo": [D],
    "L1_attn_theta": [8], "L1_attn_gamma": [D], "L1_attn_beta": [D],
    "L1_ffn_weight": [D, D], "L1_ffn_disha": [64, D],
}


def build_nc(stages, dbg=False):
    from contextlib import ExitStack
    nc = bass.Bass("TRN2", target_bir_lowering=False,
                   dynamic_dma_scratch_size=16384)
    g = St()
    g.weights = {}
    for n in WNAMES:
        g.weights[n] = nc.dram_tensor(n, WSHAPES[n], F32, kind="ExternalInput")
    g.x_in = nc.dram_tensor("x_in", [T, D], F32, kind="ExternalInput")
    g.x_out = nc.dram_tensor("x_out", [T, D], F32, kind="ExternalOutput")
    dkind = "ExternalOutput" if dbg else "Internal"
    g.Rbuf = nc.dram_tensor("Rbuf", [D, T], F32, kind=dkind)
    g.Ybuf = nc.dram_tensor("Ybuf", [D, T], F32, kind=dkind)
    g.QS = nc.dram_tensor("QS", [D, T], BF16, kind=dkind)
    g.KS = nc.dram_tensor("KS", [D, T], BF16, kind=dkind)
    g.VS = nc.dram_tensor("VS", [1024, 4160], BF16, kind=dkind)

    with ExitStack() as ctx:
        E = ctx.enter_context
        ra = Arena(nc, ctx, "regA", 65536)
        rb_ = Arena(nc, ctx, "regB", 65536)
        rw = Arena(nc, ctx, "regW", 72 * KB)
        g.REGA = ra.view([128, 32768], BF16, 0)
        g.REGB = rb_.view([128, 32768], BF16, 0)
        g.stag = rw.view([128, 2 * T], F32, 0)
        g.sqbuf = rw.view([128, T], BF16, 32 * KB)
        g.rstd_row = rw.view([1, T], BF16, 40 * KB)
        g.mr_row = rw.view([1, T], BF16, 48 * KB)
        g.rstd_b = rw.view([128, T], BF16, 56 * KB)
        g.mr_b = rw.view([128, T], BF16, 64 * KB)
        g.ystag = rw.view([128, T], F32, 40 * KB)
        g.ystag2 = rw.view([128, T], F32, 56 * KB)
        g.xstage = rw.view([128, 2 * T], F32, 32 * KB)
        g.sq_x = None
        g.sqbuf_x = rw.view([128, T], BF16, 64 * KB)
        g.rstd_row_x = rw.view([1, T], BF16, 32 * KB)
        g.mr_row_x = rw.view([1, T], BF16, 40 * KB)
        g.rstd_b_x = rw.view([128, T], BF16, 48 * KB)
        g.mr_b_x = rw.view([128, T], BF16, 56 * KB)
        g.m_w1 = ra.view([128, 16 * 1024], BF16, 0)
        g.m_w3 = ra.view([128, 16 * 1024], BF16, 32 * KB)
        g.m_gbuf = rb_.view([128, 16 * 512], BF16, 0)
        g.m_asc = rb_.view([128, 2 * 8 * 512], BF16, 16 * KB)
        g.m_asl = rb_.view([128, 2 * 8 * 512], BF16, 32 * KB)
        g.m_w2 = rw.view([128, 16 * 1024], BF16, 14 * KB)
        KBf = 1024
        g.a0_q = rw.view([128, T], BF16, 0)
        g.a0_k = rw.view([128, T], BF16, 8 * KBf)
        g.a0_v = rw.view([128, 32 * 130], BF16, 16 * KBf)
        g.a0_cos = rw.view([128, T], BF16, 24 * KBf + 512)
        g.a0_sin = rw.view([128, T], BF16, 32 * KBf + 512)
        g.a0_wq = rw.view([128, 1024], BF16, 40 * KBf + 512)
        g.a0_wqr = rw.view([128, 1024], BF16, 42 * KBf + 512)
        g.a0_wk = rw.view([128, 1024], BF16, 44 * KBf + 512)
        g.a0_wkr = rw.view([128, 1024], BF16, 46 * KBf + 512)
        g.a0_wv = rw.view([128, 1024], BF16, 48 * KBf + 512)
        g.a0_p = rw.view([128, 2048], BF16, 50 * KBf + 512)
        g.a0_t1 = rw.view([128, 512], F32, 54 * KBf + 512)
        g.a0_rows = rw.view([1, 5120], BF16, 56 * KBf + 512)
        g.a0_t2 = rw.view([128, 512], F32, 56 * KBf + 512)
        g.a0_ntile = rw.view([128, 512], F32, 58 * KBf + 512)
        g.a0_i32b = rw.view([128, 512], I32, 60 * KBf + 512)
        g.a0_tang = rw.view([128, 512], F32, 62 * KBf + 512)
        g.a0_rlbf = rw.view([128, 512], BF16, 66 * KBf + 512)
        g.a0_bcb = rw.view([128, 512], F32, 67 * KBf + 512)
        g.a0_tmpB = rw.view([128, 512], BF16, 69 * KBf + 512)
        g.a0_sq = rw.view([128, 512], BF16, 70 * KBf + 512)
        g.a0_invf = rw.view([1, 160], F32, 64 * KBf + 512)
        g.onesf32 = rw.view([1, 128], F32, 71 * KBf + 512)
        g.a0_qs1 = rw.view([128, T], BF16, 24 * KBf + 512)
        g.a0_ks1 = rw.view([128, T], BF16, 32 * KBf + 512)
        g.a0_vs1 = rw.view([128, 32 * 130], BF16, 40 * KBf + 512)
        g.a0_wpf = rw.view([128, 8192], BF16, 0)
        g.a0_bprow = rw.view([1, 1024], BF16, 16 * KBf)
        g.a1_decA = ra.view([128, 16384], F32, 0)
        g.a1_decB = rb_.view([128, 16384], F32, 0)
        g.a1_nt = rw.view([128, 2048], F32, 0)
        g.a1_sd = rw.view([128, 2048], F32, 0)
        g.a1_pb = rw.view([128, 2048], BF16, 8 * KBf)
        g.a1_ngm = rw.view([128, 2048], BF16, 12 * KBf)
        g.a1_onb = rw.view([128, 512], BF16, 0)
        g.a1_eb = rw.view([128, 2048], F32, 8 * KBf)
        g.a1_ln = rw.view([128, 4096], BF16, 60 * KBf + 512)
        g.a1_wo = rw.view([128, 8192], BF16, 16 * KBf)
        g.a1_grow = rw.view([1, 2048], BF16, 32 * KBf)
        g.a1_rl = rw.view([128, 512], F32, 32 * KBf)
        g.a1_bcb = rw.view([128, 512], F32, 34 * KBf)
        g.a1_borow = rw.view([1, 1024], BF16, 36 * KBf)
        g.a1_wq = rw.view([128, 1024], BF16, 38 * KBf + 512)
        g.a1_r1 = rw.view([1, 512], F32, 38 * KBf + 512)
        g.a1_wk = rw.view([128, 1024], BF16, 40 * KBf + 512)
        g.a1_r2 = rw.view([1, 512], F32, 40 * KBf + 512)
        g.a1_wv = rw.view([128, 1024], BF16, 42 * KBf + 512)
        g.a1_mrow = rw.view([1, 512], BF16, 42 * KBf + 512)
        g.a1_rrow = rw.view([1, 512], BF16, 43 * KBf + 512)
        g.a1_qh = rw.view([128, 4096], BF16, 44 * KBf + 512)
        g.a1_mt = rw.view([128, 2048], BF16, 44 * KBf + 512)
        g.a1_kh = rw.view([128, 4096], BF16, 52 * KBf + 512)
        g.a1_vh = rw.view([128, 4096], BF16, 60 * KBf + 512)
        g.a1_sq = rw.view([128, 512], BF16, 68 * KBf + 512)
        g.a1_qsl = rw.view([128, 512], BF16, 69 * KBf + 512)
        g.a1_rlbf = rw.view([128, 512], BF16, 70 * KBf + 512)
        g.f_weff = rw.view([128, 8 * 1024], BF16, 0)
        g.f_wstg = rw.view([128, 8 * 1024], BF16, 16 * KB)
        g.f_dish = rw.view([128, 1024], BF16, 32 * KB)
        g.m_cbb = rw.view([128, 2 * 512], BF16, 0)
        g.m_ystg = rw.view([128, 4 * 512], F32, 46 * KB)
        g.m_crow = rw.view([1, T], BF16, 6 * KB)
        g.Abuf = nc.dram_tensor("Abuf", [D, T], BF16, kind=dkind)
        g.ones_bf = E(nc.sbuf_tensor("ones_bf", [128, 512], BF16))
        g.onescol_bf = E(nc.sbuf_tensor("onescol_bf", [128, 1], BF16))
        g.epscol = E(nc.sbuf_tensor("epscol", [128, 1], F32))
        g.ident_bf = E(nc.sbuf_tensor("ident_bf", [128, 128], BF16))
        g.ident_f32 = E(nc.sbuf_tensor("ident_f32", [128, 128], F32))
        g.statsb = E(nc.sbuf_tensor("statsb", [128, 64], F32))
        g.stat_m = E(nc.sbuf_tensor("stat_m", [128, 32], F32))
        g.stat_v = E(nc.sbuf_tensor("stat_v", [128, 32], F32))
        g.stat_t = E(nc.sbuf_tensor("stat_t", [128, 32], F32))
        g.stat_pk = E(nc.sbuf_tensor("stat_pk", [128, 64], BF16))
        g.lg_sb = E(nc.sbuf_tensor("lg_sb", [128, 256], F32))
        g.r_tmp = E(nc.sbuf_tensor("r_tmp", [128, 256], F32))
        g.r_tmp2 = E(nc.sbuf_tensor("r_tmp2", [128, 256], F32))
        g.r_m1 = E(nc.sbuf_tensor("r_m1", [128, 32], F32))
        g.r_m2 = E(nc.sbuf_tensor("r_m2", [128, 32], F32))
        g.r_el = E(nc.sbuf_tensor("r_el", [128, 32], F32))
        g.c_tm = E(nc.sbuf_tensor("c_tm", [128, 256], BF16))
        g.rw_sb = E(nc.sbuf_tensor("rw_sb", [128, 64], BF16))
        g.a0_mm = E(nc.sbuf_tensor("a0_mm", [1, 16], F32))
        g.a0_negbf = E(nc.sbuf_tensor("a0_negbf", [1, 1], BF16))
        g.a0_invfc = E(nc.sbuf_tensor("a0_invfc", [128, 1], F32))
        g.a0_negmb = E(nc.sbuf_tensor("a0_negmb", [128, 1], F32))
        g.negln8 = E(nc.sbuf_tensor("negln8", [1, 1], F32))
        g.neg1e30 = E(nc.sbuf_tensor("neg1e30", [128, 1], F32))
        nc.const_aps.register(F32, -1e30, g.neg1e30[:, 0:1])
        g.negpi = E(nc.sbuf_tensor("negpi", [128, 1], F32))
        g.negln128h = E(nc.sbuf_tensor("negln128h", [1, 1], F32))
        g.a1_gcol = E(nc.sbuf_tensor("a1_gcol", [128, 8], F32))
        g.a1_bcol = E(nc.sbuf_tensor("a1_bcol", [128, 8], BF16))
        g.a1_thc = E(nc.sbuf_tensor("a1_thc", [128, 8], F32))
        g.a1_th = E(nc.sbuf_tensor("a1_th", [1, 16], F32))
        g.a1_negm8 = E(nc.sbuf_tensor("a1_negm8", [128, 8], F32))
        g.ps = E(nc.psum_tensor("ps", [128, 4096], F32))[:]
        g.psbf = g.ps.bitcast(BF16)
        g.bank = lambda b: g.ps[:, 512 * b:512 * (b + 1)]
        sems = {}
        for nm in ("pe", "act", "dve", "gp"):
            sems[nm] = Sem(E(nc.semaphore(f"s_{nm}")))
        g.s_pe, g.s_act, g.s_dve, g.s_gp = (sems[n] for n in
                                            ("pe", "act", "dve", "gp"))
        g.s_ld = [Sem(E(nc.semaphore(f"s_ld{i}"))) for i in range(2)]
        g.s_st = [Sem(E(nc.semaphore(f"s_st{i}"))) for i in range(2)]
        g.s_wt = [Sem(E(nc.semaphore(f"s_wt{i}"))) for i in range(2)]
        g.s_ya = [Sem(E(nc.semaphore(f"s_ya{i}"))) for i in range(4)]
        g.s_rw = Sem(E(nc.semaphore("s_rw")))
        g.s_w2 = Sem(E(nc.semaphore("s_w2")))
        g.xtp_lv = [0] * NCK

        block = E(nc.Block())
        p = Prog(nc)
        emit_const_setup(p, g)

        other = lambda r: g.REGB if r is g.REGA else g.REGA
        cur_a = g.REGA
        moe_kw = dict()
        if isinstance(stages, dict):
            moe_kw = stages.get("moe_kw", {})
            seq = stages["seq"]
        else:
            seq = stages
        emit_bridge(p, g, yspec=None, a_reg=cur_a, rin_x=True,
                    router=(seq and seq[0] == "moe"),
                    spill_A=(seq and seq[0] == "moe"))
        cur_y = None
        for i, body in enumerate(seq):
            nxt = seq[i + 1] if i + 1 < len(seq) else None
            if body == "moe":
                emit_moe(p, g, 0, True, **moe_kw)
                cur_y = ("hbm", g.Ybuf)
            elif body == "ffn1":
                yreg = other(cur_a)
                emit_ffn1(p, g, cur_a, yreg)
                cur_y = ("sbuf", yreg)
            elif body == "attn0":
                emit_attn0(p, g, cur_a)
                cur_y = ("sbuf", cur_a)
            elif body == "attn1":
                emit_attn1(p, g, cur_a, other(cur_a))
                cur_y = ("sbuf", cur_a)
            else:
                raise NotImplementedError(body)
            if nxt is None:
                emit_bridge(p, g, yspec=cur_y, a_reg=None, out_mode=True)
            else:
                a_new = cur_y[1] if cur_y[0] == "sbuf" else cur_a
                emit_bridge(p, g, yspec=cur_y, a_reg=a_new,
                            router=(nxt == "moe"), spill_A=(nxt == "moe"))
                cur_a = a_new
        if not seq:
            emit_bridge(p, g, yspec=None, a_reg=None, out_mode=True)

        for eng in ENGINES:
            for s in (g.s_st + g.s_ya + g.s_ld + g.s_wt):
                p.wait(eng, s, s.n)
        p.replay(block)
    return nc, g


def kernel(**inputs):
    x = np.ascontiguousarray(np.asarray(inputs["L0_attn_x"], dtype=np.float32))
    nc, g = build_nc(SEQ_FULL)
    wmap = {n: np.ascontiguousarray(np.asarray(inputs[n], dtype=np.float32))
            for n in WNAMES}
    in_maps = [dict(wmap, x_in=x[c]) for c in range(8)]
    res = run_bass_kernel_spmd(nc, in_maps, core_ids=list(range(8)))
    return np.stack([r["x_out"] for r in res.results], axis=0)


if __name__ == "__main__":
    pass
